# version 09
# baseline (speedup 1.0000x reference)
from contextlib import ExitStack

import numpy as np
import concourse.bass as bass
import concourse.mybir as mybir
from concourse.bass_utils import run_bass_kernel_spmd

F32 = mybir.dt.float32
BF16 = mybir.dt.bfloat16
ALU = mybir.AluOpType
AF = mybir.ActivationFunctionType
AXX = mybir.AxisListType.X

C = 2048
PC = C // 128
FI = 6144
NEG = -1.0e9
EPS = 1e-5
ISQ = float(1.0 / np.sqrt(np.float32(128.0)))
RC = 1.0 / 2048.0
QF = 256
QOPS = 31


class Ctr:
    def __init__(self, nc, name):
        self.h = nc.alloc_semaphore(name)
        self.n = 0

    def inc(self, instr, k=1):
        instr.then_inc(self.h, k)
        self.n += k
        return self.n


class Track:

    def __init__(self, eng):
        self._eng = eng
        self.last = None

    def __getattr__(self, name):
        f = getattr(self._eng, name)

        def wrap(*a, **k):
            r = f(*a, **k)
            self.last = r
            return r
        return wrap


class Build:
    def __init__(self, T=4096):
        assert T % 512 == 0
        self.T = T
        nc = self.nc = bass.Bass("TRN2", target_bir_lowering=False,
                                 debug=False)
        dt = nc.dram_tensor
        self.x_in = dt("x_in", [T, C], F32, kind="ExternalInput")
        self.Wq = dt("Wq", [C, C], F32, kind="ExternalInput")
        self.Wk = dt("Wk", [C, C], F32, kind="ExternalInput")
        self.Wv = dt("Wv", [C, C], F32, kind="ExternalInput")
        self.bqd = dt("bqd", [C], F32, kind="ExternalInput")
        self.bkd = dt("bkd", [C], F32, kind="ExternalInput")
        self.bvd = dt("bvd", [C], F32, kind="ExternalInput")
        self.gwd = dt("gwd", [C, FI], F32, kind="ExternalInput")
        self.uwd = dt("uwd", [C, FI], F32, kind="ExternalInput")
        self.dwd = dt("dwd", [FI, C], F32, kind="ExternalInput")
        self.gscd = dt("gscd", [FI], F32, kind="ExternalInput")
        self.uscd = dt("uscd", [FI], F32, kind="ExternalInput")
        self.dscd = dt("dscd", [C], F32, kind="ExternalInput")
        self.cvec = dt("cvec", [C], F32, kind="ExternalInput")
        self.wmod = dt("wmod", [C, 6 * C], F32, kind="ExternalInput")
        self.out = dt("out", [T, C], F32, kind="ExternalOutput")

        self.sh1f = dt("sh1f", [C, T], F32)
        self.sqf = dt("sqf", [C, T], F32)
        self.skf = dt("skf", [C, T], F32)
        self.svf = dt("svf", [T, C], F32)
        self.sxT = dt("sxT", [C, T], F32)
        self.sx1 = dt("sx1", [C, T], F32)
        self.sh2h = dt("sh2h", [C, T], BF16)
        self.sh2l = dt("sh2l", [C, T], BF16)
        self.sx1b = dt("sx1b", [C, T], F32)
        self.sgw = dt("sgw", [C, FI], BF16)
        self.suw = dt("suw", [C, FI], BF16)
        self.sdw = dt("sdw", [FI, C], BF16)

        A = nc.alloc_sbuf_tensor
        self.ident = A("ident", [128, 128], F32)
        self.iotat = A("iotat", [128, 256], F32)
        self.mask0 = A("mask0", [128, 128], F32)
        self.mask128 = A("mask128", [128, 256], F32)
        self.mask2x = A("mask2x", [128, 512], F32)
        self.mask4x = A("mask4x", [128, 512], F32)
        self.modv = A("modv", [128, 96], F32)
        self.scv = A("scv", [128, 16], F32)
        self.ca1 = A("ca1", [128, 16], F32)
        self.cb1 = A("cb1", [128, 16], F32)
        self.gav = A("gav", [128, 16], F32)
        self.gbds = A("gbds", [128, 16], F32)
        self.bqv = A("bqv", [128, 16], F32)
        self.bkv = A("bkv", [128, 16], F32)
        self.dscv = A("dscv", [128, 16], F32)
        self.gscv = A("gscv", [128, 48], F32)
        self.uscv = A("uscv", [128, 48], F32)
        self.onesf = A("onesf", [128, 128], F32)
        self.cEPS = A("cEPS", [128, 1], F32)
        self.cISQ = A("cISQ", [128, 1], F32)
        self.tmpv = A("tmpv", [128, 32], F32)
        self.ps = [nc.alloc_psum_tensor(f"ps{i}", [128, 512], F32)
                   for i in range(8)]

        self.pq = Ctr(nc, "pq")
        self.evQ = Ctr(nc, "evQ")
        self.qdl = {e: [Ctr(nc, f"qdl{e}{i}") for i in range(2)] for e in "dp"}
        self.qwl = {e: [Ctr(nc, f"qwl{e}{i}") for i in range(2)] for e in "dp"}
        self.qA, self.qB = [], []
        for (ws, wd) in ((self.gwd, self.sgw), (self.uwd, self.suw)):
            for r in range(16):
                for c in range(6):
                    self.qA.append((ws, wd, 128 * r, 512 * c))
                for c in range(6, 12):
                    self.qB.append((ws, wd, 128 * r, 512 * c))
        for r in range(48):
            lst = self.qA if r < 24 else self.qB
            for c in range(4):
                lst.append((self.dwd, self.sdw, 128 * r, 512 * c))
        self.qA += self.qB[:96]
        self.qB = self.qB[96:]
        self.qpos_i = {id(self.qA): 0, id(self.qB): 0}
        self._sid = 0

    def ctr(self, name):
        self._sid += 1
        return Ctr(self.nc, f"{name}{self._sid}")

    def build(self, stages=99):
        self.prologue()
        if stages < 1:
            return self.nc
        self.mod_matvec()
        if stages < 2:
            return self.nc
        self.derived_vectors()
        if stages < 3:
            return self.nc
        with ExitStack() as qes:
            sb = lambda n, sh, d: qes.enter_context(self.nc.sbuf_tensor(n, sh, d))
            self.qset_s = self.make_qset(sb, "s", 512)
            self.sweep_h1()
            if stages >= 4:
                self.sweep_proj("q")
            if stages >= 5:
                self.sweep_proj("k")
            if stages >= 6:
                self.sweep_proj("v")
            if stages >= 7:
                while self.qpos_i[id(self.qA)] < len(self.qA):
                    self.quant_pairs_block(self.qset_s,
                                           self.quant_take(self.qA, 8))
            with self.nc.Block() as blk:
                @blk.sync
                def _(s):
                    for sl_ in range(2):
                        s.wait_ge(self.qset_s["wl"][sl_].h,
                                  self.qset_s["wl"][sl_].n)
        if stages >= 8:
            self.sweep_attn()
        if stages >= 9:
            with ExitStack() as qes2:
                sb2 = lambda n, sh, d: qes2.enter_context(
                    self.nc.sbuf_tensor(n, sh, d))
                self.qset_t = self.make_qset(sb2, "t", 512)
                self.sweep_ffn(0)
                while self.qpos_i[id(self.qB)] < len(self.qB):
                    self.quant_pairs_block(self.qset_t,
                                           self.quant_take(self.qB, 8))
                with self.nc.Block() as blk:
                    @blk.sync
                    def _(s):
                        for sl_ in range(2):
                            s.wait_ge(self.qset_t["wl"][sl_].h,
                                      self.qset_t["wl"][sl_].n)
            self.sweep_ffn(1)
            self.finale()
            assert (self.qpos_i[id(self.qA)] >= len(self.qA)
                    and self.qpos_i[id(self.qB)] >= len(self.qB)), \
                "quantization incomplete"
        return self.nc

    def make_qset(self, alloc, tag, FD):
        qs = dict(
            FD=FD,
            raw=[alloc(f"qr{tag}{i}", [128, FD], F32) for i in range(2)],
            s1=[alloc(f"qa{tag}{i}", [128, FD], F32) for i in range(2)],
            s3=[alloc(f"qw{tag}{i}", [128, FD], F32) for i in range(2)],
            s4=[alloc(f"qc{tag}{i}", [128, FD], F32) for i in range(2)],
            b=[[alloc(f"qb{tag}{i}_{j}", [128, FD], BF16) for j in range(5)]
               for i in range(2)],
            out=[alloc(f"qo{tag}{i}", [128, FD], BF16) for i in range(2)],
            dl=[self.ctr(f"qdl{tag}"), self.ctr(f"qdl{tag}")],
            wl=[self.ctr(f"qwl{tag}"), self.ctr(f"qwl{tag}")],
            ev=self.ctr(f"qev{tag}"),
            ea=self.ctr(f"qea{tag}"),
        )
        return qs

    def _quant_act_pair(self, s, qs, pair, dlt, evt, wlt):
        s.wait_ge(qs["ev"].h, evt)
        for sl_ in range(len(pair)):
            s.wait_ge(qs["wl"][sl_].h, wlt[sl_])
            s.wait_ge(qs["dl"][sl_].h, dlt[sl_])
        last = None
        for sl_ in range(len(pair)):
            s.activation(qs["out"][sl_][:], qs["raw"][sl_][:], AF.Sign)
            last = s.activation(qs["s1"][sl_][:], qs["raw"][sl_][:], AF.Abs)
        qs["ea"].inc(last)

    def _quant_dve_pair(self, v, qs, pair, eat):
        TH = [2.0 ** -2, 2.0 ** -3, 2.0 ** -4, 2.0 ** -5]
        P = range(len(pair))
        v.wait_ge(qs["ea"].h, eat)
        raw, s1, s3, s4 = qs["raw"], qs["s1"], qs["s3"], qs["s4"]
        b, out = qs["b"], qs["out"]
        for i in P:
            for j in range(4):
                v.tensor_scalar(b[i][j][:], s1[i][:], TH[j], 1.0,
                                ALU.is_lt, ALU.add)
        v.drain()
        for i in P:
            v.tensor_tensor(b[i][0][:], b[i][0][:], b[i][1][:], ALU.mult)
            v.tensor_tensor(b[i][2][:], b[i][2][:], b[i][3][:], ALU.mult)
        v.drain()
        for i in P:
            v.tensor_tensor(b[i][1][:], b[i][0][:], b[i][2][:], ALU.mult)
        v.drain()
        for i in P:
            v.scalar_tensor_tensor(raw[i][:], s1[i][:], 16.0, b[i][1][:],
                                   ALU.mult, ALU.mult)
            v.tensor_scalar(b[i][0][:], s1[i][:], TH[0], 1.0, ALU.is_ge,
                            ALU.add)
            v.tensor_scalar(b[i][2][:], s1[i][:], TH[1], 1.0, ALU.is_ge,
                            ALU.add)
            v.tensor_scalar(b[i][3][:], s1[i][:], TH[2], 1.0, ALU.is_ge,
                            ALU.add)
            v.tensor_scalar(b[i][4][:], s1[i][:], TH[3], 1.0, ALU.is_ge,
                            ALU.add)
        v.drain()
        for i in P:
            v.tensor_scalar(s4[i][:], raw[i][:], 4.0, 4.0, ALU.is_ge,
                            ALU.mult)
            v.tensor_tensor(b[i][0][:], b[i][0][:], b[i][2][:], ALU.mult)
            v.tensor_tensor(b[i][3][:], b[i][3][:], b[i][4][:], ALU.mult)
        v.drain()
        for i in P:
            v.tensor_tensor(s3[i][:], raw[i][:], s4[i][:], ALU.subtract)
            v.tensor_tensor(b[i][0][:], b[i][0][:], b[i][3][:], ALU.mult)
        v.drain()
        for i in P:
            v.tensor_scalar(s4[i][:], s3[i][:], 2.0, 2.0, ALU.is_ge, ALU.mult)
            v.tensor_single_scalar(b[i][0][:], b[i][0][:], 2.0 ** -9,
                                   ALU.mult)
        v.drain()
        for i in P:
            v.tensor_tensor(s3[i][:], s3[i][:], s4[i][:], ALU.subtract)
            v.tensor_tensor(b[i][0][:], b[i][0][:], out[i][:], ALU.mult)
        v.drain()
        for i in P:
            v.tensor_single_scalar(s4[i][:], s3[i][:], 1.0, ALU.is_ge)
        v.drain()
        for i in P:
            v.tensor_tensor(s3[i][:], s3[i][:], s4[i][:], ALU.subtract)
        v.drain()
        for i in P:
            v.tensor_single_scalar(s4[i][:], s3[i][:], 0.75, ALU.is_ge)
        v.drain()
        for i in P:
            v.scalar_tensor_tensor(s4[i][:], s3[i][:], 0.25, s4[i][:],
                                   ALU.is_gt, ALU.add)
        v.drain()
        for i in P:
            v.tensor_tensor(s3[i][:], raw[i][:], s3[i][:], ALU.subtract)
        v.drain()
        for i in P:
            v.scalar_tensor_tensor(s3[i][:], s3[i][:], 2.0, s4[i][:],
                                   ALU.mult, ALU.add)
        v.drain()
        last = None
        for i in P:
            last = v.tensor_tensor(out[i][:], s3[i][:], b[i][0][:], ALU.mult)
        qs["ev"].inc(last)
        v.drain()

    def quant_take(self, lst, n):
        take = lst[self.qpos_i[id(lst)]:self.qpos_i[id(lst)] + n]
        self.qpos_i[id(lst)] += len(take)
        return take

    def quant_pairs_block(self, qset, units):
        nc = self.nc
        if not units:
            return
        FD = qset["FD"]
        pairs = [units[i:i + 2] for i in range(0, len(units), 2)]
        with nc.Block() as blk:
            ev0 = qset["ev"].n
            ea0 = qset["ea"].n
            wl0 = [qset["wl"][0].n, qset["wl"][1].n]
            dlt = []

            @blk.sync
            def _(s):
                for pi, pair in enumerate(pairs):
                    t = []
                    if pi >= 1:
                        s.wait_ge(qset["ev"].h, ev0 + pi)
                    for sl_, (wsrc, wdst, r0, c0) in enumerate(pair):
                        if pi >= 1:
                            s.wait_ge(qset["wl"][sl_].h, qset["wl"][sl_].n)
                        qset["dl"][sl_].inc(
                            s.dma_start(qset["raw"][sl_][:, :],
                                        wsrc[r0:r0 + 128, c0:c0 + FD]), 16)
                        t.append(qset["dl"][sl_].n)
                    dlt.append(t)
                    if pi >= 1:
                        for sl_, (wsrc, wdst, r0, c0) in enumerate(pairs[pi - 1]):
                            qset["wl"][sl_].inc(
                                s.dma_start(wdst[r0:r0 + 128, c0:c0 + FD],
                                            qset["out"][sl_][:, :]), 16)
                s.wait_ge(qset["ev"].h, ev0 + len(pairs))
                for sl_, (wsrc, wdst, r0, c0) in enumerate(pairs[-1]):
                    qset["wl"][sl_].inc(
                        s.dma_start(wdst[r0:r0 + 128, c0:c0 + FD],
                                    qset["out"][sl_][:, :]), 16)
                for sl_ in range(2):
                    s.wait_ge(qset["wl"][sl_].h, qset["wl"][sl_].n)

            @blk.scalar
            def _(s):
                for pi, pair in enumerate(pairs):
                    self._quant_act_pair(s, qset, pair, dlt[pi], ev0 + pi,
                                         [wl0[0] + 16 * pi, wl0[1] + 16 * pi])

            @blk.vector
            def _(v):
                for pi, pair in enumerate(pairs):
                    self._quant_dve_pair(v, qset, pair, ea0 + pi + 1)

    def quant_inline_make(self, blk, qset, units):
        if not units:
            return []
        FD = qset["FD"]
        pairs = [units[i:i + 2] for i in range(0, len(units), 2)]
        ev0 = qset["ev"].n
        ea0 = qset["ea"].n
        wl0 = [qset["wl"][0].n, qset["wl"][1].n]
        dlt = []

        @blk.sync
        def _(s):
            for pi, pair in enumerate(pairs):
                t = []
                s.wait_ge(qset["ev"].h, ev0 + pi)
                for sl_, (wsrc, wdst, r0, c0) in enumerate(pair):
                    s.wait_ge(qset["wl"][sl_].h, qset["wl"][sl_].n)
                    qset["dl"][sl_].inc(
                        s.dma_start(qset["raw"][sl_][:, :],
                                    wsrc[r0:r0 + 128, c0:c0 + FD]), 16)
                    t.append(qset["dl"][sl_].n)
                dlt.append(t)
                s.wait_ge(qset["ev"].h, ev0 + pi + 1)
                for sl_, (wsrc, wdst, r0, c0) in enumerate(pair):
                    qset["wl"][sl_].inc(
                        s.dma_start(wdst[r0:r0 + 128, c0:c0 + FD],
                                    qset["out"][sl_][:, :]), 16)

        acts = []
        hooks = []
        for pi, pair in enumerate(pairs):
            def act(s, pi=pi, pair=pair):
                self._quant_act_pair(s, qset, pair, dlt[pi], ev0 + pi,
                                     [wl0[0] + 16 * pi, wl0[1] + 16 * pi])

            def hook(v, pi=pi, pair=pair):
                self._quant_dve_pair(v, qset, pair, ea0 + pi + 1)
            acts.append(act)
            hooks.append(hook)
        return hooks, acts

    def prologue(self):
        nc = self.nc
        dS = self.ctr("dS")
        with nc.Block() as blk:
            @blk.sync
            def _(s):
                for dst, src, w in [(self.scv, self.cvec, 16),
                                    (self.bqv, self.bqd, 16),
                                    (self.bkv, self.bkd, 16),
                                    (self.dscv, self.dscd, 16),
                                    (self.gscv, self.gscd, 48),
                                    (self.uscv, self.uscd, 48)]:
                    dS.inc(s.dma_start(
                        dst[:, 0:w], src[:].rearrange("(j p) -> p j", p=128),
                        allow_slow_non_contiguous=True), 16)
                s.wait_ge(dS.h, dS.n)

            @blk.gpsimd
            def _(g):
                g.iota(self.iotat[:, :], [[1, 256]], channel_multiplier=-1,
                       allow_small_or_imprecise_dtypes=True)

        with nc.Block() as blk:
            @blk.vector
            def _(v):
                v.tensor_single_scalar(self.ident[:, :], self.iotat[:, 0:128],
                                       0.0, ALU.is_equal)
                v.tensor_scalar(self.mask0[:, :], self.iotat[:, 0:128],
                                0.0, NEG, ALU.is_gt, ALU.mult)
                v.tensor_scalar(self.mask128[:, :], self.iotat[:, :],
                                128.0, NEG, ALU.is_gt, ALU.mult)
                v.drain()
                for j in range(2):
                    v.tensor_copy(self.mask2x[:, 256 * j:256 * (j + 1)],
                                  self.mask128[:, :])
                for j in range(4):
                    v.tensor_copy(self.mask4x[:, 128 * j:128 * (j + 1)],
                                  self.mask0[:, :])
                v.drain()
                v.memset(self.onesf[:, :], 1.0)
                v.memset(self.cEPS[:, :], EPS)
                v.memset(self.cISQ[:, :], 128.0)

            @blk.scalar
            def _(s):
                s.activation(self.scv[:, :], self.scv[:, :], AF.Silu)
        with nc.Block() as blk:
            @blk.scalar
            def _(s):
                s.activation(self.cISQ[:, :], self.cISQ[:, :], AF.Sqrt)
        with nc.Block() as blk:
            @blk.vector
            def _(v):
                v.reciprocal(self.cISQ[:, :], self.cISQ[:, :])
                v.drain()

    def mod_matvec(self):
        nc = self.nc
        nb = 6 * C // 128
        mm = self.ctr("mm")
        ev = self.ctr("ev")
        dP = self.ctr("dP")
        dPs = [self.ctr("dPmA"), self.ctr("dPmB")]
        mes = ExitStack()
        with nc.Block() as blk:
            wmc = [mes.enter_context(nc.sbuf_tensor(f"wmc{i}", [128, C], F32))
                   for i in range(2)]

            @blk.gpsimd
            def _(g):
                for ob in range(nb):
                    if ob >= 2:
                        g.wait_ge(mm.h, ob - 1)
                    src = self.wmod[:, 128 * ob:128 * (ob + 1)]
                    dPs[ob % 2].inc(g.dma_start(
                        wmc[ob % 2][:].rearrange("p (kc o) -> p kc o", kc=PC),
                        src.rearrange("(kc p) o -> p kc o", p=128)), 16)

            @blk.tensor
            def _(t):
                t = Track(t)
                for ob in range(nb):
                    t.wait_ge(dPs[ob % 2].h, 16 * (ob // 2 + 1))
                    for kc in range(PC):
                        t.matmul(self.ps[6][:, ob:ob + 1],
                                 wmc[ob % 2][:, 128 * kc:128 * (kc + 1)],
                                 self.scv[:, kc:kc + 1],
                                 start=(kc == 0), stop=(kc == PC - 1))
                    mm.inc(t.last)

            @blk.vector
            def _(v):
                v.wait_ge(mm.h, nb)
                v.tensor_copy(self.modv[:, :], self.ps[6][:, 0:96])
        mes.close()

    def derived_vectors(self):
        nc = self.nc
        ca = self.modv[:, 16:32]
        ga = self.modv[:, 32:48]
        cb = self.modv[:, 64:80]
        gb = self.modv[:, 80:96]
        with nc.Block() as blk:
            @blk.scalar
            def _(s):
                s.activation(self.tmpv[:, 0:16], ga, AF.Abs)
                s.activation(self.tmpv[:, 16:32], gb, AF.Abs)

            @blk.vector
            def _(v):
                v.tensor_single_scalar(self.ca1[:, :], ca, 1.0, ALU.add)
                v.tensor_single_scalar(self.cb1[:, :], cb, 1.0, ALU.add)
                v.tensor_scalar(self.gav[:, :], ga, 0.0, 2.0,
                                ALU.is_ge, ALU.mult)
                v.tensor_scalar(self.gbds[:, :], gb, 0.0, 2.0,
                                ALU.is_ge, ALU.mult)
                v.drain()
                v.tensor_single_scalar(self.gav[:, :], self.gav[:, :],
                                       1.0, ALU.subtract)
                v.tensor_single_scalar(self.gbds[:, :], self.gbds[:, :],
                                       1.0, ALU.subtract)
        with nc.Block() as blk:
            @blk.vector
            def _(v):
                v.tensor_single_scalar(self.tmpv[:, 0:16], self.tmpv[:, 0:16],
                                       0.25, ALU.max)
                v.tensor_single_scalar(self.tmpv[:, 16:32],
                                       self.tmpv[:, 16:32], 0.25, ALU.max)
                v.drain()
                v.tensor_tensor(self.gav[:, :], self.gav[:, :],
                                self.tmpv[:, 0:16], ALU.mult)
                v.tensor_tensor(self.gbds[:, :], self.gbds[:, :],
                                self.tmpv[:, 16:32], ALU.mult)
                v.drain()
                v.tensor_tensor(self.gbds[:, :], self.gbds[:, :],
                                self.dscv[:, :], ALU.mult)

    def sweep_h1(self):
        nc = self.nc
        Tn = 256
        NH = self.T // Tn
        es = ExitStack()
        A = lambda n, sh, d: es.enter_context(nc.sbuf_tensor(n, sh, d))
        xst = [A(f"s1x{i}", [128, C], F32) for i in range(4)]
        xT = A("s1xT", [128, PC * Tn], F32)
        sq2 = A("s1sq", [128, 2 * Tn], F32)
        stat = A("s1st", [128, 4 * Tn], F32)
        h1f = A("s1h1f", [128, PC * Tn], F32)
        dP = self.ctr("dP1")
        dO = self.ctr("dO1")
        mm = self.ctr("mm1")
        ev = self.ctr("ev1")
        ea = self.ctr("ea1")
        with nc.Block() as blk:
            @blk.gpsimd
            def _(g):
                for r in range(2):
                    dP.inc(g.dma_start(
                        xst[r][:, :], self.x_in[128 * r:128 * (r + 1), :]), 16)
        self._pref = dP.n
        for hc in range(NH):
            self._s1_block(hc, NH, Tn, xst, xT, sq2, stat, h1f,
                           dP, dO, mm, ev, ea)
        with nc.Block() as blk:
            @blk.sync
            def _(s):
                s.wait_ge(dO.h, dO.n)
        es.close()

    def _s1_block(self, hc, NH, Tn, xst, xT, sq2, stat, h1f,
                  dP, dO, mm, ev, ea):
        nc = self.nc
        t0 = Tn * hc
        sl = slice(t0, t0 + Tn)
        m_t = stat[:, 0:Tn]
        rs_t = stat[:, Tn:2 * Tn]
        var_t = stat[:, 2 * Tn:3 * Tn]
        sd_t = stat[:, 3 * Tn:4 * Tn]
        with nc.Block() as blk:
            mm0, ev0, ea0, do0 = mm.n, ev.n, ea.n, dO.n
            pref = self._pref
            ngrp = 8

            @blk.gpsimd
            def _(g):
                g.wait_ge(ev.h, ev0 + ngrp)
                if hc + 1 < NH:
                    for r in range(2):
                        dP.inc(g.dma_start(
                            xst[((hc + 1) % 2) * 2 + r][:, :],
                            self.x_in[Tn * (hc + 1) + 128 * r:
                                      Tn * (hc + 1) + 128 * (r + 1), :]), 16)
                    self._pref = dP.n
                for pc in range(PC):
                    dO.inc(g.dma_start(
                        self.sxT[128 * pc:128 * (pc + 1), sl],
                        xT[:, Tn * pc:Tn * (pc + 1)]), 16)
                g.wait_ge(ev.h, ev0 + 10)
                for pc in range(PC):
                    dO.inc(g.dma_start(
                        self.sh1f[128 * pc:128 * (pc + 1), sl],
                        h1f[:, Tn * pc:Tn * (pc + 1)]), 16)

            @blk.tensor
            def _(t):
                t = Track(t)
                t.wait_ge(dP.h, pref)
                for grp in range(ngrp):
                    bank = self.ps[grp % 2]
                    if grp >= 2:
                        t.wait_ge(ev.h, ev0 + grp - 1)
                    for i in range(4):
                        idx = grp * 4 + i
                        r, pc = idx // PC, idx % PC
                        t.transpose(bank[:, 128 * i:128 * (i + 1)],
                                    xst[(hc % 2) * 2 + r][:, 128 * pc:
                                                          128 * (pc + 1)],
                                    self.ident[:, :])
                    mm.inc(t.last)
                for pc in range(PC):
                    t.wait_ge(ea.h, ea0 + pc + 1)
                    t.matmul(self.ps[3][:, 0:Tn], self.onesf[:, :],
                             sq2[:, Tn * (pc % 2):Tn * (pc % 2 + 1)],
                             start=(pc == 0), stop=(pc == PC - 1))
                    mm.inc(t.last)
                t.wait_ge(ev.h, ev0 + ngrp)
                for pc in range(PC):
                    t.matmul(self.ps[2][:, 0:Tn], self.onesf[:, :],
                             xT[:, Tn * pc:Tn * (pc + 1)],
                             start=(pc == 0), stop=(pc == PC - 1))
                mm.inc(t.last)

            @blk.scalar
            def _(s):
                s.wait_ge(ev.h, ev0 + ngrp)
                for pc in range(PC):
                    if pc >= 2:
                        s.wait_ge(mm.h, mm0 + ngrp + pc - 1)
                    ea.inc(s.activation(
                        sq2[:, Tn * (pc % 2):Tn * (pc % 2 + 1)],
                        xT[:, Tn * pc:Tn * (pc + 1)], AF.Square))
                s.wait_ge(ev.h, ev0 + 9)
                ea.inc(s.activation(sd_t, var_t, AF.Sqrt,
                                    bias=self.cEPS[:, 0:1], scale=1.0))

            @blk.vector
            def _(v):
                for grp in range(ngrp):
                    bank = self.ps[grp % 2]
                    v.wait_ge(mm.h, mm0 + grp + 1)
                    for i in range(4):
                        idx = grp * 4 + i
                        r, pc = idx // PC, idx % PC
                        v.tensor_copy(xT[:, Tn * pc + 128 * r:
                                         Tn * pc + 128 * (r + 1)],
                                      bank[:, 128 * i:128 * (i + 1)])
                    v.drain()
                    ev.inc(v.engine_nop())
                v.wait_ge(mm.h, mm0 + ngrp + PC + 1)
                v.tensor_single_scalar(m_t, self.ps[2][:, 0:Tn], RC, ALU.mult)
                v.drain()
                v.tensor_tensor(var_t, m_t, m_t, ALU.mult)
                v.drain()
                v.scalar_tensor_tensor(var_t, self.ps[3][:, 0:Tn], RC, var_t,
                                       ALU.mult, ALU.subtract)
                v.drain()
                ev.inc(v.engine_nop())
                v.wait_ge(ea.h, ea0 + PC + 1)
                v.reciprocal(rs_t, sd_t)
                v.drain()
                v.wait_ge(dO.h, do0 + 16 * PC)
                self.ln_apply(v, xT, m_t, rs_t, self.ca1, self.modv[:, 0:16],
                              h1f, None, Tn)
                ev.inc(v.engine_nop())

    def ln_apply(self, v, x3, m, rs, scale_pp, bias_pp, outh, outl, Tn):
        for pc in range(PC):
            s = slice(Tn * pc, Tn * (pc + 1))
            v.tensor_tensor(x3[:, s], x3[:, s], m[:, 0:Tn], ALU.subtract)
        v.drain()
        for pc in range(PC):
            s = slice(Tn * pc, Tn * (pc + 1))
            v.tensor_tensor(x3[:, s], x3[:, s], rs[:, 0:Tn], ALU.mult)
        v.drain()
        if outl is None:
            for pc in range(PC):
                s = slice(Tn * pc, Tn * (pc + 1))
                v.tensor_scalar(outh[:, s], x3[:, s], scale_pp[:, pc:pc + 1],
                                bias_pp[:, pc:pc + 1], ALU.mult, ALU.add)
            v.drain()
            return
        for pc in range(PC):
            s = slice(Tn * pc, Tn * (pc + 1))
            v.tensor_scalar(x3[:, s], x3[:, s], scale_pp[:, pc:pc + 1],
                            bias_pp[:, pc:pc + 1], ALU.mult, ALU.add)
        v.drain()
        for pc in range(PC):
            s = slice(Tn * pc, Tn * (pc + 1))
            v.tensor_copy(outh[:, s], x3[:, s])
        v.drain()
        for pc in range(PC):
            s = slice(Tn * pc, Tn * (pc + 1))
            v.scalar_tensor_tensor(outl[:, s], outh[:, s], -1.0, x3[:, s],
                                   ALU.mult, ALU.add)
        v.drain()

    def weight_transform(self, W, wt, dP, mm, ev):
        nc = self.nc
        wes = ExitStack()
        dWs = [self.ctr("dWA")]
        with nc.Block() as blk:
            wraw = [wes.enter_context(nc.sbuf_tensor(f"wr_{W.name}0",
                                                     [128, C], F32))]
            mm0, ev0, dp0 = mm.n, ev.n, dP.n

            @blk.gpsimd
            def _(g):
                for ob in range(PC):
                    if ob >= 1:
                        g.wait_ge(mm.h, mm0 + ob * 4)
                    dWs[0].inc(g.dma_start(
                        wraw[0][:, :], W[128 * ob:128 * (ob + 1), :]), 16)

            @blk.tensor
            def _(t):
                t = Track(t)
                for ob in range(PC):
                    t.wait_ge(dWs[0].h, 16 * (ob + 1))
                    for jg in range(4):
                        gi = ob * 4 + jg
                        bank = self.ps[4 + gi % 2]
                        if gi >= 2:
                            t.wait_ge(ev.h, ev0 + gi - 1)
                        for kk in range(4):
                            kc = 4 * jg + kk
                            t.transpose(bank[:, 128 * kk:128 * (kk + 1)],
                                        wraw[0][:, 128 * kc:128 * (kc + 1)],
                                        self.ident[:, :])
                        mm.inc(t.last)

            @blk.vector
            def _(v):
                h3 = wt[:].rearrange("p (kc o) -> p kc o", kc=PC)
                for ob in range(PC):
                    for jg in range(4):
                        gi = ob * 4 + jg
                        bank = self.ps[4 + gi % 2]
                        v.wait_ge(mm.h, mm0 + gi + 1)
                        dh = h3[:, 4 * jg:4 * jg + 4, 128 * ob:128 * (ob + 1)]
                        src = bank[:].rearrange("p (k o) -> p k o", k=4)
                        ev.inc(v.tensor_copy(dh, src))
                        v.drain()
        wes.close()

    def sweep_proj(self, which):
        nc = self.nc
        Tn = 256
        NH = self.T // Tn
        es = ExitStack()
        A = lambda n, sh, d: es.enter_context(nc.sbuf_tensor(n, sh, d))
        W = {"q": self.Wq, "k": self.Wk, "v": self.Wv}[which]
        wt = A(f"wt_{which}", [128, PC * C], F32)
        h1f = A(f"h1f_{which}", [128, PC * Tn], F32)
        NS = 2 if which == "v" else 4
        pf = A(f"pf_{which}", [128, NS * 512], F32)
        dP = self.ctr("dPp")
        dO = self.ctr("dOp")
        dOs = [self.ctr("dOs") for _ in range(4)]
        mm = self.ctr("mmp")
        ev = self.ctr("evp")
        ea = self.ctr("eap")
        self.weight_transform(W, wt, dP, mm, ev)
        if which == "v":
            self.bvbc = A("bvbc", [128, C], F32)
            bvrow = A("bvrow", [1, C], F32)
            dBv = self.ctr("dBv")
            mmBv = self.ctr("mmBv")
            evBv = self.ctr("evBv")
            with nc.Block() as blk:
                @blk.sync
                def _(s):
                    dBv.inc(s.dma_start(
                        bvrow[:, :],
                        self.bvd[:].rearrange("(o c) -> o c", o=1)), 16)

                @blk.tensor
                def _(t):
                    t = Track(t)
                    t.wait_ge(dBv.h, 16)
                    for j in range(C // 512):
                        if j >= 2:
                            t.wait_ge(evBv.h, j - 1)
                        t.matmul(self.ps[6 + j % 2][:, :],
                                 self.onesf[0:1, 0:128],
                                 bvrow[0:1, 512 * j:512 * (j + 1)],
                                 start=True, stop=True)
                        mmBv.inc(t.last)

                @blk.vector
                def _(v):
                    for j in range(C // 512):
                        v.wait_ge(mmBv.h, j + 1)
                        evBv.inc(v.tensor_copy(
                            self.bvbc[:, 512 * j:512 * (j + 1)],
                            self.ps[6 + j % 2][:, :]))
                        v.drain()
        with nc.Block() as blk:
            @blk.gpsimd
            def _(g):
                for pc in range(PC):
                    dP.inc(g.dma_start(
                        h1f[:, Tn * pc:Tn * (pc + 1)],
                        self.sh1f[128 * pc:128 * (pc + 1), 0:Tn]), 16)
        self._pref = dP.n
        for hc in range(NH):
            qu = self.quant_take(self.qA, 8)
            self._proj_block(which, hc, NH, Tn, wt, h1f,
                             pf, dP, dOs, mm, ev, ea, qu)
        with nc.Block() as blk:
            @blk.sync
            def _(s):
                for d in dOs:
                    s.wait_ge(d.h, d.n)
        es.close()

    def _proj_block(self, which, hc, NH, Tn, wt, h1f,
                    pf, dP, dOs, mm, ev, ea, qu=()):
        nc = self.nc
        t0 = Tn * hc
        sl = slice(t0, t0 + Tn)
        is_v = (which == "v")
        sdf = {"q": self.sqf, "k": self.skf, "v": self.svf}[which]
        bias = {"q": self.bqv, "k": self.bkv, "v": None}[which]
        NU = PC if not is_v else 8
        NN = Tn if not is_v else 512
        with nc.Block() as blk:
            mm0, ev0, ea0 = mm.n, ev.n, ea.n
            dob = [d.n for d in dOs]
            pref = self._pref
            qhooks, qacts = (self.quant_inline_make(blk, self.qset_s,
                                                    list(qu))
                             if qu else ([], []))

            @blk.tensor
            def _(t):
                t = Track(t)
                t.wait_ge(dP.h, pref)
                for u in range(NU):
                    bank = self.ps[u % 6]
                    if u >= 6:
                        if is_v:
                            t.wait_ge(ev.h, ev0 + u - 5)
                        else:
                            t.wait_ge(ea.h, ea0 + u - 5)
                    if not is_v:
                        for kc in range(PC):
                            t.matmul(bank[:, 0:NN],
                                     wt[:, C * kc + 128 * u:
                                        C * kc + 128 * (u + 1)],
                                     h1f[:, Tn * kc:Tn * (kc + 1)],
                                     start=(kc == 0),
                                     stop=(kc == PC - 1))
                    else:
                        r, n4 = u // 4, u % 4
                        for kc in range(PC):
                            t.matmul(bank[:, 0:NN],
                                     h1f[:, Tn * kc + 128 * r:
                                         Tn * kc + 128 * (r + 1)],
                                     wt[:, C * kc + 512 * n4:
                                        C * kc + 512 * (n4 + 1)],
                                     start=(kc == 0),
                                     stop=(kc == PC - 1))
                    mm.inc(t.last)

            if not is_v:
                @blk.scalar
                def _(s):
                    for k in (0, 1):
                        if k < len(qacts):
                            qacts[k](s)
                    for u in range(PC):
                        if u % 4 == 0 and u >= 4 and u // 4 + 1 < len(qacts):
                            qacts[u // 4 + 1](s)
                        s.wait_ge(mm.h, mm0 + u + 1)
                        s.wait_ge(dOs[u % 4].h, dob[u % 4] + 16 * (u // 4))
                        ea.inc(s.activation(
                            pf[:, Tn * (u % 4):Tn * (u % 4 + 1)],
                            self.ps[u % 6][:, 0:Tn], AF.Identity,
                            bias=bias[:, u:u + 1], scale=1.0))

                @blk.vector
                def _(v):
                    for hk in qhooks:
                        hk(v)
            else:
                @blk.scalar
                def _(s):
                    for k in range(len(qacts)):
                        qacts[k](s)

                @blk.vector
                def _(v):
                    for u in range(8):
                        r, n4 = u // 4, u % 4
                        if u % 2 == 0 and len(qhooks) > u // 2:
                            qhooks[u // 2](v)
                        v.wait_ge(mm.h, mm0 + u + 1)
                        v.wait_ge(dOs[u % 2].h, dob[u % 2] + 16 * (u // 2))
                        h = slice(512 * (u % 2), 512 * (u % 2 + 1))
                        ev.inc(v.scalar_tensor_tensor(
                            pf[:, h], self.ps[u % 6][:, 0:512], 1.0,
                            self.bvbc[:, 512 * n4:512 * (n4 + 1)],
                            ALU.bypass, ALU.add))
                        v.drain()

            @blk.gpsimd
            def _(g):
                for u in range(NU):
                    if not is_v:
                        g.wait_ge(ea.h, ea0 + u + 1)
                        h = slice(Tn * (u % 4), Tn * (u % 4 + 1))
                        dOs[u % 4].inc(g.dma_start(
                            sdf[128 * u:128 * (u + 1), sl], pf[:, h]), 16)
                    else:
                        g.wait_ge(ev.h, ev0 + u + 1)
                        r, n4 = u // 4, u % 4
                        h = slice(512 * (u % 2), 512 * (u % 2 + 1))
                        rows = slice(t0 + 128 * r, t0 + 128 * (r + 1))
                        cols = slice(512 * n4, 512 * (n4 + 1))
                        dOs[u % 2].inc(g.dma_start(sdf[rows, cols],
                                                   pf[:, h]), 16)
                if hc + 1 < NH:
                    g.wait_ge(mm.h, mm0 + NU)
                    nsl = slice(Tn * (hc + 1), Tn * (hc + 2))
                    for pc in range(PC):
                        dP.inc(g.dma_start(
                            h1f[:, Tn * pc:Tn * (pc + 1)],
                            self.sh1f[128 * pc:128 * (pc + 1), nsl]), 16)
                    self._pref = dP.n

    def sweep_attn(self):
        nc = self.nc
        Tc = 512
        NCH = self.T // Tc
        WW = Tc + 128
        NRT = WW // 128
        es = ExitStack()
        A = lambda n, sh, d: es.enter_context(nc.sbuf_tensor(n, sh, d))
        bufs = dict(
            qw=A("qw", [128, PC * Tc], F32),
            kw=A("kw", [128, PC * WW], F32),
            vw=A("vw", [128, NRT * C], F32),
            xbuf=A("xbuf", [128, PC * Tc], F32),
            h2h=A("a_h2h", [128, PC * Tc], BF16),
            h2l=A("a_h2l", [128, PC * Tc], BF16),
            smb=A("smb", [128, 8 * 256], F32),
            athf=A("athf", [128, 2 * 256], F32),
            nmx=A("nmx", [128, 24], F32),
            sq2=A("s5sq", [128, 2 * Tc], F32),
            stat=A("s5st", [128, 2 * Tc], F32),
        )
        dP = self.ctr("dPa")
        dO = self.ctr("dOa")
        mm = self.ctr("mma")
        ev = self.ctr("eva")
        ea = self.ctr("eaa")
        for ch in range(NCH):
            self._attn_block(ch, NCH, Tc, WW, NRT, bufs, dP, dO, mm, ev, ea)
        with nc.Block() as blk:
            @blk.sync
            def _(s):
                s.wait_ge(dO.h, dO.n)
        es.close()

    def _attn_block(self, ch, NCH, Tc, WW, NRT, B, dP, dO, mm, ev, ea):
        nc = self.nc
        T = self.T
        t0 = Tc * ch
        sl = slice(t0, t0 + Tc)
        qw = B["qw"]
        kw = B["kw"]
        vw = B["vw"]
        xbuf = B["xbuf"]
        h2h, h2l = B["h2h"], B["h2l"]
        smb = B["smb"]
        ebf = smb
        athf = B["athf"]
        nmx = B["nmx"]
        sq2, stat = B["sq2"], B["stat"]
        m_t = stat[:, 0:Tc]
        rs_t = stat[:, Tc:2 * Tc]
        var_t = sq2[:, 0:Tc]
        sd_t = sq2[:, Tc:2 * Tc]
        lo = t0 - 128
        cts = []
        for ti in range(Tc // 128):
            for cls in range(2):
                par = ti % 2
                if (cls == 0 and par == 0) or (cls == 1 and par == 1):
                    nk, off, wo = 128, 0, 128 * ti + 128
                else:
                    nk, off, wo = 256, 128, 128 * ti
                cts.append(dict(ti=ti, cls=cls, nk=nk, off=off, wo=wo))
        NCT = len(cts)
        with nc.Block() as blk:
            mm0, ev0, ea0, do0 = mm.n, ev.n, ea.n, dO.n
            mmct = []
            base = mm0
            for c, ct in enumerate(cts):
                nbk = 4 if ct["nk"] == 256 else 2
                mmct.append(base)
                base += nbk + 16
            mm_end = base

            @blk.gpsimd
            def _(g):
                g.wait_ge(dO.h, do0)
                for pc in range(PC):
                    rows = slice(128 * pc, 128 * (pc + 1))
                    dP.inc(g.dma_start(
                        qw[:, Tc * pc:Tc * (pc + 1)], self.sqf[rows, sl]), 16)
                    if ch == 0:
                        dP.inc(g.dma_start(
                            kw[:, WW * pc:WW * pc + 128],
                            self.skf[rows, T - 128:T]), 16)
                        dP.inc(g.dma_start(
                            kw[:, WW * pc + 128:WW * pc + WW],
                            self.skf[rows, 0:Tc]), 16)
                    else:
                        dP.inc(g.dma_start(
                            kw[:, WW * pc:WW * (pc + 1)],
                            self.skf[rows, lo:t0 + Tc]), 16)
                    dP.inc(g.dma_start(
                        xbuf[:, Tc * pc:Tc * (pc + 1)],
                        self.sxT[rows, sl]), 16)
                for rt in range(NRT):
                    r0 = lo + 128 * rt
                    if r0 < 0:
                        r0 += T
                    rows = slice(r0, r0 + 128)
                    dP.inc(g.dma_start(
                        vw[:, C * rt:C * (rt + 1)], self.svf[rows, :]), 16)
                g.wait_ge(ev.h, ev0 + 17 * NCT + 1)
                for pc in range(PC):
                    dO.inc(g.dma_start(
                        self.sx1[128 * pc:128 * (pc + 1), sl],
                        xbuf[:, Tc * pc:Tc * (pc + 1)]), 16)
                g.wait_ge(ev.h, ev0 + 17 * NCT + 3)
                for pc in range(PC):
                    dO.inc(g.dma_start(
                        self.sh2h[128 * pc:128 * (pc + 1), sl],
                        h2h[:, Tc * pc:Tc * (pc + 1)]), 16)
                    dO.inc(g.dma_start(
                        self.sh2l[128 * pc:128 * (pc + 1), sl],
                        h2l[:, Tc * pc:Tc * (pc + 1)]), 16)

            ldq = dP.n

            @blk.tensor
            def _(t):
                t = Track(t)
                t.wait_ge(dP.h, ldq)
                for c, ct in enumerate(cts):
                    ti, cls, nk, off, wo = (ct["ti"], ct["cls"], ct["nk"],
                                            ct["off"], ct["wo"])
                    side = 4 * (c % 2)
                    nbk = 4 if nk == 256 else 2
                    hpb = 8 // nbk
                    h0 = 8 * cls
                    if c >= 2:
                        t.wait_ge(ev.h, ev0 + 17 * (c - 2) + 17)
                    for j in range(nbk):
                        bank = self.ps[side + j]
                        for hh in range(hpb):
                            h = h0 + j * hpb + hh
                            t.matmul(bank[:, nk * hh:nk * (hh + 1)],
                                     qw[:, Tc * h + 128 * ti:
                                        Tc * h + 128 * (ti + 1)],
                                     kw[:, WW * h + wo:WW * h + wo + nk],
                                     start=(hh == 0), stop=(hh == hpb - 1))
                        mm.inc(t.last)
                    for hh in range(8):
                        h = h0 + hh
                        tb = self.ps[side + hh % 2]
                        ab = self.ps[side + 2 + hh % 2]
                        t.wait_ge(ea.h, ea0 + 2 * c + 2)
                        if hh >= 2:
                            t.wait_ge(ev.h, ev0 + 17 * c + 2 * hh - 2)
                        nblk = nk // 128
                        for j2 in range(nblk):
                            t.transpose(tb[:, 128 * j2:128 * (j2 + 1)],
                                        ebf[:, nk * hh + 128 * j2:
                                            nk * hh + 128 * (j2 + 1)],
                                        self.ident[:, :])
                        mm.inc(t.last)
                        t.wait_ge(ev.h, ev0 + 17 * c + 2 + 2 * hh)
                        if hh >= 2:
                            t.wait_ge(ev.h, ev0 + 17 * c + 2 * hh - 1)
                        for j2 in range(nblk):
                            rt = wo // 128 + j2
                            t.matmul(ab[:, 0:128],
                                     vw[:, C * rt + 128 * h:
                                        C * rt + 128 * (h + 1)],
                                     athf[:, 256 * (hh % 2) + 128 * j2:
                                          256 * (hh % 2) + 128 * (j2 + 1)],
                                     start=(j2 == 0),
                                     stop=(j2 == nblk - 1))
                        mm.inc(t.last)
                t.wait_ge(ev.h, ev0 + 17 * NCT + 1)
                for pc in range(PC):
                    t.wait_ge(ea.h, ea0 + 2 * NCT + pc + 1)
                    t.matmul(self.ps[3][:, 0:Tc], self.onesf[:, :],
                             sq2[:, Tc * (pc % 2):Tc * (pc % 2 + 1)],
                             start=(pc == 0), stop=(pc == PC - 1))
                    mm.inc(t.last)
                for pc in range(PC):
                    t.matmul(self.ps[2][:, 0:Tc], self.onesf[:, :],
                             xbuf[:, Tc * pc:Tc * (pc + 1)],
                             start=(pc == 0), stop=(pc == PC - 1))
                mm.inc(t.last)

            @blk.scalar
            def _(s):
                for c, ct in enumerate(cts):
                    nk = ct["nk"]
                    s.wait_ge(ev.h, ev0 + 17 * c + 1)
                    if c >= 1:
                        s.wait_ge(mm.h, mmct[c])
                    last = None
                    for hh in range(8):
                        last = s.activation(
                            ebf[:, nk * hh:nk * (hh + 1)],
                            smb[:, nk * hh:nk * (hh + 1)],
                            AF.Exp, bias=nmx[:, 8 + hh:9 + hh],
                            scale=self.cISQ[:, 0:1])
                    ea.inc(last)
                for pc in range(PC):
                    s.wait_ge(ev.h, ev0 + 17 * NCT + 1)
                    if pc >= 2:
                        s.wait_ge(mm.h, mm_end + pc - 1)
                    s.activation(sq2[:, Tc * (pc % 2):Tc * (pc % 2 + 1)],
                                 xbuf[:, Tc * pc:Tc * (pc + 1)],
                                 AF.Square).then_inc(ea.h, 1)
                    ea.n += 1
                s.wait_ge(ev.h, ev0 + 17 * NCT + 2)
                ea.inc(s.activation(sd_t, var_t, AF.Sqrt,
                                    bias=self.cEPS[:, 0:1], scale=1.0))

            @blk.vector
            def _(v):
                for c, ct in enumerate(cts):
                    ti, cls, nk, off = (ct["ti"], ct["cls"], ct["nk"],
                                        ct["off"])
                    side = 4 * (c % 2)
                    nbk = 4 if nk == 256 else 2
                    hpb = 8 // nbk
                    h0 = 8 * cls
                    msk = self.mask2x if nk == 256 else self.mask4x
                    if c >= 1:
                        v.wait_ge(ea.h, ea0 + 2 * (c - 1) + 1)
                    for j in range(nbk):
                        v.wait_ge(mm.h, mmct[c] + j + 1)
                        v.tensor_tensor(
                            smb[:, 512 * j:512 * j + nk * hpb],
                            self.ps[side + j][:, 0:nk * hpb],
                            msk[:, 0:nk * hpb], ALU.add)
                    v.drain()
                    for j in range(nbk):
                        sm3 = smb[:, 512 * j:512 * j + nk * hpb].rearrange(
                            "p (hh k) -> p hh k", hh=hpb)
                        v.tensor_reduce(nmx[:, j * hpb:(j + 1) * hpb], sm3,
                                        AXX, ALU.max, negate=True)
                    v.drain()
                    v.tensor_single_scalar(nmx[:, 8:16], nmx[:, 0:8],
                                           self.cISQ[:, 0:1], ALU.mult)
                    ev.inc(v.engine_nop())
                    v.wait_ge(ea.h, ea0 + 2 * c + 1)
                    e3 = ebf[:, 0:8 * nk].rearrange("p (hh k) -> p hh k",
                                                    hh=8)
                    v.tensor_reduce(nmx[:, 0:8], e3, AXX, ALU.add)
                    v.drain()
                    v.reciprocal(nmx[:, 16:24], nmx[:, 0:8])
                    v.drain()
                    last = None
                    for hh in range(8):
                        last = v.tensor_single_scalar(
                            ebf[:, nk * hh:nk * (hh + 1)],
                            ebf[:, nk * hh:nk * (hh + 1)],
                            nmx[:, 16 + hh:17 + hh], ALU.mult)
                    ea.inc(last)
                    v.drain()
                    for hh in range(8):
                        h = h0 + hh
                        tb = self.ps[side + hh % 2]
                        ab = self.ps[side + 2 + hh % 2]
                        nblk = nk // 128
                        v.wait_ge(mm.h, mmct[c] + nbk + 2 * hh + 1)
                        hsl = slice(256 * (hh % 2), 256 * (hh % 2) + nk)
                        ev.inc(v.tensor_copy(athf[:, hsl],
                                             tb[:, 0:nk]))
                        v.drain()
                        v.wait_ge(mm.h, mmct[c] + nbk + 2 * hh + 2)
                        xsl = slice(Tc * h + 128 * ti,
                                    Tc * h + 128 * (ti + 1))
                        v.scalar_tensor_tensor(xbuf[:, xsl], ab[:, 0:128],
                                               self.gav[:, h:h + 1],
                                               xbuf[:, xsl],
                                               ALU.mult, ALU.add)
                        ev.inc(v.engine_nop())
                        v.drain()
                ev.inc(v.engine_nop())
                v.wait_ge(mm.h, mm_end + PC + 1)
                v.tensor_single_scalar(m_t, self.ps[2][:, 0:Tc], RC, ALU.mult)
                v.drain()
                v.tensor_tensor(var_t, m_t, m_t, ALU.mult)
                v.drain()
                v.scalar_tensor_tensor(var_t, self.ps[3][:, 0:Tc], RC, var_t,
                                       ALU.mult, ALU.subtract)
                v.drain()
                ev.inc(v.engine_nop())
                v.wait_ge(ea.h, ea0 + 2 * NCT + PC + 1)
                v.reciprocal(rs_t, sd_t)
                v.drain()
                v.wait_ge(dO.h, do0 + 16 * PC)
                self.ln_apply(v, xbuf, m_t, rs_t, self.cb1,
                              self.modv[:, 48:64], h2h, h2l, Tc)
                ev.inc(v.engine_nop())

    def sweep_ffn(self, half):
        nc = self.nc
        Tc = 512
        NCH = self.T // Tc
        NB = 6
        es = ExitStack()
        A = lambda n, sh, d: es.enter_context(nc.sbuf_tensor(n, sh, d))
        tg = f"f{half}"
        h2bh = A(f"h2bh{tg}", [128, PC * Tc], BF16)
        h2bl = A(f"h2bl{tg}", [128, PC * Tc], BF16)
        x2a = A(f"x2a{tg}", [128, PC * Tc], F32)
        gwb = [A(f"gwb{tg}{i}", [128, PC * 512], BF16) for i in range(2)]
        uwb = [A(f"uwb{tg}{i}", [128, PC * 512], BF16) for i in range(2)]
        dwb = [A(f"dwb{tg}{i}", [128, 4 * C], BF16) for i in range(1)]
        gsb = A(f"gsb{tg}", [128, 4 * 512], F32)
        fsh = A(f"fsh{tg}", [128, 4 * 512], BF16)
        fsl = A(f"fsl{tg}", [128, 4 * 512], BF16)
        fsf = A(f"fsf{tg}", [128, 4 * 512], F32)
        outst = A(f"outst{tg}", [128, C], F32) if half == 1 else None
        bufs = (h2bh, h2bl, x2a, gwb, uwb, dwb, gsb, fsh, fsl, fsf, outst)
        dP = self.ctr("dPf")
        dPw = [self.ctr("dPwA"), self.ctr("dPwB")]
        dPd = self.ctr("dPwD")
        dO = self.ctr("dOf")
        mm = self.ctr("mmf")
        ev = self.ctr("evf")
        ea = self.ctr("eaf")
        self._wtq = []
        self._dPw = dPw
        self._dPd = dPd
        xsrc = self.sx1 if half == 0 else self.sx1b
        with nc.Block() as blk:
            @blk.gpsimd
            def _(g):
                self._ffn_load_chunk(g, 0, Tc, h2bh, h2bl, x2a, dP, xsrc)
                self._ffn_load_gu(g, half, 0, gwb[0], uwb[0], dPw[0])
                self._wtq.append((0, dPw[0].n))
        self._pref = dP.n
        for ch in range(NCH):
            qu = self.quant_take(self.qB, 24) if half == 0 else []
            self._ffn_block(ch, NCH, Tc, NB, half, bufs, dP, dO, mm, ev, ea,
                            xsrc, qu)
        with nc.Block() as blk:
            @blk.sync
            def _(s):
                s.wait_ge(dO.h, dO.n)
        if half == 1:
            self._ffn_out = dO
        es.close()

    def _ffn_load_chunk(self, g, ch, Tc, h2bh, h2bl, x2a, dP, xsrc):
        sl = slice(Tc * ch, Tc * (ch + 1))
        for pc in range(PC):
            rows = slice(128 * pc, 128 * (pc + 1))
            dP.inc(g.dma_start(h2bh[:, Tc * pc:Tc * (pc + 1)],
                               self.sh2h[rows, sl]), 16)
            dP.inc(g.dma_start(h2bl[:, Tc * pc:Tc * (pc + 1)],
                               self.sh2l[rows, sl]), 16)
            dP.inc(g.dma_start(x2a[:, Tc * pc:Tc * (pc + 1)],
                               xsrc[rows, sl]), 16)

    def _ffn_load_gu(self, g, half, b, gwbuf, uwbuf, ctr):
        cols = slice(3072 * half + 512 * b, 3072 * half + 512 * (b + 1))
        for kc in range(PC):
            rows = slice(128 * kc, 128 * (kc + 1))
            ctr.inc(g.dma_start(gwbuf[:, 512 * kc:512 * (kc + 1)],
                                self.sgw[rows, cols]), 16)
            ctr.inc(g.dma_start(uwbuf[:, 512 * kc:512 * (kc + 1)],
                                self.suw[rows, cols]), 16)

    def _ffn_load_d(self, g, half, b, dwbuf, ctr):
        for icl in range(4):
            rows = slice(3072 * half + 512 * b + 128 * icl,
                         3072 * half + 512 * b + 128 * (icl + 1))
            ctr.inc(g.dma_start(dwbuf[:, C * icl:C * (icl + 1)],
                                self.sdw[rows, :]), 16)

    def _ffn_block(self, ch, NCH, Tc, NB, half, bufs, dP, dO, mm, ev, ea,
                   xsrc, qu):
        nc = self.nc
        (h2bh, h2bl, x2a, gwb, uwb, dwb, gsb, fsh, fsl, fsf, outst) = bufs
        t0 = Tc * ch
        NTR = Tc // 128
        sl = slice(t0, t0 + Tc)
        with nc.Block() as blk:
            mm0, ev0, ea0, do0 = mm.n, ev.n, ea.n, dO.n
            pref0 = self._pref
            dPw, dPd = self._dPw, self._dPd
            qhooks, qacts = (self.quant_inline_make(blk, self.qset_t,
                                                    list(qu))
                             if qu else ([], []))
            dtq = []

            @blk.gpsimd
            def _(g):
                for b in range(NB):
                    if b >= 1:
                        g.wait_ge(mm.h, mm0 + 24 * b)
                    self._ffn_load_d(g, half, b, dwb[0], dPd)
                    dtq.append(dPd.n)
                    nxt = b + 1
                    if nxt < NB or ch + 1 < NCH:
                        tgt = nxt if nxt < NB else 0
                        self._ffn_load_gu(g, half, tgt, gwb[nxt % 2],
                                          uwb[nxt % 2], dPw[nxt % 2])
                        self._wtq.append((nxt % 2, dPw[nxt % 2].n))
                if half == 0:
                    g.wait_ge(ev.h, ev0 + 20 * NB)
                    for pc in range(PC):
                        dO.inc(g.dma_start(
                            self.sx1b[128 * pc:128 * (pc + 1), sl],
                            x2a[:, Tc * pc:Tc * (pc + 1)]), 16)
                if ch + 1 < NCH:
                    if half == 0:
                        g.wait_ge(dO.h, do0 + 16 * PC)
                    else:
                        g.wait_ge(mm.h, mm0 + 24 * NB + NTR * PC)
                    self._ffn_load_chunk(g, ch + 1, Tc, h2bh, h2bl, x2a, dP,
                                         xsrc)
                    self._pref = dP.n

            @blk.tensor
            def _(t):
                t = Track(t)
                t.wait_ge(dP.h, pref0)
                for b in range(NB):
                    ws_, wv_ = self._wtq.pop(0)
                    t.wait_ge(dPw[ws_].h, wv_)
                    gslot, uslot = gwb[b % 2], uwb[b % 2]
                    for icl in range(4):
                        bank = self.ps[icl]
                        if b >= 1:
                            t.wait_ge(ev.h, ev0 + 20 * b)
                        gi = 0
                        for hs in (h2bh, h2bl):
                            for kc in range(PC):
                                t.matmul(bank[:, 0:512],
                                         gslot[:, 512 * kc + 128 * icl:
                                               512 * kc + 128 * (icl + 1)],
                                         hs[:, Tc * kc:Tc * (kc + 1)],
                                         start=(gi == 0),
                                         stop=(gi == 2 * PC - 1))
                                gi += 1
                        mm.inc(t.last)
                    for icl in range(4):
                        bank = self.ps[4 + icl]
                        if b >= 1:
                            t.wait_ge(ev.h, ev0 + 20 * (b - 1) + 4)
                        gi = 0
                        for hs in (h2bh, h2bl):
                            for kc in range(PC):
                                t.matmul(bank[:, 0:512],
                                         uslot[:, 512 * kc + 128 * icl:
                                               512 * kc + 128 * (icl + 1)],
                                         hs[:, Tc * kc:Tc * (kc + 1)],
                                         start=(gi == 0),
                                         stop=(gi == 2 * PC - 1))
                                gi += 1
                        mm.inc(t.last)
                    t.wait_ge(dPd.h, dtq[b])
                    for oc in range(PC):
                        bank = self.ps[oc % 4]
                        t.wait_ge(ev.h, ev0 + 20 * b + 4)
                        if oc >= 4:
                            t.wait_ge(ev.h, ev0 + 20 * b + 4 + oc - 3)
                        gi = 0
                        for fs in (fsh, fsl):
                            for icl in range(4):
                                t.matmul(bank[:, 0:512],
                                         dwb[0][:, C * icl + 128 * oc:
                                                C * icl + 128 * (oc + 1)],
                                         fs[:, 512 * icl:512 * (icl + 1)],
                                         start=(gi == 0), stop=(gi == 7))
                                gi += 1
                        mm.inc(t.last)
                if half == 1:
                    for r in range(NTR):
                        for pc in range(PC):
                            idx = r * PC + pc
                            bank = self.ps[4 + idx % 4]
                            if idx == 0:
                                t.wait_ge(ev.h, ev0 + 20 * NB)
                            if idx >= 4:
                                t.wait_ge(ev.h, ev0 + 20 * NB + idx - 3)
                            t.transpose(bank[:, 0:128],
                                        x2a[:, Tc * pc + 128 * r:
                                            Tc * pc + 128 * (r + 1)],
                                        self.ident[:, :])
                            mm.inc(t.last)

            @blk.scalar
            def _(s):
                for b in range(NB):
                    if 2 * b < len(qacts):
                        qacts[2 * b](s)
                    for icl in range(4):
                        s.wait_ge(mm.h, mm0 + 24 * b + icl + 1)
                        if b >= 1:
                            s.wait_ge(ev.h, ev0 + 20 * (b - 1) + 4)
                        ic = 24 * half + 4 * b + icl
                        s.activation(gsb[:, 512 * icl:512 * (icl + 1)],
                                     self.ps[icl][:, 0:512], AF.Silu,
                                     scale=self.gscv[:, ic:ic + 1]) \
                            .then_inc(ea.h, 1)
                        ea.n += 1
                    if 2 * b + 1 < len(qacts):
                        qacts[2 * b + 1](s)

            @blk.vector
            def _(v):
                for b in range(NB):
                    for icl in range(4):
                        ic = 24 * half + 4 * b + icl
                        v.wait_ge(mm.h, mm0 + 24 * b + 4 + icl + 1)
                        v.wait_ge(ea.h, ea0 + 4 * b + icl + 1)
                        v.scalar_tensor_tensor(
                            fsf[:, 512 * icl:512 * (icl + 1)],
                            self.ps[4 + icl][:, 0:512],
                            self.uscv[:, ic:ic + 1],
                            gsb[:, 512 * icl:512 * (icl + 1)],
                            ALU.mult, ALU.mult)
                    v.drain()
                    for icl in range(4):
                        v.tensor_copy(fsh[:, 512 * icl:512 * (icl + 1)],
                                      fsf[:, 512 * icl:512 * (icl + 1)])
                    v.drain()
                    for icl in range(4):
                        v.scalar_tensor_tensor(
                            fsl[:, 512 * icl:512 * (icl + 1)],
                            fsh[:, 512 * icl:512 * (icl + 1)], -1.0,
                            fsf[:, 512 * icl:512 * (icl + 1)],
                            ALU.mult, ALU.add)
                    v.drain()
                    ev.inc(v.engine_nop(), 4)
                    for oc in range(PC):
                        v.wait_ge(mm.h, mm0 + 24 * b + 8 + oc + 1)
                        v.scalar_tensor_tensor(
                            x2a[:, Tc * oc:Tc * (oc + 1)],
                            self.ps[oc % 4][:, 0:512],
                            self.gbds[:, oc:oc + 1],
                            x2a[:, Tc * oc:Tc * (oc + 1)],
                            ALU.mult, ALU.add)
                        v.drain()
                        ev.inc(v.engine_nop())
                    for k in (2 * b, 2 * b + 1):
                        if k < len(qhooks):
                            qhooks[k](v)
                if half == 1:
                    for r in range(NTR):
                        for pc in range(PC):
                            idx = r * PC + pc
                            bank = self.ps[4 + idx % 4]
                            v.wait_ge(mm.h, mm0 + 24 * NB + idx + 1)
                            if pc == 0:
                                v.wait_ge(dO.h, do0 + 16 * r)
                            v.tensor_copy(outst[:, 128 * pc:128 * (pc + 1)],
                                          bank[:, 0:128])
                            v.drain()
                            ev.inc(v.engine_nop())

            if half == 1:
                @blk.sync
                def _(s):
                    for r in range(NTR):
                        s.wait_ge(ev.h, ev0 + 20 * NB + (r + 1) * PC)
                        dO.inc(s.dma_start(
                            self.out[t0 + 128 * r:t0 + 128 * (r + 1), :],
                            outst[:, :]), 16)

    def finale(self):
        nc = self.nc
        dOf = self._ffn_out
        with nc.Block() as blk:
            @blk.sync
            def _(s):
                s.wait_ge(dOf.h, dOf.n)
                for e in "dp":
                    for sl_ in range(2):
                        s.wait_ge(self.qwl[e][sl_].h, self.qwl[e][sl_].n)


def _shard_inputs(inputs):
    maps = []
    for b in range(8):
        maps.append(dict(
            x_in=np.ascontiguousarray(np.asarray(inputs["L0_attn_x"])[b]),
            Wq=np.asarray(inputs["L0_attn_Wq"]),
            Wk=np.asarray(inputs["L0_attn_Wk"]),
            Wv=np.asarray(inputs["L0_attn_Wv"]),
            bqd=np.asarray(inputs["L0_attn_bq"]),
            bkd=np.asarray(inputs["L0_attn_bk"]),
            bvd=np.asarray(inputs["L0_attn_bv"]),
            gwd=np.asarray(inputs["L0_ffn_gate_w"]),
            uwd=np.asarray(inputs["L0_ffn_up_w"]),
            dwd=np.asarray(inputs["L0_ffn_down_w"]),
            gscd=np.asarray(inputs["L0_ffn_gate_scale"]),
            uscd=np.asarray(inputs["L0_ffn_up_scale"]),
            dscd=np.asarray(inputs["L0_ffn_down_scale"]),
            cvec=np.ascontiguousarray(np.asarray(inputs["L0_c"])[b]),
            wmod=np.asarray(inputs["L0_wmod"]),
        ))
    return maps


def kernel(**inputs):
    T = np.asarray(inputs["L0_attn_x"]).shape[1]
    b = Build(T=T)
    nc = b.build()
    in_maps = _shard_inputs(inputs)
    res = run_bass_kernel_spmd(nc, in_maps, core_ids=list(range(8)))
    out = np.stack([np.asarray(r["out"]) for r in res.results], axis=0)
    return out.astype(np.float32)


# version 10
# speedup vs baseline: 1.0620x; 1.0620x over previous
from contextlib import ExitStack

import numpy as np
import concourse.bass as bass
import concourse.mybir as mybir
from concourse.bass_utils import run_bass_kernel_spmd

F32 = mybir.dt.float32
BF16 = mybir.dt.bfloat16
ALU = mybir.AluOpType
AF = mybir.ActivationFunctionType
AXX = mybir.AxisListType.X

C = 2048
PC = C // 128
FI = 6144
NEG = -1.0e9
EPS = 1e-5
ISQ = float(1.0 / np.sqrt(np.float32(128.0)))
RC = 1.0 / 2048.0
QF = 256
QOPS = 31


class Ctr:
    def __init__(self, nc, name):
        self.h = nc.alloc_semaphore(name)
        self.n = 0

    def inc(self, instr, k=1):
        instr.then_inc(self.h, k)
        self.n += k
        return self.n


class Track:

    def __init__(self, eng):
        self._eng = eng
        self.last = None

    def __getattr__(self, name):
        f = getattr(self._eng, name)

        def wrap(*a, **k):
            r = f(*a, **k)
            self.last = r
            return r
        return wrap


class Build:
    def __init__(self, T=4096):
        assert T % 512 == 0
        self.T = T
        nc = self.nc = bass.Bass("TRN2", target_bir_lowering=False,
                                 debug=False)
        dt = nc.dram_tensor
        self.x_in = dt("x_in", [T, C], F32, kind="ExternalInput")
        self.Wq = dt("Wq", [C, C], F32, kind="ExternalInput")
        self.Wk = dt("Wk", [C, C], F32, kind="ExternalInput")
        self.Wv = dt("Wv", [C, C], F32, kind="ExternalInput")
        self.bqd = dt("bqd", [C], F32, kind="ExternalInput")
        self.bkd = dt("bkd", [C], F32, kind="ExternalInput")
        self.bvd = dt("bvd", [C], F32, kind="ExternalInput")
        self.gwd = dt("gwd", [C, FI], F32, kind="ExternalInput")
        self.uwd = dt("uwd", [C, FI], F32, kind="ExternalInput")
        self.dwd = dt("dwd", [FI, C], F32, kind="ExternalInput")
        self.gscd = dt("gscd", [FI], F32, kind="ExternalInput")
        self.uscd = dt("uscd", [FI], F32, kind="ExternalInput")
        self.dscd = dt("dscd", [C], F32, kind="ExternalInput")
        self.cvec = dt("cvec", [C], F32, kind="ExternalInput")
        self.wmod = dt("wmod", [C, 6 * C], F32, kind="ExternalInput")
        self.out = dt("out", [T, C], F32, kind="ExternalOutput")

        self.sh1f = dt("sh1f", [C, T], F32)
        self.sqf = dt("sqf", [C, T], F32)
        self.skf = dt("skf", [C, T], F32)
        self.svf = dt("svf", [T, C], F32)
        self.sxT = dt("sxT", [C, T], F32)
        self.sx1 = dt("sx1", [C, T], F32)
        self.sh2h = dt("sh2h", [C, T], BF16)
        self.sh2l = dt("sh2l", [C, T], BF16)
        self.sx1b = dt("sx1b", [C, T], F32)
        self.sgw = dt("sgw", [C, FI], BF16)
        self.suw = dt("suw", [C, FI], BF16)
        self.sdw = dt("sdw", [FI, C], BF16)

        A = nc.alloc_sbuf_tensor
        self.ident = A("ident", [128, 128], F32)
        self.iotat = A("iotat", [128, 256], F32)
        self.mask0 = A("mask0", [128, 128], F32)
        self.mask128 = A("mask128", [128, 256], F32)
        self.mask2x = A("mask2x", [128, 512], F32)
        self.mask4x = A("mask4x", [128, 512], F32)
        self.modv = A("modv", [128, 96], F32)
        self.scv = A("scv", [128, 16], F32)
        self.ca1 = A("ca1", [128, 16], F32)
        self.cb1 = A("cb1", [128, 16], F32)
        self.gav = A("gav", [128, 16], F32)
        self.gbds = A("gbds", [128, 16], F32)
        self.bqv = A("bqv", [128, 16], F32)
        self.bkv = A("bkv", [128, 16], F32)
        self.dscv = A("dscv", [128, 16], F32)
        self.gscv = A("gscv", [128, 48], F32)
        self.uscv = A("uscv", [128, 48], F32)
        self.onesf = A("onesf", [128, 128], F32)
        self.cEPS = A("cEPS", [128, 1], F32)
        self.cISQ = A("cISQ", [128, 1], F32)
        self.tmpv = A("tmpv", [128, 32], F32)
        self.ps = [nc.alloc_psum_tensor(f"ps{i}", [128, 512], F32)
                   for i in range(8)]

        self.pq = Ctr(nc, "pq")
        self.evQ = Ctr(nc, "evQ")
        self.qdl = {e: [Ctr(nc, f"qdl{e}{i}") for i in range(2)] for e in "dp"}
        self.qwl = {e: [Ctr(nc, f"qwl{e}{i}") for i in range(2)] for e in "dp"}
        self.qA, self.qB = [], []
        for (ws, wd) in ((self.gwd, self.sgw), (self.uwd, self.suw)):
            for r in range(16):
                for c in range(6):
                    self.qA.append((ws, wd, 128 * r, 512 * c))
                for c in range(6, 12):
                    self.qB.append((ws, wd, 128 * r, 512 * c))
        for r in range(48):
            lst = self.qA if r < 24 else self.qB
            for c in range(4):
                lst.append((self.dwd, self.sdw, 128 * r, 512 * c))
        self.qA += self.qB[:96]
        self.qB = self.qB[96:]
        self.qpos_i = {id(self.qA): 0, id(self.qB): 0}
        self._sid = 0

    def ctr(self, name):
        self._sid += 1
        return Ctr(self.nc, f"{name}{self._sid}")

    def build(self, stages=99):
        self.prologue()
        if stages < 1:
            return self.nc
        self.mod_matvec()
        if stages < 2:
            return self.nc
        self.derived_vectors()
        if stages < 3:
            return self.nc
        with ExitStack() as qes:
            sb = lambda n, sh, d: qes.enter_context(self.nc.sbuf_tensor(n, sh, d))
            self.qset_s = self.make_qset(sb, "s", 512)
            self.sweep_h1()
            if stages >= 4:
                self.sweep_proj("q")
            if stages >= 5:
                self.sweep_proj("k")
            if stages >= 6:
                self.sweep_proj("v")
            if stages >= 7:
                while self.qpos_i[id(self.qA)] < len(self.qA):
                    self.quant_pairs_block(self.qset_s,
                                           self.quant_take(self.qA, 8))
            with self.nc.Block() as blk:
                @blk.sync
                def _(s):
                    for sl_ in range(2):
                        s.wait_ge(self.qset_s["wl"][sl_].h,
                                  self.qset_s["wl"][sl_].n)
        if stages >= 8:
            self.sweep_attn()
        if stages >= 9:
            with ExitStack() as qes2:
                sb2 = lambda n, sh, d: qes2.enter_context(
                    self.nc.sbuf_tensor(n, sh, d))
                self.qset_t = self.make_qset(sb2, "t", 512)
                self.sweep_ffn(0)
                while self.qpos_i[id(self.qB)] < len(self.qB):
                    self.quant_pairs_block(self.qset_t,
                                           self.quant_take(self.qB, 8))
                with self.nc.Block() as blk:
                    @blk.sync
                    def _(s):
                        for sl_ in range(2):
                            s.wait_ge(self.qset_t["wl"][sl_].h,
                                      self.qset_t["wl"][sl_].n)
            self.sweep_ffn(1)
            self.finale()
            assert (self.qpos_i[id(self.qA)] >= len(self.qA)
                    and self.qpos_i[id(self.qB)] >= len(self.qB)), \
                "quantization incomplete"
        return self.nc

    def make_qset(self, alloc, tag, FD):
        qs = dict(
            FD=FD,
            raw=[alloc(f"qr{tag}{i}", [128, FD], F32) for i in range(2)],
            s1=[alloc(f"qa{tag}{i}", [128, FD], F32) for i in range(2)],
            s3=[alloc(f"qw{tag}{i}", [128, FD], F32) for i in range(2)],
            s4=[alloc(f"qc{tag}{i}", [128, FD], F32) for i in range(2)],
            b=[[alloc(f"qb{tag}{i}_{j}", [128, FD], BF16) for j in range(5)]
               for i in range(2)],
            out=[alloc(f"qo{tag}{i}", [128, FD], BF16) for i in range(2)],
            dl=[self.ctr(f"qdl{tag}"), self.ctr(f"qdl{tag}")],
            wl=[self.ctr(f"qwl{tag}"), self.ctr(f"qwl{tag}")],
            ev=self.ctr(f"qev{tag}"),
            ea=self.ctr(f"qea{tag}"),
        )
        return qs

    def _quant_act_pair(self, s, qs, pair, dlt, evt, wlt):
        s.wait_ge(qs["ev"].h, evt)
        for sl_ in range(len(pair)):
            s.wait_ge(qs["wl"][sl_].h, wlt[sl_])
            s.wait_ge(qs["dl"][sl_].h, dlt[sl_])
        last = None
        for sl_ in range(len(pair)):
            s.activation(qs["out"][sl_][:], qs["raw"][sl_][:], AF.Sign)
            last = s.activation(qs["s1"][sl_][:], qs["raw"][sl_][:], AF.Abs)
        qs["ea"].inc(last)

    def _quant_dve_pair(self, v, qs, pair, eat):
        TH = [2.0 ** -2, 2.0 ** -3, 2.0 ** -4, 2.0 ** -5]
        P = range(len(pair))
        v.wait_ge(qs["ea"].h, eat)
        raw, s1, s3, s4 = qs["raw"], qs["s1"], qs["s3"], qs["s4"]
        b, out = qs["b"], qs["out"]
        for i in P:
            for j in range(4):
                v.tensor_scalar(b[i][j][:], s1[i][:], TH[j], 1.0,
                                ALU.is_lt, ALU.add)
        v.drain()
        for i in P:
            v.tensor_tensor(b[i][0][:], b[i][0][:], b[i][1][:], ALU.mult)
            v.tensor_tensor(b[i][2][:], b[i][2][:], b[i][3][:], ALU.mult)
        v.drain()
        for i in P:
            v.tensor_tensor(b[i][1][:], b[i][0][:], b[i][2][:], ALU.mult)
        v.drain()
        for i in P:
            v.scalar_tensor_tensor(raw[i][:], s1[i][:], 16.0, b[i][1][:],
                                   ALU.mult, ALU.mult)
            v.tensor_scalar(b[i][0][:], s1[i][:], TH[0], 1.0, ALU.is_ge,
                            ALU.add)
            v.tensor_scalar(b[i][2][:], s1[i][:], TH[1], 1.0, ALU.is_ge,
                            ALU.add)
            v.tensor_scalar(b[i][3][:], s1[i][:], TH[2], 1.0, ALU.is_ge,
                            ALU.add)
            v.tensor_scalar(b[i][4][:], s1[i][:], TH[3], 1.0, ALU.is_ge,
                            ALU.add)
        v.drain()
        for i in P:
            v.tensor_scalar(s4[i][:], raw[i][:], 4.0, 4.0, ALU.is_ge,
                            ALU.mult)
            v.tensor_tensor(b[i][0][:], b[i][0][:], b[i][2][:], ALU.mult)
            v.tensor_tensor(b[i][3][:], b[i][3][:], b[i][4][:], ALU.mult)
        v.drain()
        for i in P:
            v.tensor_tensor(s3[i][:], raw[i][:], s4[i][:], ALU.subtract)
            v.tensor_tensor(b[i][0][:], b[i][0][:], b[i][3][:], ALU.mult)
        v.drain()
        for i in P:
            v.tensor_scalar(s4[i][:], s3[i][:], 2.0, 2.0, ALU.is_ge, ALU.mult)
            v.tensor_single_scalar(b[i][0][:], b[i][0][:], 2.0 ** -9,
                                   ALU.mult)
        v.drain()
        for i in P:
            v.tensor_tensor(s3[i][:], s3[i][:], s4[i][:], ALU.subtract)
            v.tensor_tensor(b[i][0][:], b[i][0][:], out[i][:], ALU.mult)
        v.drain()
        for i in P:
            v.tensor_single_scalar(s4[i][:], s3[i][:], 1.0, ALU.is_ge)
        v.drain()
        for i in P:
            v.tensor_tensor(s3[i][:], s3[i][:], s4[i][:], ALU.subtract)
        v.drain()
        for i in P:
            v.tensor_single_scalar(s4[i][:], s3[i][:], 0.75, ALU.is_ge)
        v.drain()
        for i in P:
            v.scalar_tensor_tensor(s4[i][:], s3[i][:], 0.25, s4[i][:],
                                   ALU.is_gt, ALU.add)
        v.drain()
        for i in P:
            v.tensor_tensor(s3[i][:], raw[i][:], s3[i][:], ALU.subtract)
        v.drain()
        for i in P:
            v.scalar_tensor_tensor(s3[i][:], s3[i][:], 2.0, s4[i][:],
                                   ALU.mult, ALU.add)
        v.drain()
        last = None
        for i in P:
            last = v.tensor_tensor(out[i][:], s3[i][:], b[i][0][:], ALU.mult)
        qs["ev"].inc(last)
        v.drain()

    def quant_take(self, lst, n):
        take = lst[self.qpos_i[id(lst)]:self.qpos_i[id(lst)] + n]
        self.qpos_i[id(lst)] += len(take)
        return take

    def quant_pairs_block(self, qset, units):
        nc = self.nc
        if not units:
            return
        FD = qset["FD"]
        pairs = [units[i:i + 2] for i in range(0, len(units), 2)]
        with nc.Block() as blk:
            ev0 = qset["ev"].n
            ea0 = qset["ea"].n
            wl0 = [qset["wl"][0].n, qset["wl"][1].n]
            dlt = []

            @blk.sync
            def _(s):
                for pi, pair in enumerate(pairs):
                    t = []
                    if pi >= 1:
                        s.wait_ge(qset["ev"].h, ev0 + pi)
                    for sl_, (wsrc, wdst, r0, c0) in enumerate(pair):
                        if pi >= 1:
                            s.wait_ge(qset["wl"][sl_].h, qset["wl"][sl_].n)
                        qset["dl"][sl_].inc(
                            s.dma_start(qset["raw"][sl_][:, :],
                                        wsrc[r0:r0 + 128, c0:c0 + FD]), 16)
                        t.append(qset["dl"][sl_].n)
                    dlt.append(t)
                    if pi >= 1:
                        for sl_, (wsrc, wdst, r0, c0) in enumerate(pairs[pi - 1]):
                            qset["wl"][sl_].inc(
                                s.dma_start(wdst[r0:r0 + 128, c0:c0 + FD],
                                            qset["out"][sl_][:, :]), 16)
                s.wait_ge(qset["ev"].h, ev0 + len(pairs))
                for sl_, (wsrc, wdst, r0, c0) in enumerate(pairs[-1]):
                    qset["wl"][sl_].inc(
                        s.dma_start(wdst[r0:r0 + 128, c0:c0 + FD],
                                    qset["out"][sl_][:, :]), 16)
                for sl_ in range(2):
                    s.wait_ge(qset["wl"][sl_].h, qset["wl"][sl_].n)

            @blk.scalar
            def _(s):
                for pi, pair in enumerate(pairs):
                    self._quant_act_pair(s, qset, pair, dlt[pi], ev0 + pi,
                                         [wl0[0] + 16 * pi, wl0[1] + 16 * pi])

            @blk.vector
            def _(v):
                for pi, pair in enumerate(pairs):
                    self._quant_dve_pair(v, qset, pair, ea0 + pi + 1)

    def quant_inline_make(self, blk, qset, units):
        if not units:
            return []
        FD = qset["FD"]
        pairs = [units[i:i + 2] for i in range(0, len(units), 2)]
        ev0 = qset["ev"].n
        ea0 = qset["ea"].n
        wl0 = [qset["wl"][0].n, qset["wl"][1].n]
        dlt = []

        @blk.sync
        def _(s):
            for pi, pair in enumerate(pairs):
                t = []
                s.wait_ge(qset["ev"].h, ev0 + pi)
                for sl_, (wsrc, wdst, r0, c0) in enumerate(pair):
                    s.wait_ge(qset["wl"][sl_].h, qset["wl"][sl_].n)
                    qset["dl"][sl_].inc(
                        s.dma_start(qset["raw"][sl_][:, :],
                                    wsrc[r0:r0 + 128, c0:c0 + FD]), 16)
                    t.append(qset["dl"][sl_].n)
                dlt.append(t)
                s.wait_ge(qset["ev"].h, ev0 + pi + 1)
                for sl_, (wsrc, wdst, r0, c0) in enumerate(pair):
                    qset["wl"][sl_].inc(
                        s.dma_start(wdst[r0:r0 + 128, c0:c0 + FD],
                                    qset["out"][sl_][:, :]), 16)

        acts = []
        hooks = []
        for pi, pair in enumerate(pairs):
            def act(s, pi=pi, pair=pair):
                self._quant_act_pair(s, qset, pair, dlt[pi], ev0 + pi,
                                     [wl0[0] + 16 * pi, wl0[1] + 16 * pi])

            def hook(v, pi=pi, pair=pair):
                self._quant_dve_pair(v, qset, pair, ea0 + pi + 1)
            acts.append(act)
            hooks.append(hook)
        return hooks, acts

    def prologue(self):
        nc = self.nc
        dS = self.ctr("dS")
        with nc.Block() as blk:
            @blk.sync
            def _(s):
                for dst, src, w in [(self.scv, self.cvec, 16),
                                    (self.bqv, self.bqd, 16),
                                    (self.bkv, self.bkd, 16),
                                    (self.dscv, self.dscd, 16),
                                    (self.gscv, self.gscd, 48),
                                    (self.uscv, self.uscd, 48)]:
                    dS.inc(s.dma_start(
                        dst[:, 0:w], src[:].rearrange("(j p) -> p j", p=128),
                        allow_slow_non_contiguous=True), 16)
                s.wait_ge(dS.h, dS.n)

            @blk.gpsimd
            def _(g):
                g.iota(self.iotat[:, :], [[1, 256]], channel_multiplier=-1,
                       allow_small_or_imprecise_dtypes=True)

        with nc.Block() as blk:
            @blk.vector
            def _(v):
                v.tensor_single_scalar(self.ident[:, :], self.iotat[:, 0:128],
                                       0.0, ALU.is_equal)
                v.tensor_scalar(self.mask0[:, :], self.iotat[:, 0:128],
                                0.0, NEG, ALU.is_gt, ALU.mult)
                v.tensor_scalar(self.mask128[:, :], self.iotat[:, :],
                                128.0, NEG, ALU.is_gt, ALU.mult)
                v.drain()
                for j in range(2):
                    v.tensor_copy(self.mask2x[:, 256 * j:256 * (j + 1)],
                                  self.mask128[:, :])
                for j in range(4):
                    v.tensor_copy(self.mask4x[:, 128 * j:128 * (j + 1)],
                                  self.mask0[:, :])
                v.drain()
                v.memset(self.onesf[:, :], 1.0)
                v.memset(self.cEPS[:, :], EPS)
                v.memset(self.cISQ[:, :], 128.0)

            @blk.scalar
            def _(s):
                s.activation(self.scv[:, :], self.scv[:, :], AF.Silu)
        with nc.Block() as blk:
            @blk.scalar
            def _(s):
                s.activation(self.cISQ[:, :], self.cISQ[:, :], AF.Sqrt)
        with nc.Block() as blk:
            @blk.vector
            def _(v):
                v.reciprocal(self.cISQ[:, :], self.cISQ[:, :])
                v.drain()

    def mod_matvec(self):
        nc = self.nc
        nb = 6 * C // 128
        mm = self.ctr("mm")
        ev = self.ctr("ev")
        dP = self.ctr("dP")
        dPs = [self.ctr("dPmA"), self.ctr("dPmB")]
        mes = ExitStack()
        with nc.Block() as blk:
            wmc = [mes.enter_context(nc.sbuf_tensor(f"wmc{i}", [128, C], F32))
                   for i in range(2)]

            @blk.gpsimd
            def _(g):
                for ob in range(nb):
                    if ob >= 2:
                        g.wait_ge(mm.h, ob - 1)
                    src = self.wmod[:, 128 * ob:128 * (ob + 1)]
                    dPs[ob % 2].inc(g.dma_start(
                        wmc[ob % 2][:].rearrange("p (kc o) -> p kc o", kc=PC),
                        src.rearrange("(kc p) o -> p kc o", p=128)), 16)

            @blk.tensor
            def _(t):
                t = Track(t)
                for ob in range(nb):
                    t.wait_ge(dPs[ob % 2].h, 16 * (ob // 2 + 1))
                    for kc in range(PC):
                        t.matmul(self.ps[6][:, ob:ob + 1],
                                 wmc[ob % 2][:, 128 * kc:128 * (kc + 1)],
                                 self.scv[:, kc:kc + 1],
                                 start=(kc == 0), stop=(kc == PC - 1))
                    mm.inc(t.last)

            @blk.vector
            def _(v):
                v.wait_ge(mm.h, nb)
                v.tensor_copy(self.modv[:, :], self.ps[6][:, 0:96])
        mes.close()

    def derived_vectors(self):
        nc = self.nc
        ca = self.modv[:, 16:32]
        ga = self.modv[:, 32:48]
        cb = self.modv[:, 64:80]
        gb = self.modv[:, 80:96]
        with nc.Block() as blk:
            @blk.scalar
            def _(s):
                s.activation(self.tmpv[:, 0:16], ga, AF.Abs)
                s.activation(self.tmpv[:, 16:32], gb, AF.Abs)

            @blk.vector
            def _(v):
                v.tensor_single_scalar(self.ca1[:, :], ca, 1.0, ALU.add)
                v.tensor_single_scalar(self.cb1[:, :], cb, 1.0, ALU.add)
                v.tensor_scalar(self.gav[:, :], ga, 0.0, 2.0,
                                ALU.is_ge, ALU.mult)
                v.tensor_scalar(self.gbds[:, :], gb, 0.0, 2.0,
                                ALU.is_ge, ALU.mult)
                v.drain()
                v.tensor_single_scalar(self.gav[:, :], self.gav[:, :],
                                       1.0, ALU.subtract)
                v.tensor_single_scalar(self.gbds[:, :], self.gbds[:, :],
                                       1.0, ALU.subtract)
        with nc.Block() as blk:
            @blk.vector
            def _(v):
                v.tensor_single_scalar(self.tmpv[:, 0:16], self.tmpv[:, 0:16],
                                       0.25, ALU.max)
                v.tensor_single_scalar(self.tmpv[:, 16:32],
                                       self.tmpv[:, 16:32], 0.25, ALU.max)
                v.drain()
                v.tensor_tensor(self.gav[:, :], self.gav[:, :],
                                self.tmpv[:, 0:16], ALU.mult)
                v.tensor_tensor(self.gbds[:, :], self.gbds[:, :],
                                self.tmpv[:, 16:32], ALU.mult)
                v.drain()
                v.tensor_tensor(self.gbds[:, :], self.gbds[:, :],
                                self.dscv[:, :], ALU.mult)

    def sweep_h1(self):
        nc = self.nc
        Tn = 256
        NH = self.T // Tn
        es = ExitStack()
        A = lambda n, sh, d: es.enter_context(nc.sbuf_tensor(n, sh, d))
        xst = [A(f"s1x{i}", [128, C], F32) for i in range(4)]
        xT = A("s1xT", [128, PC * Tn], F32)
        sq2 = A("s1sq", [128, 2 * Tn], F32)
        stat = A("s1st", [128, 4 * Tn], F32)
        h1f = A("s1h1f", [128, PC * Tn], F32)
        dP = self.ctr("dP1")
        dO = self.ctr("dO1")
        mm = self.ctr("mm1")
        ev = self.ctr("ev1")
        ea = self.ctr("ea1")
        with nc.Block() as blk:
            @blk.gpsimd
            def _(g):
                for r in range(2):
                    dP.inc(g.dma_start(
                        xst[r][:, :], self.x_in[128 * r:128 * (r + 1), :]), 16)
        self._pref = dP.n
        for hc in range(NH):
            self._s1_block(hc, NH, Tn, xst, xT, sq2, stat, h1f,
                           dP, dO, mm, ev, ea)
        with nc.Block() as blk:
            @blk.sync
            def _(s):
                s.wait_ge(dO.h, dO.n)
        es.close()

    def _s1_block(self, hc, NH, Tn, xst, xT, sq2, stat, h1f,
                  dP, dO, mm, ev, ea):
        nc = self.nc
        t0 = Tn * hc
        sl = slice(t0, t0 + Tn)
        m_t = stat[:, 0:Tn]
        rs_t = stat[:, Tn:2 * Tn]
        var_t = stat[:, 2 * Tn:3 * Tn]
        sd_t = stat[:, 3 * Tn:4 * Tn]
        with nc.Block() as blk:
            mm0, ev0, ea0, do0 = mm.n, ev.n, ea.n, dO.n
            pref = self._pref
            ngrp = 8

            @blk.gpsimd
            def _(g):
                g.wait_ge(ev.h, ev0 + ngrp)
                if hc + 1 < NH:
                    for r in range(2):
                        dP.inc(g.dma_start(
                            xst[((hc + 1) % 2) * 2 + r][:, :],
                            self.x_in[Tn * (hc + 1) + 128 * r:
                                      Tn * (hc + 1) + 128 * (r + 1), :]), 16)
                    self._pref = dP.n
                for pc in range(PC):
                    dO.inc(g.dma_start(
                        self.sxT[128 * pc:128 * (pc + 1), sl],
                        xT[:, Tn * pc:Tn * (pc + 1)]), 16)
                g.wait_ge(ev.h, ev0 + 10)
                for pc in range(PC):
                    dO.inc(g.dma_start(
                        self.sh1f[128 * pc:128 * (pc + 1), sl],
                        h1f[:, Tn * pc:Tn * (pc + 1)]), 16)

            @blk.tensor
            def _(t):
                t = Track(t)
                t.wait_ge(dP.h, pref)
                for grp in range(ngrp):
                    bank = self.ps[grp % 2]
                    if grp >= 2:
                        t.wait_ge(ev.h, ev0 + grp - 1)
                    for i in range(4):
                        idx = grp * 4 + i
                        r, pc = idx // PC, idx % PC
                        t.transpose(bank[:, 128 * i:128 * (i + 1)],
                                    xst[(hc % 2) * 2 + r][:, 128 * pc:
                                                          128 * (pc + 1)],
                                    self.ident[:, :])
                    mm.inc(t.last)
                for pc in range(PC):
                    t.wait_ge(ea.h, ea0 + pc + 1)
                    t.matmul(self.ps[3][:, 0:Tn], self.onesf[:, :],
                             sq2[:, Tn * (pc % 2):Tn * (pc % 2 + 1)],
                             start=(pc == 0), stop=(pc == PC - 1))
                    mm.inc(t.last)
                t.wait_ge(ev.h, ev0 + ngrp)
                for pc in range(PC):
                    t.matmul(self.ps[2][:, 0:Tn], self.onesf[:, :],
                             xT[:, Tn * pc:Tn * (pc + 1)],
                             start=(pc == 0), stop=(pc == PC - 1))
                mm.inc(t.last)

            @blk.scalar
            def _(s):
                s.wait_ge(ev.h, ev0 + ngrp)
                for pc in range(PC):
                    if pc >= 2:
                        s.wait_ge(mm.h, mm0 + ngrp + pc - 1)
                    ea.inc(s.activation(
                        sq2[:, Tn * (pc % 2):Tn * (pc % 2 + 1)],
                        xT[:, Tn * pc:Tn * (pc + 1)], AF.Square))
                s.wait_ge(ev.h, ev0 + 9)
                ea.inc(s.activation(sd_t, var_t, AF.Sqrt,
                                    bias=self.cEPS[:, 0:1], scale=1.0))

            @blk.vector
            def _(v):
                for grp in range(ngrp):
                    bank = self.ps[grp % 2]
                    v.wait_ge(mm.h, mm0 + grp + 1)
                    for i in range(4):
                        idx = grp * 4 + i
                        r, pc = idx // PC, idx % PC
                        v.tensor_copy(xT[:, Tn * pc + 128 * r:
                                         Tn * pc + 128 * (r + 1)],
                                      bank[:, 128 * i:128 * (i + 1)])
                    v.drain()
                    ev.inc(v.engine_nop())
                v.wait_ge(mm.h, mm0 + ngrp + PC + 1)
                v.tensor_single_scalar(m_t, self.ps[2][:, 0:Tn], RC, ALU.mult)
                v.drain()
                v.tensor_tensor(var_t, m_t, m_t, ALU.mult)
                v.drain()
                v.scalar_tensor_tensor(var_t, self.ps[3][:, 0:Tn], RC, var_t,
                                       ALU.mult, ALU.subtract)
                v.drain()
                ev.inc(v.engine_nop())
                v.wait_ge(ea.h, ea0 + PC + 1)
                v.reciprocal(rs_t, sd_t)
                v.drain()
                v.wait_ge(dO.h, do0 + 16 * PC)
                self.ln_apply(v, xT, m_t, rs_t, self.ca1, self.modv[:, 0:16],
                              h1f, None, Tn)
                ev.inc(v.engine_nop())

    def ln_apply(self, v, x3, m, rs, scale_pp, bias_pp, outh, outl, Tn):
        for pc in range(PC):
            s = slice(Tn * pc, Tn * (pc + 1))
            v.tensor_tensor(x3[:, s], x3[:, s], m[:, 0:Tn], ALU.subtract)
        v.drain()
        for pc in range(PC):
            s = slice(Tn * pc, Tn * (pc + 1))
            v.tensor_tensor(x3[:, s], x3[:, s], rs[:, 0:Tn], ALU.mult)
        v.drain()
        if outl is None:
            for pc in range(PC):
                s = slice(Tn * pc, Tn * (pc + 1))
                v.tensor_scalar(outh[:, s], x3[:, s], scale_pp[:, pc:pc + 1],
                                bias_pp[:, pc:pc + 1], ALU.mult, ALU.add)
            v.drain()
            return
        for pc in range(PC):
            s = slice(Tn * pc, Tn * (pc + 1))
            v.tensor_scalar(x3[:, s], x3[:, s], scale_pp[:, pc:pc + 1],
                            bias_pp[:, pc:pc + 1], ALU.mult, ALU.add)
        v.drain()
        for pc in range(PC):
            s = slice(Tn * pc, Tn * (pc + 1))
            v.tensor_copy(outh[:, s], x3[:, s])
        v.drain()
        for pc in range(PC):
            s = slice(Tn * pc, Tn * (pc + 1))
            v.scalar_tensor_tensor(outl[:, s], outh[:, s], -1.0, x3[:, s],
                                   ALU.mult, ALU.add)
        v.drain()

    def weight_transform(self, W, wt, dP, mm, ev):
        nc = self.nc
        wes = ExitStack()
        dWs = [self.ctr("dWA")]
        with nc.Block() as blk:
            wraw = [wes.enter_context(nc.sbuf_tensor(f"wr_{W.name}0",
                                                     [128, C], F32))]
            mm0, ev0, dp0 = mm.n, ev.n, dP.n

            @blk.gpsimd
            def _(g):
                for ob in range(PC):
                    if ob >= 1:
                        g.wait_ge(mm.h, mm0 + ob * 4)
                    dWs[0].inc(g.dma_start(
                        wraw[0][:, :], W[128 * ob:128 * (ob + 1), :]), 16)

            @blk.tensor
            def _(t):
                t = Track(t)
                for ob in range(PC):
                    t.wait_ge(dWs[0].h, 16 * (ob + 1))
                    for jg in range(4):
                        gi = ob * 4 + jg
                        bank = self.ps[4 + gi % 2]
                        if gi >= 2:
                            t.wait_ge(ev.h, ev0 + gi - 1)
                        for kk in range(4):
                            kc = 4 * jg + kk
                            t.transpose(bank[:, 128 * kk:128 * (kk + 1)],
                                        wraw[0][:, 128 * kc:128 * (kc + 1)],
                                        self.ident[:, :])
                        mm.inc(t.last)

            @blk.vector
            def _(v):
                h3 = wt[:].rearrange("p (kc o) -> p kc o", kc=PC)
                for ob in range(PC):
                    for jg in range(4):
                        gi = ob * 4 + jg
                        bank = self.ps[4 + gi % 2]
                        v.wait_ge(mm.h, mm0 + gi + 1)
                        dh = h3[:, 4 * jg:4 * jg + 4, 128 * ob:128 * (ob + 1)]
                        src = bank[:].rearrange("p (k o) -> p k o", k=4)
                        ev.inc(v.tensor_copy(dh, src))
                        v.drain()
        wes.close()

    def sweep_proj(self, which):
        nc = self.nc
        Tn = 256
        NH = self.T // Tn
        es = ExitStack()
        A = lambda n, sh, d: es.enter_context(nc.sbuf_tensor(n, sh, d))
        W = {"q": self.Wq, "k": self.Wk, "v": self.Wv}[which]
        wt = A(f"wt_{which}", [128, PC * C], F32)
        h1f = A(f"h1f_{which}", [128, PC * Tn], F32)
        NS = 2 if which == "v" else 4
        pf = A(f"pf_{which}", [128, NS * 512], F32)
        dP = self.ctr("dPp")
        dO = self.ctr("dOp")
        dOs = [self.ctr("dOs") for _ in range(4)]
        mm = self.ctr("mmp")
        ev = self.ctr("evp")
        ea = self.ctr("eap")
        self.weight_transform(W, wt, dP, mm, ev)
        if which == "v":
            self.bvbc = A("bvbc", [128, C], F32)
            bvrow = A("bvrow", [1, C], F32)
            dBv = self.ctr("dBv")
            mmBv = self.ctr("mmBv")
            evBv = self.ctr("evBv")
            with nc.Block() as blk:
                @blk.sync
                def _(s):
                    dBv.inc(s.dma_start(
                        bvrow[:, :],
                        self.bvd[:].rearrange("(o c) -> o c", o=1)), 16)

                @blk.tensor
                def _(t):
                    t = Track(t)
                    t.wait_ge(dBv.h, 16)
                    for j in range(C // 512):
                        if j >= 2:
                            t.wait_ge(evBv.h, j - 1)
                        t.matmul(self.ps[6 + j % 2][:, :],
                                 self.onesf[0:1, 0:128],
                                 bvrow[0:1, 512 * j:512 * (j + 1)],
                                 start=True, stop=True)
                        mmBv.inc(t.last)

                @blk.vector
                def _(v):
                    for j in range(C // 512):
                        v.wait_ge(mmBv.h, j + 1)
                        evBv.inc(v.tensor_copy(
                            self.bvbc[:, 512 * j:512 * (j + 1)],
                            self.ps[6 + j % 2][:, :]))
                        v.drain()
        with nc.Block() as blk:
            @blk.gpsimd
            def _(g):
                for pc in range(PC):
                    dP.inc(g.dma_start(
                        h1f[:, Tn * pc:Tn * (pc + 1)],
                        self.sh1f[128 * pc:128 * (pc + 1), 0:Tn]), 16)
        self._pref = dP.n
        for hc in range(NH):
            qu = self.quant_take(self.qA, 8)
            self._proj_block(which, hc, NH, Tn, wt, h1f,
                             pf, dP, dOs, mm, ev, ea, qu)
        with nc.Block() as blk:
            @blk.sync
            def _(s):
                for d in dOs:
                    s.wait_ge(d.h, d.n)
        es.close()

    def _proj_block(self, which, hc, NH, Tn, wt, h1f,
                    pf, dP, dOs, mm, ev, ea, qu=()):
        nc = self.nc
        t0 = Tn * hc
        sl = slice(t0, t0 + Tn)
        is_v = (which == "v")
        sdf = {"q": self.sqf, "k": self.skf, "v": self.svf}[which]
        bias = {"q": self.bqv, "k": self.bkv, "v": None}[which]
        NU = PC if not is_v else 8
        NN = Tn if not is_v else 512
        with nc.Block() as blk:
            mm0, ev0, ea0 = mm.n, ev.n, ea.n
            dob = [d.n for d in dOs]
            pref = self._pref
            qhooks, qacts = (self.quant_inline_make(blk, self.qset_s,
                                                    list(qu))
                             if qu else ([], []))

            @blk.tensor
            def _(t):
                t = Track(t)
                t.wait_ge(dP.h, pref)
                for u in range(NU):
                    bank = self.ps[u % 6]
                    if u >= 6:
                        if is_v:
                            t.wait_ge(ev.h, ev0 + u - 5)
                        else:
                            t.wait_ge(ea.h, ea0 + u - 5)
                    if not is_v:
                        for kc in range(PC):
                            t.matmul(bank[:, 0:NN],
                                     wt[:, C * kc + 128 * u:
                                        C * kc + 128 * (u + 1)],
                                     h1f[:, Tn * kc:Tn * (kc + 1)],
                                     start=(kc == 0),
                                     stop=(kc == PC - 1))
                    else:
                        r, n4 = u // 4, u % 4
                        for kc in range(PC):
                            t.matmul(bank[:, 0:NN],
                                     h1f[:, Tn * kc + 128 * r:
                                         Tn * kc + 128 * (r + 1)],
                                     wt[:, C * kc + 512 * n4:
                                        C * kc + 512 * (n4 + 1)],
                                     start=(kc == 0),
                                     stop=(kc == PC - 1))
                    mm.inc(t.last)

            if not is_v:
                @blk.scalar
                def _(s):
                    for k in (0, 1):
                        if k < len(qacts):
                            qacts[k](s)
                    for u in range(PC):
                        if u % 4 == 0 and u >= 4 and u // 4 + 1 < len(qacts):
                            qacts[u // 4 + 1](s)
                        s.wait_ge(mm.h, mm0 + u + 1)
                        s.wait_ge(dOs[u % 4].h, dob[u % 4] + 16 * (u // 4))
                        ea.inc(s.activation(
                            pf[:, Tn * (u % 4):Tn * (u % 4 + 1)],
                            self.ps[u % 6][:, 0:Tn], AF.Identity,
                            bias=bias[:, u:u + 1], scale=1.0))

                @blk.vector
                def _(v):
                    for hk in qhooks:
                        hk(v)
            else:
                @blk.scalar
                def _(s):
                    for k in range(len(qacts)):
                        qacts[k](s)

                @blk.vector
                def _(v):
                    for u in range(8):
                        r, n4 = u // 4, u % 4
                        if u % 2 == 0 and len(qhooks) > u // 2:
                            qhooks[u // 2](v)
                        v.wait_ge(mm.h, mm0 + u + 1)
                        v.wait_ge(dOs[u % 2].h, dob[u % 2] + 16 * (u // 2))
                        h = slice(512 * (u % 2), 512 * (u % 2 + 1))
                        ev.inc(v.scalar_tensor_tensor(
                            pf[:, h], self.ps[u % 6][:, 0:512], 1.0,
                            self.bvbc[:, 512 * n4:512 * (n4 + 1)],
                            ALU.bypass, ALU.add))
                        v.drain()

            @blk.gpsimd
            def _(g):
                for u in range(NU):
                    if not is_v:
                        g.wait_ge(ea.h, ea0 + u + 1)
                        h = slice(Tn * (u % 4), Tn * (u % 4 + 1))
                        dOs[u % 4].inc(g.dma_start(
                            sdf[128 * u:128 * (u + 1), sl], pf[:, h]), 16)
                    else:
                        g.wait_ge(ev.h, ev0 + u + 1)
                        r, n4 = u // 4, u % 4
                        h = slice(512 * (u % 2), 512 * (u % 2 + 1))
                        rows = slice(t0 + 128 * r, t0 + 128 * (r + 1))
                        cols = slice(512 * n4, 512 * (n4 + 1))
                        dOs[u % 2].inc(g.dma_start(sdf[rows, cols],
                                                   pf[:, h]), 16)
                if hc + 1 < NH:
                    g.wait_ge(mm.h, mm0 + NU)
                    nsl = slice(Tn * (hc + 1), Tn * (hc + 2))
                    for pc in range(PC):
                        dP.inc(g.dma_start(
                            h1f[:, Tn * pc:Tn * (pc + 1)],
                            self.sh1f[128 * pc:128 * (pc + 1), nsl]), 16)
                    self._pref = dP.n

    def sweep_attn(self):
        nc = self.nc
        Tc = 512
        NCH = self.T // Tc
        WW = Tc + 128
        NRT = WW // 128
        es = ExitStack()
        A = lambda n, sh, d: es.enter_context(nc.sbuf_tensor(n, sh, d))
        bufs = dict(
            qw=A("qw", [128, PC * Tc], F32),
            kw=A("kw", [128, PC * WW], F32),
            vw=A("vw", [128, NRT * C], F32),
            xbuf=A("xbuf", [128, PC * Tc], F32),
            smb=A("smb", [128, 8 * 256], F32),
            athf=A("athf", [128, 8 * 256], F32),
            nmx=A("nmx", [128, 24], F32),
            sq2=A("s5sq", [128, 2 * Tc], F32),
            stat=A("s5st", [128, 2 * Tc], F32),
        )
        dP = self.ctr("dPa")
        dO = self.ctr("dOa")
        mm = self.ctr("mma")
        ev = self.ctr("eva")
        ea = self.ctr("eaa")
        for ch in range(NCH):
            self._attn_block(ch, NCH, Tc, WW, NRT, bufs, dP, dO, mm, ev, ea)
        with nc.Block() as blk:
            @blk.sync
            def _(s):
                s.wait_ge(dO.h, dO.n)
        es.close()

    def _attn_block(self, ch, NCH, Tc, WW, NRT, B, dP, dO, mm, ev, ea):
        nc = self.nc
        T = self.T
        t0 = Tc * ch
        sl = slice(t0, t0 + Tc)
        qw = B["qw"]
        kw = B["kw"]
        vw = B["vw"]
        qwb = qw[:].bitcast(BF16)
        h2h = qwb[:, 0:PC * Tc]
        h2l = qwb[:, PC * Tc:2 * PC * Tc]
        xbuf = B["xbuf"]
        smb = B["smb"]
        ebf = smb
        athf = B["athf"]
        nmx = B["nmx"]
        sq2, stat = B["sq2"], B["stat"]
        m_t = stat[:, 0:Tc]
        rs_t = stat[:, Tc:2 * Tc]
        var_t = sq2[:, 0:Tc]
        sd_t = sq2[:, Tc:2 * Tc]
        lo = t0 - 128
        cts = []
        for ti in range(Tc // 128):
            for cls in range(2):
                par = ti % 2
                if (cls == 0 and par == 0) or (cls == 1 and par == 1):
                    nk, off, wo = 128, 0, 128 * ti + 128
                else:
                    nk, off, wo = 256, 128, 128 * ti
                cts.append(dict(ti=ti, cls=cls, nk=nk, off=off, wo=wo))
        NCT = len(cts)
        with nc.Block() as blk:
            mm0, ev0, ea0, do0 = mm.n, ev.n, ea.n, dO.n
            mmct = []
            base = mm0
            for c, ct in enumerate(cts):
                nbk = 4 if ct["nk"] == 256 else 2
                mmct.append(base)
                base += nbk + 2
            mm_end = base
            nbks = [4 if ct["nk"] == 256 else 2 for ct in cts]

            @blk.gpsimd
            def _(g):
                g.wait_ge(dO.h, do0)
                for pc in range(PC):
                    rows = slice(128 * pc, 128 * (pc + 1))
                    dP.inc(g.dma_start(
                        qw[:, Tc * pc:Tc * (pc + 1)], self.sqf[rows, sl]), 16)
                    if ch == 0:
                        dP.inc(g.dma_start(
                            kw[:, WW * pc:WW * pc + 128],
                            self.skf[rows, T - 128:T]), 16)
                        dP.inc(g.dma_start(
                            kw[:, WW * pc + 128:WW * pc + WW],
                            self.skf[rows, 0:Tc]), 16)
                    else:
                        dP.inc(g.dma_start(
                            kw[:, WW * pc:WW * (pc + 1)],
                            self.skf[rows, lo:t0 + Tc]), 16)
                    dP.inc(g.dma_start(
                        xbuf[:, Tc * pc:Tc * (pc + 1)],
                        self.sxT[rows, sl]), 16)
                for rt in range(NRT):
                    r0 = lo + 128 * rt
                    if r0 < 0:
                        r0 += T
                    rows = slice(r0, r0 + 128)
                    dP.inc(g.dma_start(
                        vw[:, C * rt:C * (rt + 1)], self.svf[rows, :]), 16)
                g.wait_ge(ev.h, ev0 + 3 * NCT + 1)
                for pc in range(PC):
                    dO.inc(g.dma_start(
                        self.sx1[128 * pc:128 * (pc + 1), sl],
                        xbuf[:, Tc * pc:Tc * (pc + 1)]), 16)
                g.wait_ge(ev.h, ev0 + 3 * NCT + 3)
                for pc in range(PC):
                    dO.inc(g.dma_start(
                        self.sh2h[128 * pc:128 * (pc + 1), sl],
                        h2h[:, Tc * pc:Tc * (pc + 1)]), 16)
                    dO.inc(g.dma_start(
                        self.sh2l[128 * pc:128 * (pc + 1), sl],
                        h2l[:, Tc * pc:Tc * (pc + 1)]), 16)

            ldq = dP.n

            @blk.tensor
            def _(t):
                t = Track(t)
                t.wait_ge(dP.h, ldq)
                for c, ct in enumerate(cts):
                    ti, cls, nk, off, wo = (ct["ti"], ct["cls"], ct["nk"],
                                            ct["off"], ct["wo"])
                    side = 4 * (c % 2)
                    nbk = nbks[c]
                    hpb = 8 // nbk
                    h0 = 8 * cls
                    nblk = nk // 128
                    if c >= 2:
                        t.wait_ge(ev.h, ev0 + 3 * (c - 2) + 4)
                    for j in range(nbk):
                        bank = self.ps[side + j]
                        for hh in range(hpb):
                            h = h0 + j * hpb + hh
                            t.matmul(bank[:, nk * hh:nk * (hh + 1)],
                                     qw[:, Tc * h + 128 * ti:
                                        Tc * h + 128 * (ti + 1)],
                                     kw[:, WW * h + wo:WW * h + wo + nk],
                                     start=(hh == 0), stop=(hh == hpb - 1))
                        mm.inc(t.last)
                    t.wait_ge(ea.h, ea0 + 2 * c + 2)
                    for hh in range(8):
                        for j2 in range(nblk):
                            g2 = hh * nblk + j2
                            tb = self.ps[side + g2 // 4]
                            t.transpose(tb[:, 128 * (g2 % 4):
                                            128 * (g2 % 4 + 1)],
                                        ebf[:, nk * hh + 128 * j2:
                                            nk * hh + 128 * (j2 + 1)],
                                        self.ident[:, :])
                    mm.inc(t.last)
                    t.wait_ge(ev.h, ev0 + 3 * c + 2)
                    for jb in range(2):
                        ab = self.ps[side + jb]
                        ng = 4 * nblk
                        gi = 0
                        for hh2 in range(4):
                            hh = 4 * jb + hh2
                            h = h0 + hh
                            for j2 in range(nblk):
                                rt = wo // 128 + j2
                                t.matmul(ab[:, 128 * hh2:128 * (hh2 + 1)],
                                         vw[:, C * rt + 128 * h:
                                            C * rt + 128 * (h + 1)],
                                         athf[:, 128 * (hh * nblk + j2):
                                              128 * (hh * nblk + j2 + 1)],
                                         start=(gi == 0), stop=(gi == ng - 1))
                                gi += 1
                    mm.inc(t.last)
                t.wait_ge(ev.h, ev0 + 3 * NCT + 1)
                for pc in range(PC):
                    t.wait_ge(ea.h, ea0 + 2 * NCT + pc + 1)
                    t.matmul(self.ps[3][:, 0:Tc], self.onesf[:, :],
                             sq2[:, Tc * (pc % 2):Tc * (pc % 2 + 1)],
                             start=(pc == 0), stop=(pc == PC - 1))
                    mm.inc(t.last)
                for pc in range(PC):
                    t.matmul(self.ps[2][:, 0:Tc], self.onesf[:, :],
                             xbuf[:, Tc * pc:Tc * (pc + 1)],
                             start=(pc == 0), stop=(pc == PC - 1))
                mm.inc(t.last)

            @blk.scalar
            def _(s):
                for c, ct in enumerate(cts):
                    nk = ct["nk"]
                    s.wait_ge(ev.h, ev0 + (3 * c if c else 1))
                    if c >= 1:
                        s.wait_ge(mm.h, mmct[c - 1] + nbks[c - 1] + 1)
                    last = None
                    for hh in range(8):
                        last = s.activation(
                            ebf[:, nk * hh:nk * (hh + 1)],
                            smb[:, nk * hh:nk * (hh + 1)],
                            AF.Exp, bias=nmx[:, 8 + hh:9 + hh],
                            scale=self.cISQ[:, 0:1])
                    ea.inc(last)
                for pc in range(PC):
                    s.wait_ge(ev.h, ev0 + 3 * NCT + 1)
                    if pc >= 2:
                        s.wait_ge(mm.h, mm_end + pc - 1)
                    s.activation(sq2[:, Tc * (pc % 2):Tc * (pc % 2 + 1)],
                                 xbuf[:, Tc * pc:Tc * (pc + 1)],
                                 AF.Square).then_inc(ea.h, 1)
                    ea.n += 1
                s.wait_ge(ev.h, ev0 + 3 * NCT + 2)
                ea.inc(s.activation(sd_t, var_t, AF.Sqrt,
                                    bias=self.cEPS[:, 0:1], scale=1.0))

            def tail_copies(v, c):
                ct = cts[c]
                side = 4 * (c % 2)
                ntile = 8 * (ct["nk"] // 128)
                v.wait_ge(mm.h, mmct[c] + nbks[c] + 1)
                for jb in range((ntile + 3) // 4):
                    v.tensor_copy(athf[:, 512 * jb:512 * jb + 512],
                                  self.ps[side + jb][:, 0:512])
                ev.inc(v.engine_nop())
                v.drain()

            def tail_x1(v, c):
                ct = cts[c]
                ti, h0 = ct["ti"], 8 * ct["cls"]
                side = 4 * (c % 2)
                v.wait_ge(mm.h, mmct[c] + nbks[c] + 2)
                for hh in range(8):
                    h = h0 + hh
                    ab = self.ps[side + hh // 4]
                    xsl = slice(Tc * h + 128 * ti, Tc * h + 128 * (ti + 1))
                    v.scalar_tensor_tensor(xbuf[:, xsl],
                                           ab[:, 128 * (hh % 4):
                                              128 * (hh % 4 + 1)],
                                           self.gav[:, h:h + 1],
                                           xbuf[:, xsl], ALU.mult, ALU.add)
                ev.inc(v.engine_nop())
                v.drain()

            @blk.vector
            def _(v):
                for c, ct in enumerate(cts):
                    ti, cls, nk, off = (ct["ti"], ct["cls"], ct["nk"],
                                        ct["off"])
                    side = 4 * (c % 2)
                    nbk = nbks[c]
                    hpb = 8 // nbk
                    h0 = 8 * cls
                    msk = self.mask2x if nk == 256 else self.mask4x
                    if c >= 1:
                        tail_copies(v, c - 1)
                    if c >= 1:
                        v.wait_ge(ea.h, ea0 + 2 * (c - 1) + 1)
                    for j in range(nbk):
                        v.wait_ge(mm.h, mmct[c] + j + 1)
                        v.tensor_tensor(
                            smb[:, 512 * j:512 * j + nk * hpb],
                            self.ps[side + j][:, 0:nk * hpb],
                            msk[:, 0:nk * hpb], ALU.add)
                    v.drain()
                    for j in range(nbk):
                        sm3 = smb[:, 512 * j:512 * j + nk * hpb].rearrange(
                            "p (hh k) -> p hh k", hh=hpb)
                        v.tensor_reduce(nmx[:, j * hpb:(j + 1) * hpb], sm3,
                                        AXX, ALU.max, negate=True)
                    v.drain()
                    v.tensor_single_scalar(nmx[:, 8:16], nmx[:, 0:8],
                                           self.cISQ[:, 0:1], ALU.mult)
                    ev.inc(v.engine_nop())
                    v.drain()
                    if c >= 1:
                        tail_x1(v, c - 1)
                    v.wait_ge(ea.h, ea0 + 2 * c + 1)
                    e3 = ebf[:, 0:8 * nk].rearrange("p (hh k) -> p hh k",
                                                    hh=8)
                    v.tensor_reduce(nmx[:, 0:8], e3, AXX, ALU.add)
                    v.drain()
                    v.reciprocal(nmx[:, 16:24], nmx[:, 0:8])
                    v.drain()
                    last = None
                    for hh in range(8):
                        last = v.tensor_single_scalar(
                            ebf[:, nk * hh:nk * (hh + 1)],
                            ebf[:, nk * hh:nk * (hh + 1)],
                            nmx[:, 16 + hh:17 + hh], ALU.mult)
                    ea.inc(last)
                    v.drain()
                tail_copies(v, NCT - 1)
                tail_x1(v, NCT - 1)
                ev.inc(v.engine_nop())
                v.wait_ge(mm.h, mm_end + PC + 1)
                v.tensor_single_scalar(m_t, self.ps[2][:, 0:Tc], RC, ALU.mult)
                v.drain()
                v.tensor_tensor(var_t, m_t, m_t, ALU.mult)
                v.drain()
                v.scalar_tensor_tensor(var_t, self.ps[3][:, 0:Tc], RC, var_t,
                                       ALU.mult, ALU.subtract)
                v.drain()
                ev.inc(v.engine_nop())
                v.wait_ge(ea.h, ea0 + 2 * NCT + PC + 1)
                v.reciprocal(rs_t, sd_t)
                v.drain()
                v.wait_ge(dO.h, do0 + 16 * PC)
                self.ln_apply(v, xbuf, m_t, rs_t, self.cb1,
                              self.modv[:, 48:64], h2h, h2l, Tc)
                ev.inc(v.engine_nop())

    def sweep_ffn(self, half):
        nc = self.nc
        Tc = 512
        NCH = self.T // Tc
        NB = 6
        es = ExitStack()
        A = lambda n, sh, d: es.enter_context(nc.sbuf_tensor(n, sh, d))
        tg = f"f{half}"
        h2bh = A(f"h2bh{tg}", [128, PC * Tc], BF16)
        h2bl = A(f"h2bl{tg}", [128, PC * Tc], BF16)
        x2a = A(f"x2a{tg}", [128, PC * Tc], F32)
        gwb = [A(f"gwb{tg}{i}", [128, PC * 512], BF16) for i in range(2)]
        uwb = [A(f"uwb{tg}{i}", [128, PC * 512], BF16) for i in range(2)]
        dwb = [A(f"dwb{tg}{i}", [128, 4 * C], BF16) for i in range(1)]
        gsb = A(f"gsb{tg}", [128, 4 * 512], F32)
        fsh = A(f"fsh{tg}", [128, 4 * 512], BF16)
        fsl = A(f"fsl{tg}", [128, 4 * 512], BF16)
        fsf = A(f"fsf{tg}", [128, 4 * 512], F32)
        outst = A(f"outst{tg}", [128, C], F32) if half == 1 else None
        bufs = (h2bh, h2bl, x2a, gwb, uwb, dwb, gsb, fsh, fsl, fsf, outst)
        dP = self.ctr("dPf")
        dPw = [self.ctr("dPwA"), self.ctr("dPwB")]
        dPd = self.ctr("dPwD")
        dO = self.ctr("dOf")
        mm = self.ctr("mmf")
        ev = self.ctr("evf")
        ea = self.ctr("eaf")
        self._wtq = []
        self._dPw = dPw
        self._dPd = dPd
        xsrc = self.sx1 if half == 0 else self.sx1b
        with nc.Block() as blk:
            @blk.gpsimd
            def _(g):
                self._ffn_load_chunk(g, 0, Tc, h2bh, h2bl, x2a, dP, xsrc)
                self._ffn_load_gu(g, half, 0, gwb[0], uwb[0], dPw[0])
                self._wtq.append((0, dPw[0].n))
        self._pref = dP.n
        for ch in range(NCH):
            qu = self.quant_take(self.qB, 24) if half == 0 else []
            self._ffn_block(ch, NCH, Tc, NB, half, bufs, dP, dO, mm, ev, ea,
                            xsrc, qu)
        with nc.Block() as blk:
            @blk.sync
            def _(s):
                s.wait_ge(dO.h, dO.n)
        if half == 1:
            self._ffn_out = dO
        es.close()

    def _ffn_load_chunk(self, g, ch, Tc, h2bh, h2bl, x2a, dP, xsrc):
        sl = slice(Tc * ch, Tc * (ch + 1))
        for pc in range(PC):
            rows = slice(128 * pc, 128 * (pc + 1))
            dP.inc(g.dma_start(h2bh[:, Tc * pc:Tc * (pc + 1)],
                               self.sh2h[rows, sl]), 16)
            dP.inc(g.dma_start(h2bl[:, Tc * pc:Tc * (pc + 1)],
                               self.sh2l[rows, sl]), 16)
            dP.inc(g.dma_start(x2a[:, Tc * pc:Tc * (pc + 1)],
                               xsrc[rows, sl]), 16)

    def _ffn_load_gu(self, g, half, b, gwbuf, uwbuf, ctr):
        cols = slice(3072 * half + 512 * b, 3072 * half + 512 * (b + 1))
        for kc in range(PC):
            rows = slice(128 * kc, 128 * (kc + 1))
            ctr.inc(g.dma_start(gwbuf[:, 512 * kc:512 * (kc + 1)],
                                self.sgw[rows, cols]), 16)
            ctr.inc(g.dma_start(uwbuf[:, 512 * kc:512 * (kc + 1)],
                                self.suw[rows, cols]), 16)

    def _ffn_load_d(self, g, half, b, dwbuf, ctr):
        for icl in range(4):
            rows = slice(3072 * half + 512 * b + 128 * icl,
                         3072 * half + 512 * b + 128 * (icl + 1))
            ctr.inc(g.dma_start(dwbuf[:, C * icl:C * (icl + 1)],
                                self.sdw[rows, :]), 16)

    def _ffn_block(self, ch, NCH, Tc, NB, half, bufs, dP, dO, mm, ev, ea,
                   xsrc, qu):
        nc = self.nc
        (h2bh, h2bl, x2a, gwb, uwb, dwb, gsb, fsh, fsl, fsf, outst) = bufs
        t0 = Tc * ch
        NTR = Tc // 128
        sl = slice(t0, t0 + Tc)
        with nc.Block() as blk:
            mm0, ev0, ea0, do0 = mm.n, ev.n, ea.n, dO.n
            pref0 = self._pref
            dPw, dPd = self._dPw, self._dPd
            qhooks, qacts = (self.quant_inline_make(blk, self.qset_t,
                                                    list(qu))
                             if qu else ([], []))
            dtq = []

            @blk.gpsimd
            def _(g):
                for b in range(NB):
                    if b >= 1:
                        g.wait_ge(mm.h, mm0 + 24 * b)
                    self._ffn_load_d(g, half, b, dwb[0], dPd)
                    dtq.append(dPd.n)
                    nxt = b + 1
                    if nxt < NB or ch + 1 < NCH:
                        tgt = nxt if nxt < NB else 0
                        self._ffn_load_gu(g, half, tgt, gwb[nxt % 2],
                                          uwb[nxt % 2], dPw[nxt % 2])
                        self._wtq.append((nxt % 2, dPw[nxt % 2].n))
                if half == 0:
                    g.wait_ge(ev.h, ev0 + 20 * NB)
                    for pc in range(PC):
                        dO.inc(g.dma_start(
                            self.sx1b[128 * pc:128 * (pc + 1), sl],
                            x2a[:, Tc * pc:Tc * (pc + 1)]), 16)
                if ch + 1 < NCH:
                    if half == 0:
                        g.wait_ge(dO.h, do0 + 16 * PC)
                    else:
                        g.wait_ge(mm.h, mm0 + 24 * NB + NTR * PC)
                    self._ffn_load_chunk(g, ch + 1, Tc, h2bh, h2bl, x2a, dP,
                                         xsrc)
                    self._pref = dP.n

            @blk.tensor
            def _(t):
                t = Track(t)
                t.wait_ge(dP.h, pref0)
                for b in range(NB):
                    ws_, wv_ = self._wtq.pop(0)
                    t.wait_ge(dPw[ws_].h, wv_)
                    gslot, uslot = gwb[b % 2], uwb[b % 2]
                    for icl in range(4):
                        bank = self.ps[icl]
                        if b >= 1:
                            t.wait_ge(ev.h, ev0 + 20 * b)
                        gi = 0
                        for hs in (h2bh, h2bl):
                            for kc in range(PC):
                                t.matmul(bank[:, 0:512],
                                         gslot[:, 512 * kc + 128 * icl:
                                               512 * kc + 128 * (icl + 1)],
                                         hs[:, Tc * kc:Tc * (kc + 1)],
                                         start=(gi == 0),
                                         stop=(gi == 2 * PC - 1))
                                gi += 1
                        mm.inc(t.last)
                    for icl in range(4):
                        bank = self.ps[4 + icl]
                        if b >= 1:
                            t.wait_ge(ev.h, ev0 + 20 * (b - 1) + 4)
                        gi = 0
                        for hs in (h2bh, h2bl):
                            for kc in range(PC):
                                t.matmul(bank[:, 0:512],
                                         uslot[:, 512 * kc + 128 * icl:
                                               512 * kc + 128 * (icl + 1)],
                                         hs[:, Tc * kc:Tc * (kc + 1)],
                                         start=(gi == 0),
                                         stop=(gi == 2 * PC - 1))
                                gi += 1
                        mm.inc(t.last)
                    t.wait_ge(dPd.h, dtq[b])
                    for oc in range(PC):
                        bank = self.ps[oc % 4]
                        t.wait_ge(ev.h, ev0 + 20 * b + 4)
                        if oc >= 4:
                            t.wait_ge(ev.h, ev0 + 20 * b + 4 + oc - 3)
                        gi = 0
                        for fs in (fsh, fsl):
                            for icl in range(4):
                                t.matmul(bank[:, 0:512],
                                         dwb[0][:, C * icl + 128 * oc:
                                                C * icl + 128 * (oc + 1)],
                                         fs[:, 512 * icl:512 * (icl + 1)],
                                         start=(gi == 0), stop=(gi == 7))
                                gi += 1
                        mm.inc(t.last)
                if half == 1:
                    for r in range(NTR):
                        for pc in range(PC):
                            idx = r * PC + pc
                            bank = self.ps[4 + idx % 4]
                            if idx == 0:
                                t.wait_ge(ev.h, ev0 + 20 * NB)
                            if idx >= 4:
                                t.wait_ge(ev.h, ev0 + 20 * NB + idx - 3)
                            t.transpose(bank[:, 0:128],
                                        x2a[:, Tc * pc + 128 * r:
                                            Tc * pc + 128 * (r + 1)],
                                        self.ident[:, :])
                            mm.inc(t.last)

            @blk.scalar
            def _(s):
                for b in range(NB):
                    if 2 * b < len(qacts):
                        qacts[2 * b](s)
                    for icl in range(4):
                        s.wait_ge(mm.h, mm0 + 24 * b + icl + 1)
                        if b >= 1:
                            s.wait_ge(ev.h, ev0 + 20 * (b - 1) + 4)
                        ic = 24 * half + 4 * b + icl
                        s.activation(gsb[:, 512 * icl:512 * (icl + 1)],
                                     self.ps[icl][:, 0:512], AF.Silu,
                                     scale=self.gscv[:, ic:ic + 1]) \
                            .then_inc(ea.h, 1)
                        ea.n += 1
                    if 2 * b + 1 < len(qacts):
                        qacts[2 * b + 1](s)

            @blk.vector
            def _(v):
                for b in range(NB):
                    for icl in range(4):
                        ic = 24 * half + 4 * b + icl
                        v.wait_ge(mm.h, mm0 + 24 * b + 4 + icl + 1)
                        v.wait_ge(ea.h, ea0 + 4 * b + icl + 1)
                        v.scalar_tensor_tensor(
                            fsf[:, 512 * icl:512 * (icl + 1)],
                            self.ps[4 + icl][:, 0:512],
                            self.uscv[:, ic:ic + 1],
                            gsb[:, 512 * icl:512 * (icl + 1)],
                            ALU.mult, ALU.mult)
                    v.drain()
                    for icl in range(4):
                        v.tensor_copy(fsh[:, 512 * icl:512 * (icl + 1)],
                                      fsf[:, 512 * icl:512 * (icl + 1)])
                    v.drain()
                    for icl in range(4):
                        v.scalar_tensor_tensor(
                            fsl[:, 512 * icl:512 * (icl + 1)],
                            fsh[:, 512 * icl:512 * (icl + 1)], -1.0,
                            fsf[:, 512 * icl:512 * (icl + 1)],
                            ALU.mult, ALU.add)
                    v.drain()
                    ev.inc(v.engine_nop(), 4)
                    for oc in range(PC):
                        v.wait_ge(mm.h, mm0 + 24 * b + 8 + oc + 1)
                        v.scalar_tensor_tensor(
                            x2a[:, Tc * oc:Tc * (oc + 1)],
                            self.ps[oc % 4][:, 0:512],
                            self.gbds[:, oc:oc + 1],
                            x2a[:, Tc * oc:Tc * (oc + 1)],
                            ALU.mult, ALU.add)
                        v.drain()
                        ev.inc(v.engine_nop())
                    for k in (2 * b, 2 * b + 1):
                        if k < len(qhooks):
                            qhooks[k](v)
                if half == 1:
                    for r in range(NTR):
                        for pc in range(PC):
                            idx = r * PC + pc
                            bank = self.ps[4 + idx % 4]
                            v.wait_ge(mm.h, mm0 + 24 * NB + idx + 1)
                            if pc == 0:
                                v.wait_ge(dO.h, do0 + 16 * r)
                            v.tensor_copy(outst[:, 128 * pc:128 * (pc + 1)],
                                          bank[:, 0:128])
                            v.drain()
                            ev.inc(v.engine_nop())

            if half == 1:
                @blk.sync
                def _(s):
                    for r in range(NTR):
                        s.wait_ge(ev.h, ev0 + 20 * NB + (r + 1) * PC)
                        dO.inc(s.dma_start(
                            self.out[t0 + 128 * r:t0 + 128 * (r + 1), :],
                            outst[:, :]), 16)

    def finale(self):
        nc = self.nc
        dOf = self._ffn_out
        with nc.Block() as blk:
            @blk.sync
            def _(s):
                s.wait_ge(dOf.h, dOf.n)
                for e in "dp":
                    for sl_ in range(2):
                        s.wait_ge(self.qwl[e][sl_].h, self.qwl[e][sl_].n)


def _shard_inputs(inputs):
    maps = []
    for b in range(8):
        maps.append(dict(
            x_in=np.ascontiguousarray(np.asarray(inputs["L0_attn_x"])[b]),
            Wq=np.asarray(inputs["L0_attn_Wq"]),
            Wk=np.asarray(inputs["L0_attn_Wk"]),
            Wv=np.asarray(inputs["L0_attn_Wv"]),
            bqd=np.asarray(inputs["L0_attn_bq"]),
            bkd=np.asarray(inputs["L0_attn_bk"]),
            bvd=np.asarray(inputs["L0_attn_bv"]),
            gwd=np.asarray(inputs["L0_ffn_gate_w"]),
            uwd=np.asarray(inputs["L0_ffn_up_w"]),
            dwd=np.asarray(inputs["L0_ffn_down_w"]),
            gscd=np.asarray(inputs["L0_ffn_gate_scale"]),
            uscd=np.asarray(inputs["L0_ffn_up_scale"]),
            dscd=np.asarray(inputs["L0_ffn_down_scale"]),
            cvec=np.ascontiguousarray(np.asarray(inputs["L0_c"])[b]),
            wmod=np.asarray(inputs["L0_wmod"]),
        ))
    return maps


def kernel(**inputs):
    T = np.asarray(inputs["L0_attn_x"]).shape[1]
    b = Build(T=T)
    nc = b.build()
    in_maps = _shard_inputs(inputs)
    res = run_bass_kernel_spmd(nc, in_maps, core_ids=list(range(8)))
    out = np.stack([np.asarray(r["out"]) for r in res.results], axis=0)
    return out.astype(np.float32)


# version 11
# speedup vs baseline: 1.0681x; 1.0058x over previous
from contextlib import ExitStack

import numpy as np
import concourse.bass as bass
import concourse.mybir as mybir
from concourse.bass_utils import run_bass_kernel_spmd

F32 = mybir.dt.float32
BF16 = mybir.dt.bfloat16
ALU = mybir.AluOpType
AF = mybir.ActivationFunctionType
AXX = mybir.AxisListType.X

C = 2048
PC = C // 128
FI = 6144
NEG = -1.0e9
EPS = 1e-5
ISQ = float(1.0 / np.sqrt(np.float32(128.0)))
RC = 1.0 / 2048.0
QF = 256
QOPS = 31


class Ctr:
    def __init__(self, nc, name):
        self.h = nc.alloc_semaphore(name)
        self.n = 0

    def inc(self, instr, k=1):
        instr.then_inc(self.h, k)
        self.n += k
        return self.n


class Track:

    def __init__(self, eng):
        self._eng = eng
        self.last = None

    def __getattr__(self, name):
        f = getattr(self._eng, name)

        def wrap(*a, **k):
            r = f(*a, **k)
            self.last = r
            return r
        return wrap


class Build:
    def __init__(self, T=4096):
        assert T % 512 == 0
        self.T = T
        nc = self.nc = bass.Bass("TRN2", target_bir_lowering=False,
                                 debug=False)
        dt = nc.dram_tensor
        self.x_in = dt("x_in", [T, C], F32, kind="ExternalInput")
        self.Wq = dt("Wq", [C, C], F32, kind="ExternalInput")
        self.Wk = dt("Wk", [C, C], F32, kind="ExternalInput")
        self.Wv = dt("Wv", [C, C], F32, kind="ExternalInput")
        self.bqd = dt("bqd", [C], F32, kind="ExternalInput")
        self.bkd = dt("bkd", [C], F32, kind="ExternalInput")
        self.bvd = dt("bvd", [C], F32, kind="ExternalInput")
        self.gwd = dt("gwd", [C, FI], F32, kind="ExternalInput")
        self.uwd = dt("uwd", [C, FI], F32, kind="ExternalInput")
        self.dwd = dt("dwd", [FI, C], F32, kind="ExternalInput")
        self.gscd = dt("gscd", [FI], F32, kind="ExternalInput")
        self.uscd = dt("uscd", [FI], F32, kind="ExternalInput")
        self.dscd = dt("dscd", [C], F32, kind="ExternalInput")
        self.cvec = dt("cvec", [C], F32, kind="ExternalInput")
        self.wmod = dt("wmod", [C, 6 * C], F32, kind="ExternalInput")
        self.out = dt("out", [T, C], F32, kind="ExternalOutput")

        self.sh1f = dt("sh1f", [C, T], F32)
        self.sqf = dt("sqf", [C, T], F32)
        self.skf = dt("skf", [C, T], F32)
        self.svf = dt("svf", [T, C], F32)
        self.sxT = dt("sxT", [C, T], F32)
        self.sx1 = dt("sx1", [C, T], F32)
        self.sh2h = dt("sh2h", [C, T], BF16)
        self.sh2l = dt("sh2l", [C, T], BF16)
        self.sx1b = dt("sx1b", [C, T], F32)
        self.sgw = dt("sgw", [C, FI], BF16)
        self.suw = dt("suw", [C, FI], BF16)
        self.sdw = dt("sdw", [FI, C], BF16)

        A = nc.alloc_sbuf_tensor
        self.ident = A("ident", [128, 128], F32)
        self.iotat = A("iotat", [128, 256], F32)
        self.mask0 = A("mask0", [128, 128], F32)
        self.mask128 = A("mask128", [128, 256], F32)
        self.mask2x = A("mask2x", [128, 512], F32)
        self.mask4x = A("mask4x", [128, 512], F32)
        self.modv = A("modv", [128, 96], F32)
        self.scv = A("scv", [128, 16], F32)
        self.ca1 = A("ca1", [128, 16], F32)
        self.cb1 = A("cb1", [128, 16], F32)
        self.gav = A("gav", [128, 16], F32)
        self.gbds = A("gbds", [128, 16], F32)
        self.bqv = A("bqv", [128, 16], F32)
        self.bkv = A("bkv", [128, 16], F32)
        self.dscv = A("dscv", [128, 16], F32)
        self.gscv = A("gscv", [128, 48], F32)
        self.uscv = A("uscv", [128, 48], F32)
        self.onesf = A("onesf", [128, 128], F32)
        self.cEPS = A("cEPS", [128, 1], F32)
        self.cISQ = A("cISQ", [128, 1], F32)
        self.tmpv = A("tmpv", [128, 32], F32)
        self.ps = [nc.alloc_psum_tensor(f"ps{i}", [128, 512], F32)
                   for i in range(8)]

        self.pq = Ctr(nc, "pq")
        self.evQ = Ctr(nc, "evQ")
        self.qdl = {e: [Ctr(nc, f"qdl{e}{i}") for i in range(2)] for e in "dp"}
        self.qwl = {e: [Ctr(nc, f"qwl{e}{i}") for i in range(2)] for e in "dp"}
        self.qA, self.qB = [], []
        for (ws, wd) in ((self.gwd, self.sgw), (self.uwd, self.suw)):
            for r in range(16):
                for c in range(6):
                    self.qA.append((ws, wd, 128 * r, 512 * c))
                for c in range(6, 12):
                    self.qB.append((ws, wd, 128 * r, 512 * c))
        for r in range(48):
            lst = self.qA if r < 24 else self.qB
            for c in range(4):
                lst.append((self.dwd, self.sdw, 128 * r, 512 * c))
        self.qA += self.qB[:96]
        self.qB = self.qB[96:]
        self.qpos_i = {id(self.qA): 0, id(self.qB): 0}
        self._sid = 0

    def ctr(self, name):
        self._sid += 1
        return Ctr(self.nc, f"{name}{self._sid}")

    def build(self, stages=99):
        self.prologue()
        if stages < 1:
            return self.nc
        self.mod_matvec()
        if stages < 2:
            return self.nc
        self.derived_vectors()
        if stages < 3:
            return self.nc
        with ExitStack() as qes:
            sb = lambda n, sh, d: qes.enter_context(self.nc.sbuf_tensor(n, sh, d))
            self.qset_s = self.make_qset(sb, "s", 512)
            self.sweep_h1()
            if stages >= 4:
                self.sweep_proj("q")
            if stages >= 5:
                self.sweep_proj("k")
            if stages >= 6:
                self.sweep_proj("v")
            if stages >= 7:
                while self.qpos_i[id(self.qA)] < len(self.qA):
                    self.quant_pairs_block(self.qset_s,
                                           self.quant_take(self.qA, 8))
            with self.nc.Block() as blk:
                @blk.sync
                def _(s):
                    for sl_ in range(2):
                        s.wait_ge(self.qset_s["wl"][sl_].h,
                                  self.qset_s["wl"][sl_].n)
        if stages >= 8:
            self.sweep_attn()
        if stages >= 9:
            with ExitStack() as qes2:
                sb2 = lambda n, sh, d: qes2.enter_context(
                    self.nc.sbuf_tensor(n, sh, d))
                self.qset_t = self.make_qset(sb2, "t", 512)
                self.sweep_ffn(0)
                while self.qpos_i[id(self.qB)] < len(self.qB):
                    self.quant_pairs_block(self.qset_t,
                                           self.quant_take(self.qB, 8))
                with self.nc.Block() as blk:
                    @blk.sync
                    def _(s):
                        for sl_ in range(2):
                            s.wait_ge(self.qset_t["wl"][sl_].h,
                                      self.qset_t["wl"][sl_].n)
            self.sweep_ffn(1)
            self.finale()
            assert (self.qpos_i[id(self.qA)] >= len(self.qA)
                    and self.qpos_i[id(self.qB)] >= len(self.qB)), \
                "quantization incomplete"
        return self.nc

    def make_qset(self, alloc, tag, FD):
        qs = dict(
            FD=FD,
            raw=[alloc(f"qr{tag}{i}", [128, FD], F32) for i in range(2)],
            s1=[alloc(f"qa{tag}{i}", [128, FD], F32) for i in range(2)],
            s3=[alloc(f"qw{tag}{i}", [128, FD], F32) for i in range(2)],
            s4=[alloc(f"qc{tag}{i}", [128, FD], F32) for i in range(2)],
            b=[[alloc(f"qb{tag}{i}_{j}", [128, FD], BF16) for j in range(5)]
               for i in range(2)],
            out=[alloc(f"qo{tag}{i}", [128, FD], BF16) for i in range(2)],
            dl=[self.ctr(f"qdl{tag}"), self.ctr(f"qdl{tag}")],
            wl=[self.ctr(f"qwl{tag}"), self.ctr(f"qwl{tag}")],
            ev=self.ctr(f"qev{tag}"),
            ea=self.ctr(f"qea{tag}"),
        )
        return qs

    def _quant_act_pair(self, s, qs, pair, dlt, evt, wlt):
        s.wait_ge(qs["ev"].h, evt)
        for sl_ in range(len(pair)):
            s.wait_ge(qs["wl"][sl_].h, wlt[sl_])
            s.wait_ge(qs["dl"][sl_].h, dlt[sl_])
        last = None
        for sl_ in range(len(pair)):
            s.activation(qs["out"][sl_][:], qs["raw"][sl_][:], AF.Sign)
            last = s.activation(qs["s1"][sl_][:], qs["raw"][sl_][:], AF.Abs)
        qs["ea"].inc(last)

    def _quant_dve_pair(self, v, qs, pair, eat):
        TH = [2.0 ** -2, 2.0 ** -3, 2.0 ** -4, 2.0 ** -5]
        P = range(len(pair))
        v.wait_ge(qs["ea"].h, eat)
        raw, s1, s3, s4 = qs["raw"], qs["s1"], qs["s3"], qs["s4"]
        b, out = qs["b"], qs["out"]
        for i in P:
            for j in range(4):
                v.tensor_scalar(b[i][j][:], s1[i][:], TH[j], 1.0,
                                ALU.is_lt, ALU.add)
        v.drain()
        for i in P:
            v.tensor_tensor(b[i][0][:], b[i][0][:], b[i][1][:], ALU.mult)
            v.tensor_tensor(b[i][2][:], b[i][2][:], b[i][3][:], ALU.mult)
        v.drain()
        for i in P:
            v.tensor_tensor(b[i][1][:], b[i][0][:], b[i][2][:], ALU.mult)
        v.drain()
        for i in P:
            v.scalar_tensor_tensor(raw[i][:], s1[i][:], 16.0, b[i][1][:],
                                   ALU.mult, ALU.mult)
            v.tensor_scalar(b[i][0][:], s1[i][:], TH[0], 1.0, ALU.is_ge,
                            ALU.add)
            v.tensor_scalar(b[i][2][:], s1[i][:], TH[1], 1.0, ALU.is_ge,
                            ALU.add)
            v.tensor_scalar(b[i][3][:], s1[i][:], TH[2], 1.0, ALU.is_ge,
                            ALU.add)
            v.tensor_scalar(b[i][4][:], s1[i][:], TH[3], 1.0, ALU.is_ge,
                            ALU.add)
        v.drain()
        for i in P:
            v.tensor_scalar(s4[i][:], raw[i][:], 4.0, 4.0, ALU.is_ge,
                            ALU.mult)
            v.tensor_tensor(b[i][0][:], b[i][0][:], b[i][2][:], ALU.mult)
            v.tensor_tensor(b[i][3][:], b[i][3][:], b[i][4][:], ALU.mult)
        v.drain()
        for i in P:
            v.tensor_tensor(s3[i][:], raw[i][:], s4[i][:], ALU.subtract)
            v.tensor_tensor(b[i][0][:], b[i][0][:], b[i][3][:], ALU.mult)
        v.drain()
        for i in P:
            v.tensor_scalar(s4[i][:], s3[i][:], 2.0, 2.0, ALU.is_ge, ALU.mult)
            v.tensor_single_scalar(b[i][0][:], b[i][0][:], 2.0 ** -9,
                                   ALU.mult)
        v.drain()
        for i in P:
            v.tensor_tensor(s3[i][:], s3[i][:], s4[i][:], ALU.subtract)
            v.tensor_tensor(b[i][0][:], b[i][0][:], out[i][:], ALU.mult)
        v.drain()
        for i in P:
            v.tensor_single_scalar(s4[i][:], s3[i][:], 1.0, ALU.is_ge)
        v.drain()
        for i in P:
            v.tensor_tensor(s3[i][:], s3[i][:], s4[i][:], ALU.subtract)
        v.drain()
        for i in P:
            v.tensor_single_scalar(s4[i][:], s3[i][:], 0.75, ALU.is_ge)
        v.drain()
        for i in P:
            v.scalar_tensor_tensor(s4[i][:], s3[i][:], 0.25, s4[i][:],
                                   ALU.is_gt, ALU.add)
        v.drain()
        for i in P:
            v.tensor_tensor(s3[i][:], raw[i][:], s3[i][:], ALU.subtract)
        v.drain()
        for i in P:
            v.scalar_tensor_tensor(s3[i][:], s3[i][:], 2.0, s4[i][:],
                                   ALU.mult, ALU.add)
        v.drain()
        last = None
        for i in P:
            last = v.tensor_tensor(out[i][:], s3[i][:], b[i][0][:], ALU.mult)
        qs["ev"].inc(last)
        v.drain()

    def quant_take(self, lst, n):
        take = lst[self.qpos_i[id(lst)]:self.qpos_i[id(lst)] + n]
        self.qpos_i[id(lst)] += len(take)
        return take

    def quant_pairs_block(self, qset, units):
        nc = self.nc
        if not units:
            return
        FD = qset["FD"]
        pairs = [units[i:i + 2] for i in range(0, len(units), 2)]
        with nc.Block() as blk:
            ev0 = qset["ev"].n
            ea0 = qset["ea"].n
            wl0 = [qset["wl"][0].n, qset["wl"][1].n]
            dlt = []

            @blk.sync
            def _(s):
                for pi, pair in enumerate(pairs):
                    t = []
                    if pi >= 1:
                        s.wait_ge(qset["ev"].h, ev0 + pi)
                    for sl_, (wsrc, wdst, r0, c0) in enumerate(pair):
                        if pi >= 1:
                            s.wait_ge(qset["wl"][sl_].h, qset["wl"][sl_].n)
                        qset["dl"][sl_].inc(
                            s.dma_start(qset["raw"][sl_][:, :],
                                        wsrc[r0:r0 + 128, c0:c0 + FD]), 16)
                        t.append(qset["dl"][sl_].n)
                    dlt.append(t)
                    if pi >= 1:
                        for sl_, (wsrc, wdst, r0, c0) in enumerate(pairs[pi - 1]):
                            qset["wl"][sl_].inc(
                                s.dma_start(wdst[r0:r0 + 128, c0:c0 + FD],
                                            qset["out"][sl_][:, :]), 16)
                s.wait_ge(qset["ev"].h, ev0 + len(pairs))
                for sl_, (wsrc, wdst, r0, c0) in enumerate(pairs[-1]):
                    qset["wl"][sl_].inc(
                        s.dma_start(wdst[r0:r0 + 128, c0:c0 + FD],
                                    qset["out"][sl_][:, :]), 16)
                for sl_ in range(2):
                    s.wait_ge(qset["wl"][sl_].h, qset["wl"][sl_].n)

            @blk.scalar
            def _(s):
                for pi, pair in enumerate(pairs):
                    self._quant_act_pair(s, qset, pair, dlt[pi], ev0 + pi,
                                         [wl0[0] + 16 * pi, wl0[1] + 16 * pi])

            @blk.vector
            def _(v):
                for pi, pair in enumerate(pairs):
                    self._quant_dve_pair(v, qset, pair, ea0 + pi + 1)

    def quant_inline_make(self, blk, qset, units):
        if not units:
            return []
        FD = qset["FD"]
        pairs = [units[i:i + 2] for i in range(0, len(units), 2)]
        ev0 = qset["ev"].n
        ea0 = qset["ea"].n
        wl0 = [qset["wl"][0].n, qset["wl"][1].n]
        dlt = []

        @blk.sync
        def _(s):
            for pi, pair in enumerate(pairs):
                t = []
                s.wait_ge(qset["ev"].h, ev0 + pi)
                for sl_, (wsrc, wdst, r0, c0) in enumerate(pair):
                    s.wait_ge(qset["wl"][sl_].h, qset["wl"][sl_].n)
                    qset["dl"][sl_].inc(
                        s.dma_start(qset["raw"][sl_][:, :],
                                    wsrc[r0:r0 + 128, c0:c0 + FD]), 16)
                    t.append(qset["dl"][sl_].n)
                dlt.append(t)
                s.wait_ge(qset["ev"].h, ev0 + pi + 1)
                for sl_, (wsrc, wdst, r0, c0) in enumerate(pair):
                    qset["wl"][sl_].inc(
                        s.dma_start(wdst[r0:r0 + 128, c0:c0 + FD],
                                    qset["out"][sl_][:, :]), 16)

        acts = []
        hooks = []
        for pi, pair in enumerate(pairs):
            def act(s, pi=pi, pair=pair):
                self._quant_act_pair(s, qset, pair, dlt[pi], ev0 + pi,
                                     [wl0[0] + 16 * pi, wl0[1] + 16 * pi])

            def hook(v, pi=pi, pair=pair):
                self._quant_dve_pair(v, qset, pair, ea0 + pi + 1)
            acts.append(act)
            hooks.append(hook)
        return hooks, acts

    def prologue(self):
        nc = self.nc
        dS = self.ctr("dS")
        with nc.Block() as blk:
            @blk.sync
            def _(s):
                for dst, src, w in [(self.scv, self.cvec, 16),
                                    (self.bqv, self.bqd, 16),
                                    (self.bkv, self.bkd, 16),
                                    (self.dscv, self.dscd, 16),
                                    (self.gscv, self.gscd, 48),
                                    (self.uscv, self.uscd, 48)]:
                    dS.inc(s.dma_start(
                        dst[:, 0:w], src[:].rearrange("(j p) -> p j", p=128),
                        allow_slow_non_contiguous=True), 16)
                s.wait_ge(dS.h, dS.n)

            @blk.gpsimd
            def _(g):
                g.iota(self.iotat[:, :], [[1, 256]], channel_multiplier=-1,
                       allow_small_or_imprecise_dtypes=True)

        with nc.Block() as blk:
            @blk.vector
            def _(v):
                v.tensor_single_scalar(self.ident[:, :], self.iotat[:, 0:128],
                                       0.0, ALU.is_equal)
                v.tensor_scalar(self.mask0[:, :], self.iotat[:, 0:128],
                                0.0, NEG, ALU.is_gt, ALU.mult)
                v.tensor_scalar(self.mask128[:, :], self.iotat[:, :],
                                128.0, NEG, ALU.is_gt, ALU.mult)
                v.drain()
                for j in range(2):
                    v.tensor_copy(self.mask2x[:, 256 * j:256 * (j + 1)],
                                  self.mask128[:, :])
                for j in range(4):
                    v.tensor_copy(self.mask4x[:, 128 * j:128 * (j + 1)],
                                  self.mask0[:, :])
                v.drain()
                v.memset(self.onesf[:, :], 1.0)
                v.memset(self.cEPS[:, :], EPS)
                v.memset(self.cISQ[:, :], 128.0)

            @blk.scalar
            def _(s):
                s.activation(self.scv[:, :], self.scv[:, :], AF.Silu)
        with nc.Block() as blk:
            @blk.scalar
            def _(s):
                s.activation(self.cISQ[:, :], self.cISQ[:, :], AF.Sqrt)
        with nc.Block() as blk:
            @blk.vector
            def _(v):
                v.reciprocal(self.cISQ[:, :], self.cISQ[:, :])
                v.drain()

    def mod_matvec(self):
        nc = self.nc
        nb = 6 * C // 128
        mm = self.ctr("mm")
        ev = self.ctr("ev")
        dP = self.ctr("dP")
        dPs = [self.ctr("dPmA"), self.ctr("dPmB")]
        mes = ExitStack()
        with nc.Block() as blk:
            wmc = [mes.enter_context(nc.sbuf_tensor(f"wmc{i}", [128, C], F32))
                   for i in range(2)]

            @blk.gpsimd
            def _(g):
                for ob in range(nb):
                    if ob >= 2:
                        g.wait_ge(mm.h, ob - 1)
                    src = self.wmod[:, 128 * ob:128 * (ob + 1)]
                    dPs[ob % 2].inc(g.dma_start(
                        wmc[ob % 2][:].rearrange("p (kc o) -> p kc o", kc=PC),
                        src.rearrange("(kc p) o -> p kc o", p=128)), 16)

            @blk.tensor
            def _(t):
                t = Track(t)
                for ob in range(nb):
                    t.wait_ge(dPs[ob % 2].h, 16 * (ob // 2 + 1))
                    for kc in range(PC):
                        t.matmul(self.ps[6][:, ob:ob + 1],
                                 wmc[ob % 2][:, 128 * kc:128 * (kc + 1)],
                                 self.scv[:, kc:kc + 1],
                                 start=(kc == 0), stop=(kc == PC - 1))
                    mm.inc(t.last)

            @blk.vector
            def _(v):
                v.wait_ge(mm.h, nb)
                v.tensor_copy(self.modv[:, :], self.ps[6][:, 0:96])
        mes.close()

    def derived_vectors(self):
        nc = self.nc
        ca = self.modv[:, 16:32]
        ga = self.modv[:, 32:48]
        cb = self.modv[:, 64:80]
        gb = self.modv[:, 80:96]
        with nc.Block() as blk:
            @blk.scalar
            def _(s):
                s.activation(self.tmpv[:, 0:16], ga, AF.Abs)
                s.activation(self.tmpv[:, 16:32], gb, AF.Abs)

            @blk.vector
            def _(v):
                v.tensor_single_scalar(self.ca1[:, :], ca, 1.0, ALU.add)
                v.tensor_single_scalar(self.cb1[:, :], cb, 1.0, ALU.add)
                v.tensor_scalar(self.gav[:, :], ga, 0.0, 2.0,
                                ALU.is_ge, ALU.mult)
                v.tensor_scalar(self.gbds[:, :], gb, 0.0, 2.0,
                                ALU.is_ge, ALU.mult)
                v.drain()
                v.tensor_single_scalar(self.gav[:, :], self.gav[:, :],
                                       1.0, ALU.subtract)
                v.tensor_single_scalar(self.gbds[:, :], self.gbds[:, :],
                                       1.0, ALU.subtract)
        with nc.Block() as blk:
            @blk.vector
            def _(v):
                v.tensor_single_scalar(self.tmpv[:, 0:16], self.tmpv[:, 0:16],
                                       0.25, ALU.max)
                v.tensor_single_scalar(self.tmpv[:, 16:32],
                                       self.tmpv[:, 16:32], 0.25, ALU.max)
                v.drain()
                v.tensor_tensor(self.gav[:, :], self.gav[:, :],
                                self.tmpv[:, 0:16], ALU.mult)
                v.tensor_tensor(self.gbds[:, :], self.gbds[:, :],
                                self.tmpv[:, 16:32], ALU.mult)
                v.drain()
                v.tensor_tensor(self.gbds[:, :], self.gbds[:, :],
                                self.dscv[:, :], ALU.mult)

    def sweep_h1(self):
        nc = self.nc
        Tn = 256
        NH = self.T // Tn
        es = ExitStack()
        A = lambda n, sh, d: es.enter_context(nc.sbuf_tensor(n, sh, d))
        xst = [A(f"s1x{i}", [128, C], F32) for i in range(4)]
        xT = A("s1xT", [128, PC * Tn], F32)
        sq2 = A("s1sq", [128, 2 * Tn], F32)
        stat = A("s1st", [128, 4 * Tn], F32)
        h1f = A("s1h1f", [128, PC * Tn], F32)
        dP = self.ctr("dP1")
        dO = self.ctr("dO1")
        mm = self.ctr("mm1")
        ev = self.ctr("ev1")
        ea = self.ctr("ea1")
        with nc.Block() as blk:
            @blk.gpsimd
            def _(g):
                for r in range(2):
                    dP.inc(g.dma_start(
                        xst[r][:, :], self.x_in[128 * r:128 * (r + 1), :]), 16)
        self._pref = dP.n
        for hc in range(NH):
            self._s1_block(hc, NH, Tn, xst, xT, sq2, stat, h1f,
                           dP, dO, mm, ev, ea)
        with nc.Block() as blk:
            @blk.sync
            def _(s):
                s.wait_ge(dO.h, dO.n)
        es.close()

    def _s1_block(self, hc, NH, Tn, xst, xT, sq2, stat, h1f,
                  dP, dO, mm, ev, ea):
        nc = self.nc
        t0 = Tn * hc
        sl = slice(t0, t0 + Tn)
        m_t = stat[:, 0:Tn]
        rs_t = stat[:, Tn:2 * Tn]
        var_t = stat[:, 2 * Tn:3 * Tn]
        sd_t = stat[:, 3 * Tn:4 * Tn]
        with nc.Block() as blk:
            mm0, ev0, ea0, do0 = mm.n, ev.n, ea.n, dO.n
            pref = self._pref
            ngrp = 8

            @blk.gpsimd
            def _(g):
                g.wait_ge(ev.h, ev0 + ngrp)
                if hc + 1 < NH:
                    for r in range(2):
                        dP.inc(g.dma_start(
                            xst[((hc + 1) % 2) * 2 + r][:, :],
                            self.x_in[Tn * (hc + 1) + 128 * r:
                                      Tn * (hc + 1) + 128 * (r + 1), :]), 16)
                    self._pref = dP.n
                for pc in range(PC):
                    dO.inc(g.dma_start(
                        self.sxT[128 * pc:128 * (pc + 1), sl],
                        xT[:, Tn * pc:Tn * (pc + 1)]), 16)
                g.wait_ge(ev.h, ev0 + 10)
                for pc in range(PC):
                    dO.inc(g.dma_start(
                        self.sh1f[128 * pc:128 * (pc + 1), sl],
                        h1f[:, Tn * pc:Tn * (pc + 1)]), 16)

            @blk.tensor
            def _(t):
                t = Track(t)
                t.wait_ge(dP.h, pref)
                for grp in range(ngrp):
                    bank = self.ps[grp % 2]
                    if grp >= 2:
                        t.wait_ge(ev.h, ev0 + grp - 1)
                    for i in range(4):
                        idx = grp * 4 + i
                        r, pc = idx // PC, idx % PC
                        t.transpose(bank[:, 128 * i:128 * (i + 1)],
                                    xst[(hc % 2) * 2 + r][:, 128 * pc:
                                                          128 * (pc + 1)],
                                    self.ident[:, :])
                    mm.inc(t.last)
                for pc in range(PC):
                    t.wait_ge(ea.h, ea0 + pc + 1)
                    t.matmul(self.ps[3][:, 0:Tn], self.onesf[:, :],
                             sq2[:, Tn * (pc % 2):Tn * (pc % 2 + 1)],
                             start=(pc == 0), stop=(pc == PC - 1))
                    mm.inc(t.last)
                t.wait_ge(ev.h, ev0 + ngrp)
                for pc in range(PC):
                    t.matmul(self.ps[2][:, 0:Tn], self.onesf[:, :],
                             xT[:, Tn * pc:Tn * (pc + 1)],
                             start=(pc == 0), stop=(pc == PC - 1))
                mm.inc(t.last)

            @blk.scalar
            def _(s):
                s.wait_ge(ev.h, ev0 + ngrp)
                for pc in range(PC):
                    if pc >= 2:
                        s.wait_ge(mm.h, mm0 + ngrp + pc - 1)
                    ea.inc(s.activation(
                        sq2[:, Tn * (pc % 2):Tn * (pc % 2 + 1)],
                        xT[:, Tn * pc:Tn * (pc + 1)], AF.Square))
                s.wait_ge(ev.h, ev0 + 9)
                ea.inc(s.activation(sd_t, var_t, AF.Sqrt,
                                    bias=self.cEPS[:, 0:1], scale=1.0))

            @blk.vector
            def _(v):
                for grp in range(ngrp):
                    bank = self.ps[grp % 2]
                    v.wait_ge(mm.h, mm0 + grp + 1)
                    for i in range(4):
                        idx = grp * 4 + i
                        r, pc = idx // PC, idx % PC
                        v.tensor_copy(xT[:, Tn * pc + 128 * r:
                                         Tn * pc + 128 * (r + 1)],
                                      bank[:, 128 * i:128 * (i + 1)])
                    v.drain()
                    ev.inc(v.engine_nop())
                v.wait_ge(mm.h, mm0 + ngrp + PC + 1)
                v.tensor_single_scalar(m_t, self.ps[2][:, 0:Tn], RC, ALU.mult)
                v.drain()
                v.tensor_tensor(var_t, m_t, m_t, ALU.mult)
                v.drain()
                v.scalar_tensor_tensor(var_t, self.ps[3][:, 0:Tn], RC, var_t,
                                       ALU.mult, ALU.subtract)
                v.drain()
                ev.inc(v.engine_nop())
                v.wait_ge(ea.h, ea0 + PC + 1)
                v.reciprocal(rs_t, sd_t)
                v.drain()
                v.wait_ge(dO.h, do0 + 16 * PC)
                self.ln_apply(v, xT, m_t, rs_t, self.ca1, self.modv[:, 0:16],
                              h1f, None, Tn)
                ev.inc(v.engine_nop())

    def ln_apply(self, v, x3, m, rs, scale_pp, bias_pp, outh, outl, Tn):
        for pc in range(PC):
            s = slice(Tn * pc, Tn * (pc + 1))
            v.tensor_tensor(x3[:, s], x3[:, s], m[:, 0:Tn], ALU.subtract)
        v.drain()
        for pc in range(PC):
            s = slice(Tn * pc, Tn * (pc + 1))
            v.tensor_tensor(x3[:, s], x3[:, s], rs[:, 0:Tn], ALU.mult)
        v.drain()
        if outl is None:
            for pc in range(PC):
                s = slice(Tn * pc, Tn * (pc + 1))
                v.tensor_scalar(outh[:, s], x3[:, s], scale_pp[:, pc:pc + 1],
                                bias_pp[:, pc:pc + 1], ALU.mult, ALU.add)
            v.drain()
            return
        for pc in range(PC):
            s = slice(Tn * pc, Tn * (pc + 1))
            v.tensor_scalar(x3[:, s], x3[:, s], scale_pp[:, pc:pc + 1],
                            bias_pp[:, pc:pc + 1], ALU.mult, ALU.add)
        v.drain()
        for pc in range(PC):
            s = slice(Tn * pc, Tn * (pc + 1))
            v.tensor_copy(outh[:, s], x3[:, s])
        v.drain()
        for pc in range(PC):
            s = slice(Tn * pc, Tn * (pc + 1))
            v.scalar_tensor_tensor(outl[:, s], outh[:, s], -1.0, x3[:, s],
                                   ALU.mult, ALU.add)
        v.drain()

    def weight_transform(self, W, wt, dP, mm, ev):
        nc = self.nc
        wes = ExitStack()
        dWs = [self.ctr("dWA")]
        with nc.Block() as blk:
            wraw = [wes.enter_context(nc.sbuf_tensor(f"wr_{W.name}0",
                                                     [128, C], F32))]
            mm0, ev0, dp0 = mm.n, ev.n, dP.n

            @blk.gpsimd
            def _(g):
                for ob in range(PC):
                    if ob >= 1:
                        g.wait_ge(mm.h, mm0 + ob * 4)
                    dWs[0].inc(g.dma_start(
                        wraw[0][:, :], W[128 * ob:128 * (ob + 1), :]), 16)

            @blk.tensor
            def _(t):
                t = Track(t)
                for ob in range(PC):
                    t.wait_ge(dWs[0].h, 16 * (ob + 1))
                    for jg in range(4):
                        gi = ob * 4 + jg
                        bank = self.ps[4 + gi % 2]
                        if gi >= 2:
                            t.wait_ge(ev.h, ev0 + gi - 1)
                        for kk in range(4):
                            kc = 4 * jg + kk
                            t.transpose(bank[:, 128 * kk:128 * (kk + 1)],
                                        wraw[0][:, 128 * kc:128 * (kc + 1)],
                                        self.ident[:, :])
                        mm.inc(t.last)

            @blk.vector
            def _(v):
                h3 = wt[:].rearrange("p (kc o) -> p kc o", kc=PC)
                for ob in range(PC):
                    for jg in range(4):
                        gi = ob * 4 + jg
                        bank = self.ps[4 + gi % 2]
                        v.wait_ge(mm.h, mm0 + gi + 1)
                        dh = h3[:, 4 * jg:4 * jg + 4, 128 * ob:128 * (ob + 1)]
                        src = bank[:].rearrange("p (k o) -> p k o", k=4)
                        ev.inc(v.tensor_copy(dh, src))
                        v.drain()
        wes.close()

    def sweep_proj(self, which):
        nc = self.nc
        Tn = 256
        NH = self.T // Tn
        es = ExitStack()
        A = lambda n, sh, d: es.enter_context(nc.sbuf_tensor(n, sh, d))
        W = {"q": self.Wq, "k": self.Wk, "v": self.Wv}[which]
        wt = A(f"wt_{which}", [128, PC * C], F32)
        h1f = A(f"h1f_{which}", [128, PC * Tn], F32)
        NS = 2 if which == "v" else 4
        pf = A(f"pf_{which}", [128, NS * 512], F32)
        dP = self.ctr("dPp")
        dO = self.ctr("dOp")
        dOs = [self.ctr("dOs") for _ in range(4)]
        mm = self.ctr("mmp")
        ev = self.ctr("evp")
        ea = self.ctr("eap")
        self.weight_transform(W, wt, dP, mm, ev)
        if which == "v":
            self.bvbc = A("bvbc", [128, C], F32)
            bvrow = A("bvrow", [1, C], F32)
            dBv = self.ctr("dBv")
            mmBv = self.ctr("mmBv")
            evBv = self.ctr("evBv")
            with nc.Block() as blk:
                @blk.sync
                def _(s):
                    dBv.inc(s.dma_start(
                        bvrow[:, :],
                        self.bvd[:].rearrange("(o c) -> o c", o=1)), 16)

                @blk.tensor
                def _(t):
                    t = Track(t)
                    t.wait_ge(dBv.h, 16)
                    for j in range(C // 512):
                        if j >= 2:
                            t.wait_ge(evBv.h, j - 1)
                        t.matmul(self.ps[6 + j % 2][:, :],
                                 self.onesf[0:1, 0:128],
                                 bvrow[0:1, 512 * j:512 * (j + 1)],
                                 start=True, stop=True)
                        mmBv.inc(t.last)

                @blk.vector
                def _(v):
                    for j in range(C // 512):
                        v.wait_ge(mmBv.h, j + 1)
                        evBv.inc(v.tensor_copy(
                            self.bvbc[:, 512 * j:512 * (j + 1)],
                            self.ps[6 + j % 2][:, :]))
                        v.drain()
        with nc.Block() as blk:
            @blk.gpsimd
            def _(g):
                for pc in range(PC):
                    dP.inc(g.dma_start(
                        h1f[:, Tn * pc:Tn * (pc + 1)],
                        self.sh1f[128 * pc:128 * (pc + 1), 0:Tn]), 16)
        self._pref = dP.n
        for hc in range(NH):
            qu = self.quant_take(self.qA, 8)
            self._proj_block(which, hc, NH, Tn, wt, h1f,
                             pf, dP, dOs, mm, ev, ea, qu)
        with nc.Block() as blk:
            @blk.sync
            def _(s):
                for d in dOs:
                    s.wait_ge(d.h, d.n)
        es.close()

    def _proj_block(self, which, hc, NH, Tn, wt, h1f,
                    pf, dP, dOs, mm, ev, ea, qu=()):
        nc = self.nc
        t0 = Tn * hc
        sl = slice(t0, t0 + Tn)
        is_v = (which == "v")
        sdf = {"q": self.sqf, "k": self.skf, "v": self.svf}[which]
        bias = {"q": self.bqv, "k": self.bkv, "v": None}[which]
        NU = PC if not is_v else 8
        NN = Tn if not is_v else 512
        with nc.Block() as blk:
            mm0, ev0, ea0 = mm.n, ev.n, ea.n
            dob = [d.n for d in dOs]
            pref = self._pref
            qhooks, qacts = (self.quant_inline_make(blk, self.qset_s,
                                                    list(qu))
                             if qu else ([], []))

            @blk.tensor
            def _(t):
                t = Track(t)
                t.wait_ge(dP.h, pref)
                for u in range(NU):
                    bank = self.ps[u % 6]
                    if u >= 6:
                        if is_v:
                            t.wait_ge(ev.h, ev0 + u - 5)
                        else:
                            t.wait_ge(ea.h, ea0 + u - 5)
                    if not is_v:
                        for kc in range(PC):
                            t.matmul(bank[:, 0:NN],
                                     wt[:, C * kc + 128 * u:
                                        C * kc + 128 * (u + 1)],
                                     h1f[:, Tn * kc:Tn * (kc + 1)],
                                     start=(kc == 0),
                                     stop=(kc == PC - 1))
                    else:
                        r, n4 = u // 4, u % 4
                        for kc in range(PC):
                            t.matmul(bank[:, 0:NN],
                                     h1f[:, Tn * kc + 128 * r:
                                         Tn * kc + 128 * (r + 1)],
                                     wt[:, C * kc + 512 * n4:
                                        C * kc + 512 * (n4 + 1)],
                                     start=(kc == 0),
                                     stop=(kc == PC - 1))
                    mm.inc(t.last)

            if not is_v:
                @blk.scalar
                def _(s):
                    for k in (0, 1):
                        if k < len(qacts):
                            qacts[k](s)
                    for u in range(PC):
                        if u % 4 == 0 and u >= 4 and u // 4 + 1 < len(qacts):
                            qacts[u // 4 + 1](s)
                        s.wait_ge(mm.h, mm0 + u + 1)
                        s.wait_ge(dOs[u % 4].h, dob[u % 4] + 16 * (u // 4))
                        ea.inc(s.activation(
                            pf[:, Tn * (u % 4):Tn * (u % 4 + 1)],
                            self.ps[u % 6][:, 0:Tn], AF.Identity,
                            bias=bias[:, u:u + 1], scale=1.0))

                @blk.vector
                def _(v):
                    for hk in qhooks:
                        hk(v)
            else:
                @blk.scalar
                def _(s):
                    for k in range(len(qacts)):
                        qacts[k](s)

                @blk.vector
                def _(v):
                    for u in range(8):
                        r, n4 = u // 4, u % 4
                        if u % 2 == 0 and len(qhooks) > u // 2:
                            qhooks[u // 2](v)
                        v.wait_ge(mm.h, mm0 + u + 1)
                        v.wait_ge(dOs[u % 2].h, dob[u % 2] + 16 * (u // 2))
                        h = slice(512 * (u % 2), 512 * (u % 2 + 1))
                        ev.inc(v.scalar_tensor_tensor(
                            pf[:, h], self.ps[u % 6][:, 0:512], 1.0,
                            self.bvbc[:, 512 * n4:512 * (n4 + 1)],
                            ALU.bypass, ALU.add))
                        v.drain()

            @blk.gpsimd
            def _(g):
                for u in range(NU):
                    if not is_v:
                        g.wait_ge(ea.h, ea0 + u + 1)
                        h = slice(Tn * (u % 4), Tn * (u % 4 + 1))
                        dOs[u % 4].inc(g.dma_start(
                            sdf[128 * u:128 * (u + 1), sl], pf[:, h]), 16)
                    else:
                        g.wait_ge(ev.h, ev0 + u + 1)
                        r, n4 = u // 4, u % 4
                        h = slice(512 * (u % 2), 512 * (u % 2 + 1))
                        rows = slice(t0 + 128 * r, t0 + 128 * (r + 1))
                        cols = slice(512 * n4, 512 * (n4 + 1))
                        dOs[u % 2].inc(g.dma_start(sdf[rows, cols],
                                                   pf[:, h]), 16)
                if hc + 1 < NH:
                    g.wait_ge(mm.h, mm0 + NU)
                    nsl = slice(Tn * (hc + 1), Tn * (hc + 2))
                    for pc in range(PC):
                        dP.inc(g.dma_start(
                            h1f[:, Tn * pc:Tn * (pc + 1)],
                            self.sh1f[128 * pc:128 * (pc + 1), nsl]), 16)
                    self._pref = dP.n

    def sweep_attn(self):
        nc = self.nc
        Tc = 512
        NCH = self.T // Tc
        WW = Tc + 128
        NRT = WW // 128
        es = ExitStack()
        A = lambda n, sh, d: es.enter_context(nc.sbuf_tensor(n, sh, d))
        bufs = dict(
            qw=A("qw", [128, PC * Tc], F32),
            kw=A("kw", [128, PC * WW], F32),
            vw=A("vw", [128, NRT * C], F32),
            xbuf=A("xbuf", [128, PC * Tc], F32),
            smb=A("smb", [128, 8 * 256], F32),
            athf=A("athf", [128, 8 * 256], F32),
            nmx=A("nmx", [128, 24], F32),
            sq2=A("s5sq", [128, 2 * Tc], F32),
            stat=A("s5st", [128, 2 * Tc], F32),
        )
        dP = self.ctr("dPa")
        dPk = self.ctr("dPak")
        dPx = self.ctr("dPax")
        dO = self.ctr("dOa")
        mm = self.ctr("mma")
        ev = self.ctr("eva")
        ea = self.ctr("eaa")
        for ch in range(NCH):
            self._attn_block(ch, NCH, Tc, WW, NRT, bufs, dP, dPk, dPx,
                             dO, mm, ev, ea)
        with nc.Block() as blk:
            @blk.sync
            def _(s):
                s.wait_ge(dO.h, dO.n)
        es.close()

    def _attn_block(self, ch, NCH, Tc, WW, NRT, B, dP, dPk, dPx,
                    dO, mm, ev, ea):
        nc = self.nc
        T = self.T
        t0 = Tc * ch
        sl = slice(t0, t0 + Tc)
        qw = B["qw"]
        kw = B["kw"]
        vw = B["vw"]
        qwb = qw[:].bitcast(BF16)
        h2h = qwb[:, 0:PC * Tc]
        h2l = qwb[:, PC * Tc:2 * PC * Tc]
        xbuf = B["xbuf"]
        smb = B["smb"]
        ebf = smb
        athf = B["athf"]
        nmx = B["nmx"]
        sq2, stat = B["sq2"], B["stat"]
        m_t = stat[:, 0:Tc]
        rs_t = stat[:, Tc:2 * Tc]
        var_t = sq2[:, 0:Tc]
        sd_t = sq2[:, Tc:2 * Tc]
        lo = t0 - 128
        cts = []
        for ti in range(Tc // 128):
            for cls in range(2):
                par = ti % 2
                if (cls == 0 and par == 0) or (cls == 1 and par == 1):
                    nk, off, wo = 128, 0, 128 * ti + 128
                else:
                    nk, off, wo = 256, 128, 128 * ti
                cts.append(dict(ti=ti, cls=cls, nk=nk, off=off, wo=wo))
        NCT = len(cts)
        with nc.Block() as blk:
            mm0, ev0, ea0, do0 = mm.n, ev.n, ea.n, dO.n
            mmct = []
            base = mm0
            for c, ct in enumerate(cts):
                nbk = 4 if ct["nk"] == 256 else 2
                mmct.append(base)
                base += nbk + 2
            mm_end = base
            nbks = [4 if ct["nk"] == 256 else 2 for ct in cts]

            @blk.sync
            def _(s):
                for pc in range(PC):
                    rows = slice(128 * pc, 128 * (pc + 1))
                    if ch == 0:
                        dPk.inc(s.dma_start(
                            kw[:, WW * pc:WW * pc + 128],
                            self.skf[rows, T - 128:T]), 16)
                        dPk.inc(s.dma_start(
                            kw[:, WW * pc + 128:WW * pc + WW],
                            self.skf[rows, 0:Tc]), 16)
                    else:
                        dPk.inc(s.dma_start(
                            kw[:, WW * pc:WW * (pc + 1)],
                            self.skf[rows, lo:t0 + Tc]), 16)
                r0v = lo + 128 * 1
                if r0v < 0:
                    r0v += T
                dPk.inc(s.dma_start(
                    vw[:, C * 1:C * 1 + C],
                    self.svf[r0v:r0v + 128, :]), 16)
                r0v = lo + 128 * 2
                if r0v < 0:
                    r0v += T
                dPk.inc(s.dma_start(
                    vw[:, C * 2:C * 2 + C],
                    self.svf[r0v:r0v + 128, :]), 16)

            @blk.gpsimd
            def _(g):
                g.wait_ge(dO.h, do0)
                for pc in range(PC):
                    rows = slice(128 * pc, 128 * (pc + 1))
                    dP.inc(g.dma_start(
                        qw[:, Tc * pc:Tc * (pc + 1)], self.sqf[rows, sl]), 16)
                r0v = lo + 128 * 0
                if r0v < 0:
                    r0v += T
                dP.inc(g.dma_start(
                    vw[:, C * 0:C * 0 + C],
                    self.svf[r0v:r0v + 128, :]), 16)
                g.wait_ge(ev.h, ev0 + 3 * NCT + 1)
                for pc in range(PC):
                    dO.inc(g.dma_start(
                        self.sx1[128 * pc:128 * (pc + 1), sl],
                        xbuf[:, Tc * pc:Tc * (pc + 1)]), 16)
                g.wait_ge(ev.h, ev0 + 3 * NCT + 3)
                for pc in range(PC):
                    dO.inc(g.dma_start(
                        self.sh2h[128 * pc:128 * (pc + 1), sl],
                        h2h[:, Tc * pc:Tc * (pc + 1)]), 16)
                    dO.inc(g.dma_start(
                        self.sh2l[128 * pc:128 * (pc + 1), sl],
                        h2l[:, Tc * pc:Tc * (pc + 1)]), 16)

            ldq = dP.n
            ldk = dPk.n
            ldx = dPx.n + 16 * PC + 32

            @blk.tensor
            def _(t):
                t = Track(t)
                t.wait_ge(dP.h, ldq)
                t.wait_ge(dPk.h, ldk)
                for c, ct in enumerate(cts):
                    ti, cls, nk, off, wo = (ct["ti"], ct["cls"], ct["nk"],
                                            ct["off"], ct["wo"])
                    side = 4 * (c % 2)
                    nbk = nbks[c]
                    hpb = 8 // nbk
                    h0 = 8 * cls
                    nblk = nk // 128
                    if c >= 2:
                        t.wait_ge(ev.h, ev0 + 3 * (c - 2) + 4)
                    for j in range(nbk):
                        bank = self.ps[side + j]
                        for hh in range(hpb):
                            h = h0 + j * hpb + hh
                            t.matmul(bank[:, nk * hh:nk * (hh + 1)],
                                     qw[:, Tc * h + 128 * ti:
                                        Tc * h + 128 * (ti + 1)],
                                     kw[:, WW * h + wo:WW * h + wo + nk],
                                     start=(hh == 0), stop=(hh == hpb - 1))
                        mm.inc(t.last)
                    t.wait_ge(ea.h, ea0 + 2 * c + 2)
                    for hh in range(8):
                        for j2 in range(nblk):
                            g2 = hh * nblk + j2
                            tb = self.ps[side + g2 // 4]
                            t.transpose(tb[:, 128 * (g2 % 4):
                                            128 * (g2 % 4 + 1)],
                                        ebf[:, nk * hh + 128 * j2:
                                            nk * hh + 128 * (j2 + 1)],
                                        self.ident[:, :])
                    mm.inc(t.last)
                    if c == 0:
                        t.wait_ge(dPx.h, ldx)
                    t.wait_ge(ev.h, ev0 + 3 * c + 2)
                    for jb in range(2):
                        ab = self.ps[side + jb]
                        ng = 4 * nblk
                        gi = 0
                        for hh2 in range(4):
                            hh = 4 * jb + hh2
                            h = h0 + hh
                            for j2 in range(nblk):
                                rt = wo // 128 + j2
                                t.matmul(ab[:, 128 * hh2:128 * (hh2 + 1)],
                                         vw[:, C * rt + 128 * h:
                                            C * rt + 128 * (h + 1)],
                                         athf[:, 128 * (hh * nblk + j2):
                                              128 * (hh * nblk + j2 + 1)],
                                         start=(gi == 0), stop=(gi == ng - 1))
                                gi += 1
                    mm.inc(t.last)
                t.wait_ge(ev.h, ev0 + 3 * NCT + 1)
                for pc in range(PC):
                    t.wait_ge(ea.h, ea0 + 2 * NCT + pc + 1)
                    t.matmul(self.ps[3][:, 0:Tc], self.onesf[:, :],
                             sq2[:, Tc * (pc % 2):Tc * (pc % 2 + 1)],
                             start=(pc == 0), stop=(pc == PC - 1))
                    mm.inc(t.last)
                for pc in range(PC):
                    t.matmul(self.ps[2][:, 0:Tc], self.onesf[:, :],
                             xbuf[:, Tc * pc:Tc * (pc + 1)],
                             start=(pc == 0), stop=(pc == PC - 1))
                mm.inc(t.last)

            @blk.scalar
            def _(s):
                s.wait_ge(dO.h, do0)
                for pc in range(PC):
                    dPx.inc(s.dma_start(
                        xbuf[:, Tc * pc:Tc * (pc + 1)],
                        self.sxT[128 * pc:128 * (pc + 1), sl]), 16)
                r0v = lo + 128 * 3
                if r0v < 0:
                    r0v += T
                dPx.inc(s.dma_start(
                    vw[:, C * 3:C * 3 + C],
                    self.svf[r0v:r0v + 128, :]), 16)
                r0v = lo + 128 * 4
                if r0v < 0:
                    r0v += T
                dPx.inc(s.dma_start(
                    vw[:, C * 4:C * 4 + C],
                    self.svf[r0v:r0v + 128, :]), 16)
                for c, ct in enumerate(cts):
                    nk = ct["nk"]
                    s.wait_ge(ev.h, ev0 + (3 * c if c else 1))
                    if c >= 1:
                        s.wait_ge(mm.h, mmct[c - 1] + nbks[c - 1] + 1)
                    last = None
                    for hh in range(8):
                        last = s.activation(
                            ebf[:, nk * hh:nk * (hh + 1)],
                            smb[:, nk * hh:nk * (hh + 1)],
                            AF.Exp, bias=nmx[:, 8 + hh:9 + hh],
                            scale=self.cISQ[:, 0:1])
                    ea.inc(last)
                for pc in range(PC):
                    s.wait_ge(ev.h, ev0 + 3 * NCT + 1)
                    if pc >= 2:
                        s.wait_ge(mm.h, mm_end + pc - 1)
                    s.activation(sq2[:, Tc * (pc % 2):Tc * (pc % 2 + 1)],
                                 xbuf[:, Tc * pc:Tc * (pc + 1)],
                                 AF.Square).then_inc(ea.h, 1)
                    ea.n += 1
                s.wait_ge(ev.h, ev0 + 3 * NCT + 2)
                ea.inc(s.activation(sd_t, var_t, AF.Sqrt,
                                    bias=self.cEPS[:, 0:1], scale=1.0))

            def tail_copies(v, c):
                ct = cts[c]
                side = 4 * (c % 2)
                ntile = 8 * (ct["nk"] // 128)
                v.wait_ge(mm.h, mmct[c] + nbks[c] + 1)
                for jb in range((ntile + 3) // 4):
                    v.tensor_copy(athf[:, 512 * jb:512 * jb + 512],
                                  self.ps[side + jb][:, 0:512])
                ev.inc(v.engine_nop())
                v.drain()

            def tail_x1(v, c):
                ct = cts[c]
                ti, h0 = ct["ti"], 8 * ct["cls"]
                side = 4 * (c % 2)
                if c == 0:
                    v.wait_ge(dPx.h, ldx)
                v.wait_ge(mm.h, mmct[c] + nbks[c] + 2)
                for hh in range(8):
                    h = h0 + hh
                    ab = self.ps[side + hh // 4]
                    xsl = slice(Tc * h + 128 * ti, Tc * h + 128 * (ti + 1))
                    v.scalar_tensor_tensor(xbuf[:, xsl],
                                           ab[:, 128 * (hh % 4):
                                              128 * (hh % 4 + 1)],
                                           self.gav[:, h:h + 1],
                                           xbuf[:, xsl], ALU.mult, ALU.add)
                ev.inc(v.engine_nop())
                v.drain()

            @blk.vector
            def _(v):
                for c, ct in enumerate(cts):
                    ti, cls, nk, off = (ct["ti"], ct["cls"], ct["nk"],
                                        ct["off"])
                    side = 4 * (c % 2)
                    nbk = nbks[c]
                    hpb = 8 // nbk
                    h0 = 8 * cls
                    msk = self.mask2x if nk == 256 else self.mask4x
                    if c >= 1:
                        tail_copies(v, c - 1)
                    if c >= 1:
                        v.wait_ge(ea.h, ea0 + 2 * (c - 1) + 1)
                    for j in range(nbk):
                        v.wait_ge(mm.h, mmct[c] + j + 1)
                        v.tensor_tensor(
                            smb[:, 512 * j:512 * j + nk * hpb],
                            self.ps[side + j][:, 0:nk * hpb],
                            msk[:, 0:nk * hpb], ALU.add)
                    v.drain()
                    for j in range(nbk):
                        sm3 = smb[:, 512 * j:512 * j + nk * hpb].rearrange(
                            "p (hh k) -> p hh k", hh=hpb)
                        v.tensor_reduce(nmx[:, j * hpb:(j + 1) * hpb], sm3,
                                        AXX, ALU.max, negate=True)
                    v.drain()
                    v.tensor_single_scalar(nmx[:, 8:16], nmx[:, 0:8],
                                           self.cISQ[:, 0:1], ALU.mult)
                    ev.inc(v.engine_nop())
                    v.drain()
                    if c >= 1:
                        tail_x1(v, c - 1)
                    v.wait_ge(ea.h, ea0 + 2 * c + 1)
                    e3 = ebf[:, 0:8 * nk].rearrange("p (hh k) -> p hh k",
                                                    hh=8)
                    v.tensor_reduce(nmx[:, 0:8], e3, AXX, ALU.add)
                    v.drain()
                    v.reciprocal(nmx[:, 16:24], nmx[:, 0:8])
                    v.drain()
                    last = None
                    for hh in range(8):
                        last = v.tensor_single_scalar(
                            ebf[:, nk * hh:nk * (hh + 1)],
                            ebf[:, nk * hh:nk * (hh + 1)],
                            nmx[:, 16 + hh:17 + hh], ALU.mult)
                    ea.inc(last)
                    v.drain()
                tail_copies(v, NCT - 1)
                tail_x1(v, NCT - 1)
                ev.inc(v.engine_nop())
                v.wait_ge(mm.h, mm_end + PC + 1)
                v.tensor_single_scalar(m_t, self.ps[2][:, 0:Tc], RC, ALU.mult)
                v.drain()
                v.tensor_tensor(var_t, m_t, m_t, ALU.mult)
                v.drain()
                v.scalar_tensor_tensor(var_t, self.ps[3][:, 0:Tc], RC, var_t,
                                       ALU.mult, ALU.subtract)
                v.drain()
                ev.inc(v.engine_nop())
                v.wait_ge(ea.h, ea0 + 2 * NCT + PC + 1)
                v.reciprocal(rs_t, sd_t)
                v.drain()
                v.wait_ge(dO.h, do0 + 16 * PC)
                self.ln_apply(v, xbuf, m_t, rs_t, self.cb1,
                              self.modv[:, 48:64], h2h, h2l, Tc)
                ev.inc(v.engine_nop())

    def sweep_ffn(self, half):
        nc = self.nc
        Tc = 512
        NCH = self.T // Tc
        NB = 6
        es = ExitStack()
        A = lambda n, sh, d: es.enter_context(nc.sbuf_tensor(n, sh, d))
        tg = f"f{half}"
        h2bh = A(f"h2bh{tg}", [128, PC * Tc], BF16)
        h2bl = A(f"h2bl{tg}", [128, PC * Tc], BF16)
        x2a = A(f"x2a{tg}", [128, PC * Tc], F32)
        gwb = [A(f"gwb{tg}{i}", [128, PC * 512], BF16) for i in range(2)]
        uwb = [A(f"uwb{tg}{i}", [128, PC * 512], BF16) for i in range(2)]
        dwb = [A(f"dwb{tg}{i}", [128, 4 * C], BF16) for i in range(1)]
        gsb = A(f"gsb{tg}", [128, 4 * 512], F32)
        fsh = A(f"fsh{tg}", [128, 4 * 512], BF16)
        fsl = A(f"fsl{tg}", [128, 4 * 512], BF16)
        fsf = A(f"fsf{tg}", [128, 4 * 512], F32)
        outst = A(f"outst{tg}", [128, C], F32) if half == 1 else None
        bufs = (h2bh, h2bl, x2a, gwb, uwb, dwb, gsb, fsh, fsl, fsf, outst)
        dP = self.ctr("dPf")
        dPw = [self.ctr("dPwA"), self.ctr("dPwB")]
        dPd = self.ctr("dPwD")
        dO = self.ctr("dOf")
        mm = self.ctr("mmf")
        ev = self.ctr("evf")
        ea = self.ctr("eaf")
        self._wtq = []
        self._dPw = dPw
        self._dPd = dPd
        xsrc = self.sx1 if half == 0 else self.sx1b
        with nc.Block() as blk:
            @blk.gpsimd
            def _(g):
                self._ffn_load_chunk(g, 0, Tc, h2bh, h2bl, x2a, dP, xsrc)
                self._ffn_load_gu(g, half, 0, gwb[0], uwb[0], dPw[0])
                self._wtq.append((0, dPw[0].n))
        self._pref = dP.n
        for ch in range(NCH):
            qu = self.quant_take(self.qB, 24) if half == 0 else []
            self._ffn_block(ch, NCH, Tc, NB, half, bufs, dP, dO, mm, ev, ea,
                            xsrc, qu)
        with nc.Block() as blk:
            @blk.sync
            def _(s):
                s.wait_ge(dO.h, dO.n)
        if half == 1:
            self._ffn_out = dO
        es.close()

    def _ffn_load_chunk(self, g, ch, Tc, h2bh, h2bl, x2a, dP, xsrc):
        sl = slice(Tc * ch, Tc * (ch + 1))
        for pc in range(PC):
            rows = slice(128 * pc, 128 * (pc + 1))
            dP.inc(g.dma_start(h2bh[:, Tc * pc:Tc * (pc + 1)],
                               self.sh2h[rows, sl]), 16)
            dP.inc(g.dma_start(h2bl[:, Tc * pc:Tc * (pc + 1)],
                               self.sh2l[rows, sl]), 16)
            dP.inc(g.dma_start(x2a[:, Tc * pc:Tc * (pc + 1)],
                               xsrc[rows, sl]), 16)

    def _ffn_load_gu(self, g, half, b, gwbuf, uwbuf, ctr):
        cols = slice(3072 * half + 512 * b, 3072 * half + 512 * (b + 1))
        for kc in range(PC):
            rows = slice(128 * kc, 128 * (kc + 1))
            ctr.inc(g.dma_start(gwbuf[:, 512 * kc:512 * (kc + 1)],
                                self.sgw[rows, cols]), 16)
            ctr.inc(g.dma_start(uwbuf[:, 512 * kc:512 * (kc + 1)],
                                self.suw[rows, cols]), 16)

    def _ffn_load_d(self, g, half, b, dwbuf, ctr):
        for icl in range(4):
            rows = slice(3072 * half + 512 * b + 128 * icl,
                         3072 * half + 512 * b + 128 * (icl + 1))
            ctr.inc(g.dma_start(dwbuf[:, C * icl:C * (icl + 1)],
                                self.sdw[rows, :]), 16)

    def _ffn_block(self, ch, NCH, Tc, NB, half, bufs, dP, dO, mm, ev, ea,
                   xsrc, qu):
        nc = self.nc
        (h2bh, h2bl, x2a, gwb, uwb, dwb, gsb, fsh, fsl, fsf, outst) = bufs
        t0 = Tc * ch
        NTR = Tc // 128
        sl = slice(t0, t0 + Tc)
        with nc.Block() as blk:
            mm0, ev0, ea0, do0 = mm.n, ev.n, ea.n, dO.n
            pref0 = self._pref
            dPw, dPd = self._dPw, self._dPd
            qhooks, qacts = (self.quant_inline_make(blk, self.qset_t,
                                                    list(qu))
                             if qu else ([], []))
            dtq = []

            @blk.gpsimd
            def _(g):
                for b in range(NB):
                    if b >= 1:
                        g.wait_ge(mm.h, mm0 + 24 * b)
                    self._ffn_load_d(g, half, b, dwb[0], dPd)
                    dtq.append(dPd.n)
                    nxt = b + 1
                    if nxt < NB or ch + 1 < NCH:
                        tgt = nxt if nxt < NB else 0
                        self._ffn_load_gu(g, half, tgt, gwb[nxt % 2],
                                          uwb[nxt % 2], dPw[nxt % 2])
                        self._wtq.append((nxt % 2, dPw[nxt % 2].n))
                if half == 0:
                    g.wait_ge(ev.h, ev0 + 20 * NB)
                    for pc in range(PC):
                        dO.inc(g.dma_start(
                            self.sx1b[128 * pc:128 * (pc + 1), sl],
                            x2a[:, Tc * pc:Tc * (pc + 1)]), 16)
                if ch + 1 < NCH:
                    if half == 0:
                        g.wait_ge(dO.h, do0 + 16 * PC)
                    else:
                        g.wait_ge(mm.h, mm0 + 24 * NB + NTR * PC)
                    self._ffn_load_chunk(g, ch + 1, Tc, h2bh, h2bl, x2a, dP,
                                         xsrc)
                    self._pref = dP.n

            @blk.tensor
            def _(t):
                t = Track(t)
                t.wait_ge(dP.h, pref0)
                for b in range(NB):
                    ws_, wv_ = self._wtq.pop(0)
                    t.wait_ge(dPw[ws_].h, wv_)
                    gslot, uslot = gwb[b % 2], uwb[b % 2]
                    for icl in range(4):
                        bank = self.ps[icl]
                        if b >= 1:
                            t.wait_ge(ev.h, ev0 + 20 * b)
                        gi = 0
                        for hs in (h2bh, h2bl):
                            for kc in range(PC):
                                t.matmul(bank[:, 0:512],
                                         gslot[:, 512 * kc + 128 * icl:
                                               512 * kc + 128 * (icl + 1)],
                                         hs[:, Tc * kc:Tc * (kc + 1)],
                                         start=(gi == 0),
                                         stop=(gi == 2 * PC - 1))
                                gi += 1
                        mm.inc(t.last)
                    for icl in range(4):
                        bank = self.ps[4 + icl]
                        if b >= 1:
                            t.wait_ge(ev.h, ev0 + 20 * (b - 1) + 4)
                        gi = 0
                        for hs in (h2bh, h2bl):
                            for kc in range(PC):
                                t.matmul(bank[:, 0:512],
                                         uslot[:, 512 * kc + 128 * icl:
                                               512 * kc + 128 * (icl + 1)],
                                         hs[:, Tc * kc:Tc * (kc + 1)],
                                         start=(gi == 0),
                                         stop=(gi == 2 * PC - 1))
                                gi += 1
                        mm.inc(t.last)
                    t.wait_ge(dPd.h, dtq[b])
                    for oc in range(PC):
                        bank = self.ps[oc % 4]
                        t.wait_ge(ev.h, ev0 + 20 * b + 4)
                        if oc >= 4:
                            t.wait_ge(ev.h, ev0 + 20 * b + 4 + oc - 3)
                        gi = 0
                        for fs in (fsh, fsl):
                            for icl in range(4):
                                t.matmul(bank[:, 0:512],
                                         dwb[0][:, C * icl + 128 * oc:
                                                C * icl + 128 * (oc + 1)],
                                         fs[:, 512 * icl:512 * (icl + 1)],
                                         start=(gi == 0), stop=(gi == 7))
                                gi += 1
                        mm.inc(t.last)
                if half == 1:
                    for r in range(NTR):
                        for pc in range(PC):
                            idx = r * PC + pc
                            bank = self.ps[4 + idx % 4]
                            if idx == 0:
                                t.wait_ge(ev.h, ev0 + 20 * NB)
                            if idx >= 4:
                                t.wait_ge(ev.h, ev0 + 20 * NB + idx - 3)
                            t.transpose(bank[:, 0:128],
                                        x2a[:, Tc * pc + 128 * r:
                                            Tc * pc + 128 * (r + 1)],
                                        self.ident[:, :])
                            mm.inc(t.last)

            @blk.scalar
            def _(s):
                for b in range(NB):
                    if 2 * b < len(qacts):
                        qacts[2 * b](s)
                    for icl in range(4):
                        s.wait_ge(mm.h, mm0 + 24 * b + icl + 1)
                        if b >= 1:
                            s.wait_ge(ev.h, ev0 + 20 * (b - 1) + 4)
                        ic = 24 * half + 4 * b + icl
                        s.activation(gsb[:, 512 * icl:512 * (icl + 1)],
                                     self.ps[icl][:, 0:512], AF.Silu,
                                     scale=self.gscv[:, ic:ic + 1]) \
                            .then_inc(ea.h, 1)
                        ea.n += 1
                    if 2 * b + 1 < len(qacts):
                        qacts[2 * b + 1](s)

            @blk.vector
            def _(v):
                for b in range(NB):
                    for icl in range(4):
                        ic = 24 * half + 4 * b + icl
                        v.wait_ge(mm.h, mm0 + 24 * b + 4 + icl + 1)
                        v.wait_ge(ea.h, ea0 + 4 * b + icl + 1)
                        v.scalar_tensor_tensor(
                            fsf[:, 512 * icl:512 * (icl + 1)],
                            self.ps[4 + icl][:, 0:512],
                            self.uscv[:, ic:ic + 1],
                            gsb[:, 512 * icl:512 * (icl + 1)],
                            ALU.mult, ALU.mult)
                    v.drain()
                    for icl in range(4):
                        v.tensor_copy(fsh[:, 512 * icl:512 * (icl + 1)],
                                      fsf[:, 512 * icl:512 * (icl + 1)])
                    v.drain()
                    for icl in range(4):
                        v.scalar_tensor_tensor(
                            fsl[:, 512 * icl:512 * (icl + 1)],
                            fsh[:, 512 * icl:512 * (icl + 1)], -1.0,
                            fsf[:, 512 * icl:512 * (icl + 1)],
                            ALU.mult, ALU.add)
                    v.drain()
                    ev.inc(v.engine_nop(), 4)
                    for oc in range(PC):
                        v.wait_ge(mm.h, mm0 + 24 * b + 8 + oc + 1)
                        v.scalar_tensor_tensor(
                            x2a[:, Tc * oc:Tc * (oc + 1)],
                            self.ps[oc % 4][:, 0:512],
                            self.gbds[:, oc:oc + 1],
                            x2a[:, Tc * oc:Tc * (oc + 1)],
                            ALU.mult, ALU.add)
                        v.drain()
                        ev.inc(v.engine_nop())
                    for k in (2 * b, 2 * b + 1):
                        if k < len(qhooks):
                            qhooks[k](v)
                if half == 1:
                    for r in range(NTR):
                        for pc in range(PC):
                            idx = r * PC + pc
                            bank = self.ps[4 + idx % 4]
                            v.wait_ge(mm.h, mm0 + 24 * NB + idx + 1)
                            if pc == 0:
                                v.wait_ge(dO.h, do0 + 16 * r)
                            v.tensor_copy(outst[:, 128 * pc:128 * (pc + 1)],
                                          bank[:, 0:128])
                            v.drain()
                            ev.inc(v.engine_nop())

            if half == 1:
                @blk.sync
                def _(s):
                    for r in range(NTR):
                        s.wait_ge(ev.h, ev0 + 20 * NB + (r + 1) * PC)
                        dO.inc(s.dma_start(
                            self.out[t0 + 128 * r:t0 + 128 * (r + 1), :],
                            outst[:, :]), 16)

    def finale(self):
        nc = self.nc
        dOf = self._ffn_out
        with nc.Block() as blk:
            @blk.sync
            def _(s):
                s.wait_ge(dOf.h, dOf.n)
                for e in "dp":
                    for sl_ in range(2):
                        s.wait_ge(self.qwl[e][sl_].h, self.qwl[e][sl_].n)


def _shard_inputs(inputs):
    maps = []
    for b in range(8):
        maps.append(dict(
            x_in=np.ascontiguousarray(np.asarray(inputs["L0_attn_x"])[b]),
            Wq=np.asarray(inputs["L0_attn_Wq"]),
            Wk=np.asarray(inputs["L0_attn_Wk"]),
            Wv=np.asarray(inputs["L0_attn_Wv"]),
            bqd=np.asarray(inputs["L0_attn_bq"]),
            bkd=np.asarray(inputs["L0_attn_bk"]),
            bvd=np.asarray(inputs["L0_attn_bv"]),
            gwd=np.asarray(inputs["L0_ffn_gate_w"]),
            uwd=np.asarray(inputs["L0_ffn_up_w"]),
            dwd=np.asarray(inputs["L0_ffn_down_w"]),
            gscd=np.asarray(inputs["L0_ffn_gate_scale"]),
            uscd=np.asarray(inputs["L0_ffn_up_scale"]),
            dscd=np.asarray(inputs["L0_ffn_down_scale"]),
            cvec=np.ascontiguousarray(np.asarray(inputs["L0_c"])[b]),
            wmod=np.asarray(inputs["L0_wmod"]),
        ))
    return maps


def kernel(**inputs):
    T = np.asarray(inputs["L0_attn_x"]).shape[1]
    b = Build(T=T)
    nc = b.build()
    in_maps = _shard_inputs(inputs)
    res = run_bass_kernel_spmd(nc, in_maps, core_ids=list(range(8)))
    out = np.stack([np.asarray(r["out"]) for r in res.results], axis=0)
    return out.astype(np.float32)
